# version 05
# baseline (speedup 1.0000x reference)
import contextlib

import numpy as np

import concourse.bass as bass
import concourse.mybir as mybir
from concourse.bass_core_rust import IndirectOffsetOnAxis
from concourse.bass_utils import run_bass_kernel_spmd

F32 = mybir.dt.float32
BF16 = mybir.dt.bfloat16
I32 = mybir.dt.int32
ALU = mybir.AluOpType
ACT = mybir.ActivationFunctionType
AX = mybir.AxisListType.X

B, S, H = 4, 4096, 1024
E, C, FF = 32, 640, 1024
TOWN = 2048
EC = E * C
ELOC = 4
BIG = float(1 << 26)
EPS = 1e-5
NCORE = 8
CC_ALL = [[0, 1, 2, 3, 4, 5, 6, 7]]
CC_PAIRS = [[0, 1], [2, 3], [4, 5], [6, 7]]


class Prog:

    BODIES = ("t", "v", "a", "g", "sy")

    def __init__(self, nc, es):
        self.nc = nc
        names = ("t", "v", "a", "g", "gd", "sy", "cc")
        self.sem = {n: es.enter_context(nc.semaphore("sem_" + n)) for n in names}
        self.cnt = {n: 0 for n in names}
        self.waited = {b: {} for b in self.BODIES}
        self.stream = {b: [] for b in self.BODIES}

    def _waits(self, body, deps):
        own = {"t": "t", "v": "v", "a": "a", "g": "g", "sy": None}[body]
        need = {}
        for d in deps:
            if d is None:
                continue
            dn, dv = d
            if dn == own:
                continue
            if dv > self.waited[body].get(dn, 0):
                need[dn] = max(need.get(dn, 0), dv)
        out = []
        for dn, dv in need.items():
            self.waited[body][dn] = dv
            out.append((self.sem[dn], dv))
        return out

    def step(self, body, fn, deps=(), fast=False):
        waits = self._waits(body, list(deps))
        self.cnt[body] += 1
        tok = (body, self.cnt[body])
        sem = self.sem[body]
        drain = body == "t" and not fast

        def emit(eng, waits=waits, fn=fn, drain=drain):
            for s_, v_ in waits:
                eng.wait_ge(s_, v_)
            res = fn(eng)
            if drain:
                res = eng.drain()
            if res is not None and hasattr(res, "then_inc"):
                res.then_inc(sem, 1)
            else:
                eng.sem_inc(sem, 1)
        self.stream[body].append(emit)
        return tok

    def dma(self, body, n, fn, deps=()):
        sname = "gd" if body == "g" else "sy"
        waits = self._waits(body, list(deps))
        self.cnt[sname] += 16 * n
        tok = (sname, self.cnt[sname])
        sem = self.sem[sname]
        issued = [0]

        def emit(eng, waits=waits, fn=fn, n=n):
            for s_, v_ in waits:
                eng.wait_ge(s_, v_)

            def fin(instr=None, **kw):
                if instr is None:
                    instr = eng.dma_start(**kw)
                issued[0] += 1
                return instr.then_inc(sem, 16)
            fn(eng, fin)
            assert issued[0] == n, f"dma count {issued[0]} != {n}"
        self.stream[body].append(emit)
        return tok

    def cc(self, kind, op, groups, ins, outs, deps=()):
        waits = self._waits("g", list(deps))
        self.cnt["cc"] += 1
        tok = ("cc", self.cnt["cc"])
        sem = self.sem["cc"]

        def emit(eng, waits=waits):
            for s_, v_ in waits:
                eng.wait_ge(s_, v_)
            eng.collective_compute(kind, op, replica_groups=groups,
                                   ins=ins, outs=outs).then_inc(sem, 1)
        self.stream["g"].append(emit)
        return tok

    def finalize(self, block):
        hosts = {"t": block.tensor, "v": block.vector, "a": block.scalar,
                 "g": block.gpsimd, "sy": block.sync}
        for body in self.BODIES:
            emitters = self.stream[body]

            def run_all(eng, emitters=emitters):
                for em in emitters:
                    em(eng)
            hosts[body](run_all)


def build_program(debug=False):
    nc = bass.Bass("TRN2", target_bir_lowering=False, num_devices=NCORE,
                   detect_race_conditions=False)

    din = {}
    for nm, shp in [
        ("x_b", [S, H]),
        ("wqT", [H, 512]), ("wkT", [H, 512]), ("wvT", [H, 512]),
        ("bqs", [512]), ("bks", [512]), ("bvs", [512]),
        ("woT", [512, H]), ("bo", [H]),
        ("w_router", [H, E]),
        ("wg", [ELOC, H, FF]), ("wu", [ELOC, H, FF]), ("wd", [ELOC, FF, H]),
        ("meta", [128, 4]),
        ("maskrow", [NCORE]),
    ]:
        din[nm] = nc.dram_tensor(nm, shp, F32, kind="ExternalInput").ap()

    out_own = nc.dram_tensor("out_own", [TOWN, H], F32, kind="ExternalOutput").ap()

    dkind = "ExternalOutput" if debug else "Internal"
    ds = {}
    for nm, shp in [
        ("k_hbm", [512, S]), ("v_hbm", [S, 512]), ("o_hbm", [512, S]),
        ("xf_full", [H, S]),
        ("base_hbm", [TOWN, H]),
        ("buf", [ELOC * C, H]),
    ]:
        ds[nm] = nc.dram_tensor(nm, shp, F32, kind=dkind).ap()
    for nm, shp in [
        ("rs_in", [2, H, TOWN]), ("rs_out", [H, TOWN]),
        ("cnt_in", [E]), ("cnt_out", [NCORE * E]),
        ("slot_in", [TOWN]), ("slot_out", [NCORE * TOWN]),
        ("agx_in", [TOWN, H]), ("xt_full", [NCORE * TOWN, H]),
        ("agy_in", [ELOC * C, H]), ("y_full", [EC, H]),
    ]:
        ds[nm] = nc.dram_tensor(nm, shp, F32, kind="Internal").ap()
    if debug:
        for nm, shp in [("rs_out_d", [H, TOWN]), ("agx_in_d", [TOWN, H]),
                        ("xt_full_d", [NCORE * TOWN, H]),
                        ("y_full_d", [EC, H]),
                        ("slot_out_d", [NCORE * TOWN])]:
            ds[nm] = nc.dram_tensor(nm, shp, F32, kind="ExternalOutput").ap()
    if debug:
        for nm, shp in [("q_dbg", [128, 16384]), ("rt_dbg", [128, 16 * 8]),
                        ("ln_dbg", [128, 4096]), ("st_dbg", [128, 16]),
                        ("rsin_d", [2, H, TOWN]), ("x2_d", [H, TOWN]),
                        ("x2b_d", [H, TOWN]),
                        ("lr_d", [4, 4 * 2048]),
                        ("lg_dbg", [128, 512]), ("ohm_dbg", [32, TOWN]),
                        ("ccm_dbg", [32, TOWN]), ("rows_dbg", [8, TOWN]),
                        ("m_dbg", [1, 16])]:
            ds[nm] = nc.dram_tensor(nm, shp, F32, kind="ExternalOutput").ap()

    es = contextlib.ExitStack()
    with es:
        def sbuf(name, shp, dt_=F32):
            return es.enter_context(nc.sbuf_tensor(name, shp, dt_))

        wbuf = sbuf("wbuf", [128, 16384])
        big1 = sbuf("big1", [128, 16384])
        scp = sbuf("scp", [128, 16384])

        def reg(a, b):
            return scp[:, a:b]

        ident = sbuf("ident", [128, 128])
        identi = sbuf("identi", [128, 128], I32)
        pcol = sbuf("pcol", [128, 1])
        e2ind = sbuf("e2ind", [128, 2])
        sel01 = sbuf("sel01", [128, 2])
        ones1 = sbuf("ones1", [1, 128])
        onesc = sbuf("onesc", [128, 1])
        bia = sbuf("bia", [128, 12])
        bo_sb = sbuf("bo_sb", [128, 8])
        wr_sb = sbuf("wr_sb", [128, 8 * 32])
        iotaE = sbuf("iotaE", [128, 32])
        iotaEi = sbuf("iotaEi", [128, 32], I32)
        w32 = sbuf("w32", [128, 32])
        w32i = sbuf("w32i", [128, 32], I32)
        probs = sbuf("probs", [128, 64])
        scr = sbuf("scr", [128, 16])
        rt = sbuf("rt", [128, 16 * 8])
        meta_sb = sbuf("meta_sb", [128, 4])
        mrow = sbuf("mrow", [8, 1])
        offc = sbuf("offc", [32, 1])
        cnt8 = sbuf("cnt8", [8, 32])
        r8t = sbuf("r8t", [128, 8])
        idxt = sbuf("idxt", [128, 8])
        idxi = sbuf("idxi", [128, 2], I32)
        gidxi = sbuf("gidxi", [128, 2], I32)
        nmh = sbuf("nmh", [128, 2])
        srow = sbuf("srow", [1, 16])
        stat = sbuf("stat", [128, 16])
        kn2m = sbuf("kn2m", [2, 4])
        qn2m = sbuf("qn2m", [2, 4])
        masks = sbuf("masks", [128, 4 * 512])
        maski = sbuf("maski", [128, 512], I32)
        bfa = sbuf("bfa", [128, 128])

        psb = [es.enter_context(nc.psum_tensor(f"psb{i}", [128, 512], F32))
               for i in range(8)]

        P = Prog(nc, es)
        bc_scat = nc.gpsimd.alloc_register("bc_scat")
        bc_gath = nc.gpsimd.alloc_register("bc_gath")

        mask_toks = []
        t_iota0 = P.step("g", lambda gp: [
            gp.reg_mov(bc_scat, ELOC * C - 1),
            gp.reg_mov(bc_gath, EC - 1),
            gp.iota(identi[:, :], pattern=[[-1, 128]], base=0,
                    channel_multiplier=1),
            gp.iota(iotaEi[:, :], pattern=[[1, 32]], base=0,
                    channel_multiplier=0),
            gp.iota(w32i[:, :], pattern=[[-1, 32]], base=32,
                    channel_multiplier=0)][-1])
        t_c0 = P.step("v", lambda v: [
            v.tensor_single_scalar(ident[:, :], identi[:, :], 0, ALU.is_equal),
            v.tensor_copy(pcol[:, :], identi[:, 0:1]),
            v.tensor_single_scalar(e2ind[:, 0:1], identi[:, 0:1], 64, ALU.is_lt),
            v.tensor_single_scalar(e2ind[:, 1:2], identi[:, 0:1], 63, ALU.is_gt),
            v.tensor_single_scalar(sel01[:, 0:1], identi[:, 0:1], 0, ALU.is_equal),
            v.tensor_single_scalar(sel01[:, 1:2], identi[:, 0:1], 1, ALU.is_equal),
            v.tensor_copy(iotaE[:, :], iotaEi[:, :]),
            v.tensor_copy(w32[:, :], w32i[:, :]),
            v.memset(ones1[:, :], 1.0),
            v.memset(onesc[:, :], 1.0),
            v.memset(nmh[:, 1:2], 0.125),
            v.memset(kn2m[:, :], 0.0),
            v.memset(qn2m[:, :], 0.0),
            v.memset(rt[:, :], 0.0),
            v.memset(probs[:, :], 0.0),
            v.memset(scr[:, :], 0.0),
            v.memset(srow[:, :], 0.0),
            v.memset(stat[:, :], 0.0)][-1], deps=[t_iota0])
        for i, dlt in enumerate([0, -128, -256, -384]):
            tm1 = P.step("g", lambda gp, dlt=dlt: gp.iota(
                maski[:, :], pattern=[[1, 512]], base=dlt,
                channel_multiplier=-1), deps=mask_toks[-1:] or [t_iota0])
            mask_toks.append(P.step("v", lambda v, i=i: v.tensor_single_scalar(
                masks[:, i * 512:(i + 1) * 512], maski[:, :], -1, ALU.is_gt),
                deps=[tm1]))

        def w_loads(sy, fin):
            for i, nm in enumerate(["wqT", "wkT", "wvT"]):
                for hc in range(8):
                    fin(out=wbuf[:, i * 4096 + hc * 512:i * 4096 + (hc + 1) * 512],
                        in_=din[nm][hc * 128:(hc + 1) * 128, :])
            for dc in range(4):
                fin(out=wbuf[:, 12288 + dc * 1024:12288 + (dc + 1) * 1024],
                    in_=din["woT"][dc * 128:(dc + 1) * 128, :])
            for i, nm in enumerate(["bqs", "bks", "bvs"]):
                for mj in range(4):
                    fin(out=bia[:, i * 4 + mj:i * 4 + mj + 1],
                        in_=din[nm][mj * 128:(mj + 1) * 128].unsqueeze(1))
            for hc in range(8):
                fin(out=bo_sb[:, hc:hc + 1],
                    in_=din["bo"][hc * 128:(hc + 1) * 128].unsqueeze(1))
            for hc in range(8):
                fin(out=wr_sb[:, hc * 32:(hc + 1) * 32],
                    in_=din["w_router"][hc * 128:(hc + 1) * 128, :])
            fin(out=meta_sb[:, :], in_=din["meta"][:, :])
            fin(out=mrow[:, :], in_=din["maskrow"][:].unsqueeze(1))
        t_w = P.dma("sy", 28 + 12 + 8 + 8 + 2, w_loads)

        xa = reg(0, 4096)
        lnta = reg(4096, 8192)
        lnf = reg(8192, 12288)
        sqt = reg(12288, 13312)
        ktmp = reg(13312, 13824)
        vtmp = reg(13824, 14336)
        ksq = reg(14336, 14848)
        evt = reg(14848, 15360)

        xa_readers, lnf_readers, lnta_guard = [], [], []
        prev_evac = {}
        kv_store_toks = []
        xf_store_toks = []
        q_evac_toks = []
        t_qn2_last = None
        t_kn2_last = None
        xf_ev_m2 = None

        for tau in range(8):
            t0 = tau * 512
            t_ldx = P.dma("g", 4, lambda sy, fin, t0=t0: [
                fin(out=xa[:, s4 * 1024:(s4 + 1) * 1024],
                    in_=din["x_b"][t0 + s4 * 128:t0 + (s4 + 1) * 128, :])
                for s4 in range(4)][-1],
                deps=xa_readers + [t_c0])
            xa_readers = []

            t_sums = P.step("v", lambda v: [
                v.reduce_sum(stat[:, s4 * 4:s4 * 4 + 1],
                             xa[:, s4 * 1024:(s4 + 1) * 1024], AX)
                for s4 in range(4)][-1], deps=[t_ldx])
            t_sq = P.step("a", lambda sc: [
                sc.activation(lnta[:, s4 * 1024:(s4 + 1) * 1024],
                              xa[:, s4 * 1024:(s4 + 1) * 1024], ACT.Square,
                              accum_out=stat[:, s4 * 4 + 1:s4 * 4 + 2])
                for s4 in range(4)][-1], deps=[t_ldx] + lnta_guard)
            xa_readers += [t_sums, t_sq]

            def fin_stats(v):
                for s4 in range(4):
                    c0 = s4 * 4
                    v.tensor_scalar_mul(stat[:, c0:c0 + 1],
                                        stat[:, c0:c0 + 1], 1.0 / 1024)
                for s4 in range(4):
                    c0 = s4 * 4
                    v.tensor_scalar_mul(stat[:, c0 + 1:c0 + 2],
                                        stat[:, c0 + 1:c0 + 2], 1.0 / 1024)
                for s4 in range(4):
                    c0 = s4 * 4
                    v.tensor_tensor(stat[:, c0 + 2:c0 + 3], stat[:, c0:c0 + 1],
                                    stat[:, c0:c0 + 1], ALU.mult)
                for s4 in range(4):
                    c0 = s4 * 4
                    v.tensor_tensor(stat[:, c0 + 2:c0 + 3],
                                    stat[:, c0 + 1:c0 + 2],
                                    stat[:, c0 + 2:c0 + 3], ALU.subtract)
                for s4 in range(4):
                    c0 = s4 * 4
                    last = v.tensor_scalar_add(stat[:, c0 + 2:c0 + 3],
                                               stat[:, c0 + 2:c0 + 3], EPS)
                return last
            t_fs = P.step("v", fin_stats, deps=[t_sums, t_sq])
            t_sqr = P.step("a", lambda sc: [
                sc.activation(stat[:, s4 * 4 + 3:s4 * 4 + 4],
                              stat[:, s4 * 4 + 2:s4 * 4 + 3], ACT.Sqrt)
                for s4 in range(4)][-1], deps=[t_fs])

            def fin_rstd(v):
                for s4 in range(4):
                    c0 = s4 * 4
                    v.reciprocal(stat[:, c0 + 2:c0 + 3],
                                 stat[:, c0 + 3:c0 + 4])
                for s4 in range(4):
                    c0 = s4 * 4
                    last = v.scalar_tensor_tensor(stat[:, c0 + 3:c0 + 4],
                                                  stat[:, c0:c0 + 1], -1.0,
                                                  stat[:, c0 + 2:c0 + 3],
                                                  ALU.mult, ALU.mult)
                return last
            t_rs = P.step("v", fin_rstd, deps=[t_sqr])
            t_ln = P.step("a", lambda sc: [
                sc.activation(lnta[:, s4 * 1024:(s4 + 1) * 1024],
                              xa[:, s4 * 1024:(s4 + 1) * 1024], ACT.Identity,
                              bias=stat[:, s4 * 4 + 3:s4 * 4 + 4],
                              scale=stat[:, s4 * 4 + 2:s4 * 4 + 3])
                for s4 in range(4)][-1], deps=[t_rs] + lnta_guard)
            xa_readers.append(t_ln)

            ev_prev = []
            for i, (s4, hc) in enumerate((s4, hc) for s4 in range(4)
                                         for hc in range(8)):
                bank = psb[4 + i % 2]
                ttr = P.step("t", lambda t, s4=s4, hc=hc, bank=bank:
                             t.transpose(bank[:, :128],
                                         lnta[:, s4 * 1024 + hc * 128:
                                              s4 * 1024 + (hc + 1) * 128],
                                         ident[:, :]),
                             deps=[t_ln, t_c0] +
                             (lnf_readers if i == 0 else []) +
                             ([ev_prev[i - 2]] if i >= 2 else []))
                tev = P.step("v", lambda v, s4=s4, hc=hc, bank=bank:
                             v.tensor_copy(lnf[:, hc * 512 + s4 * 128:
                                               hc * 512 + (s4 + 1) * 128],
                                           bank[:, :128]), deps=[ttr])
                ev_prev.append(tev)
            lnf_ready = list(ev_prev)
            lnta_guard = ev_prev[-2:]
            lnf_readers = []
            if debug and tau == 0:
                tdbg = P.dma("g", 2, lambda gp, fin: [
                    fin(out=ds["ln_dbg"][:, :], in_=lnf[:, :]),
                    fin(out=ds["st_dbg"][:, :], in_=stat[:, :])][-1],
                    deps=lnf_ready + [t_rs])
                lnf_readers.append(tdbg)

            for hc in range(8):
                evs = []
                for s4 in range(4):
                    i = hc * 4 + s4
                    bank = psb[6 + i % 2]
                    dep_dma = [xf_store_toks[-2]] if len(xf_store_toks) >= 2 \
                        else []
                    ttr = P.step("t", lambda t, s4=s4, hc=hc, bank=bank:
                                 t.transpose(bank[:, :128],
                                             xa[:, s4 * 1024 + hc * 128:
                                                s4 * 1024 + (hc + 1) * 128],
                                             ident[:, :]),
                                 deps=[t_ldx, t_c0] + dep_dma +
                                 ([evs[-2]] if len(evs) >= 2 else
                                  ([xf_ev_m2] if xf_ev_m2 else [])))
                    tev = P.step("v", lambda v, s4=s4, bank=bank:
                                 v.tensor_copy(evt[:, s4 * 128:(s4 + 1) * 128],
                                               bank[:, :128]),
                                 deps=[ttr] + xf_store_toks[-1:] +
                                 kv_store_toks[-1:])
                    evs.append(tev)
                xf_ev_m2 = evs[-2]
                xa_readers.append(evs[-1])
                xf_store_toks.append(P.dma("g", 1, lambda gp, fin, hc=hc, t0=t0:
                                           fin(out=ds["xf_full"][
                                               hc * 128:(hc + 1) * 128,
                                               t0:t0 + 512],
                                               in_=evt[:, :]),
                                           deps=[evs[-1]]))

            for mat in range(3):
                for mj in range(4):
                    bank = psb[mj % 2]
                    tmm = P.step("t", lambda t, mat=mat, mj=mj, bank=bank: [
                        t.matmul(bank[:, :],
                                 wbuf[:, mat * 4096 + hc * 512 + mj * 128:
                                      mat * 4096 + hc * 512 + (mj + 1) * 128],
                                 lnf[:, hc * 512:(hc + 1) * 512],
                                 start=(hc == 0), stop=(hc == 7))
                        for hc in range(8)][-1],
                        deps=lnf_ready + [t_w] + prev_evac.get((mat, mj), []))
                    lnf_readers.append(tmm)
                    if mat == 0:
                        tev = P.step("v", lambda v, mj=mj, bank=bank, t0=t0:
                                     v.tensor_scalar(
                                         big1[:, mj * 4096 + t0:
                                              mj * 4096 + t0 + 512],
                                         bank[:, :], bia[:, mj:mj + 1], None,
                                         ALU.add), deps=[tmm])
                        q_evac_toks.append(tev)
                        tsq = P.step("a", lambda sc, mj=mj, t0=t0:
                                     sc.activation(ksq[:, :],
                                                   big1[:, mj * 4096 + t0:
                                                        mj * 4096 + t0 + 512],
                                                   ACT.Square),
                                     deps=[tev, t_qn2_last, t_kn2_last])
                        tnm = P.step("t", lambda t:
                                     t.matmul(psb[3][0:2, :], e2ind[:, :],
                                              ksq[:, :], start=True, stop=True),
                                     deps=[tsq, t_qn2_last])
                        t_qn2_last = P.step("v", lambda v, mj=mj: [
                            v.reduce_max(scr[0:2, 4:5], psb[3][0:2, :], AX),
                            v.tensor_scalar_add(scr[:, 7:8], scr[:, 7:8], 0.0),
                            v.tensor_tensor(qn2m[:, mj:mj + 1],
                                            qn2m[:, mj:mj + 1],
                                            scr[0:2, 4:5], ALU.max)][-1],
                            deps=[tnm])
                        prev_evac[(mat, mj)] = [tev, tsq]
                    elif mat == 1:
                        tev = P.step("v", lambda v, mj=mj, bank=bank:
                                     v.tensor_scalar(ktmp[:, :], bank[:, :],
                                                     bia[:, 4 + mj:5 + mj],
                                                     None, ALU.add),
                                     deps=[tmm] + prev_evac.get(("kst", 0), []))
                        tdm = P.dma("g", 1, lambda gp, fin, mj=mj, t0=t0:
                                    fin(out=ds["k_hbm"][mj * 128:(mj + 1) * 128,
                                                        t0:t0 + 512],
                                        in_=ktmp[:, :]), deps=[tev])
                        kv_store_toks.append(tdm)
                        tsq = P.step("a", lambda sc: sc.activation(
                            ksq[:, :], ktmp[:, :], ACT.Square),
                            deps=[tev, t_kn2_last, t_qn2_last])
                        tnm = P.step("t", lambda t:
                                     t.matmul(psb[3][0:2, :], e2ind[:, :],
                                              ksq[:, :], start=True, stop=True),
                                     deps=[tsq, t_kn2_last])
                        t_kn2_last = P.step("v", lambda v, mj=mj: [
                            v.reduce_max(scr[0:2, 4:5], psb[3][0:2, :], AX),
                            v.tensor_scalar_add(scr[:, 7:8], scr[:, 7:8], 0.0),
                            v.tensor_tensor(kn2m[:, mj:mj + 1],
                                            kn2m[:, mj:mj + 1],
                                            scr[0:2, 4:5], ALU.max)][-1],
                            deps=[tnm])
                        prev_evac[(mat, mj)] = [tdm, tsq]
                        prev_evac[("kst", 0)] = [tdm, tsq]
                    else:
                        tev = P.step("v", lambda v, mj=mj, bank=bank:
                                     v.tensor_scalar(vtmp[:, :], bank[:, :],
                                                     bia[:, 8 + mj:9 + mj],
                                                     None, ALU.add),
                                     deps=[tmm] + prev_evac.get(("vst", 0), []))
                        evs = []
                        for s4 in range(4):
                            ttr = P.step("t", lambda t, s4=s4:
                                         t.transpose(psb[2][:, :128],
                                                     vtmp[:, s4 * 128:
                                                          (s4 + 1) * 128],
                                                     ident[:, :]),
                                         deps=[tev] + (evs[-1:] if evs else
                                                       prev_evac.get(("vtr", 0),
                                                                     [])))
                            tev2 = P.step("v", lambda v, s4=s4:
                                          v.tensor_copy(
                                              evt[:, s4 * 128:(s4 + 1) * 128],
                                              psb[2][:, :128]),
                                          deps=[ttr] + xf_store_toks[-1:] +
                                          kv_store_toks[-1:])
                            evs.append(tev2)
                        tdm = P.dma("g", 1, lambda gp, fin, mj=mj, t0=t0: fin(
                            out=ds["v_hbm"][t0:t0 + 512,
                                            mj * 128:(mj + 1) * 128]
                            .rearrange("(a p) d -> p a d", p=128),
                            in_=evt[:, :].rearrange("p (a d) -> p a d", d=128)),
                            deps=[evs[-1]])
                        kv_store_toks.append(tdm)
                        prev_evac[(mat, mj)] = [tdm]
                        prev_evac[("vst", 0)] = [tdm]
                        prev_evac[("vtr", 0)] = [evs[-1]]

        t_msq = P.step("a", lambda sc: [
            sc.activation(qn2m[:, :], qn2m[:, :], ACT.Sqrt),
            sc.activation(kn2m[:, :], kn2m[:, :], ACT.Sqrt)][-1],
            deps=[t_qn2_last, t_kn2_last])
        t_mm2 = P.step("v", lambda v: [
            v.tensor_tensor(kn2m[:, :], kn2m[:, :], qn2m[:, :], ALU.mult),
            v.tensor_scalar_add(scr[:, 7:8], scr[:, 7:8], 0.0),
            v.tensor_scalar_mul(kn2m[:, :], kn2m[:, :], -0.125)][-1],
            deps=[t_msq])
        t_se0 = P.step("t", lambda t: t.matmul(
            psb[3][0:1, 0:4], sel01[0:2, 0:1], kn2m[:, :],
            start=True, stop=True), deps=[t_mm2, t_qn2_last, t_kn2_last])
        t_sv0 = P.step("v", lambda v: v.tensor_copy(
            srow[0:1, 0:4], psb[3][0:1, 0:4]), deps=[t_se0])
        t_se1 = P.step("t", lambda t: t.matmul(
            psb[3][0:1, 0:4], sel01[0:2, 1:2], kn2m[:, :],
            start=True, stop=True), deps=[t_sv0])
        t_mrow = P.step("v", lambda v: v.tensor_copy(
            srow[0:1, 4:8], psb[3][0:1, 0:4]), deps=[t_se1])
        if debug:
            P.dma("g", 8, lambda gp, fin: [
                fin(out=ds["q_dbg"][:, i * 2048:(i + 1) * 2048],
                    in_=big1[:, i * 2048:(i + 1) * 2048]) for i in range(8)]
                and None, deps=q_evac_toks)
            P.dma("g", 1, lambda gp, fin: fin(out=ds["m_dbg"][:, :],
                                              in_=srow[:, :].unsqueeze(0)
                                              if False else srow[:, :]),
                  deps=[t_mrow])

        kstage = reg(0, 4096)
        vhx = reg(4096, 6176).bitcast(BF16)
        vlx = reg(6176, 8256).bitcast(BF16)
        vstage = reg(8256, 10336)
        ptile = reg(10336, 11872)
        phx = reg(11872, 12640).bitcast(BF16)
        plx = reg(12640, 13408).bitcast(BF16)
        rrow = reg(13408, 13920)[0:1, :]
        ofh = reg(13920, 14432)[0:64, :]
        rbc = reg(14432, 14944)[0:64, :]
        khx = wbuf[:, 0:2048].bitcast(BF16)
        klx = wbuf[:, 2048:4096].bitcast(BF16)
        qhx = wbuf[:, 4096:8192].bitcast(BF16)
        qlx = wbuf[:, 8192:12288].bitcast(BF16)

        o_store_toks = []
        head_users = {}
        exp_toks = {}
        aexp_last = {}
        prev_n4 = None
        prev_bc = None
        t_ksp = {}
        t_vsp = {}
        t_qsp = {}
        t_vf = None
        for h in range(8):
            par = h % 2
            cb = par
            hf = h // 2
            qh2 = (hf % 2) * 4096
            if h % 2 == 0:
                war_q = [exp_toks[(h - 3, 7, 31)]] if h >= 4 else []
                tqh = P.step("a", lambda sc, hf=hf, qh2=qh2: sc.activation(
                    qhx[:, qh2:qh2 + 4096],
                    big1[:, hf * 4096:(hf + 1) * 4096], ACT.Copy),
                    deps=[q_evac_toks[-1], t_mrow] + war_q)
                t_qsp[hf] = P.step("v", lambda v, hf=hf, qh2=qh2:
                                   v.tensor_tensor(
                                       qlx[:, qh2:qh2 + 4096],
                                       big1[:, hf * 4096:(hf + 1) * 4096],
                                       qhx[:, qh2:qh2 + 4096], ALU.subtract),
                                   deps=[tqh])
            t_ldk = P.dma("g", 1, lambda gp, fin, h=h, par=par: fin(
                out=kstage[par * 64:par * 64 + 64, :],
                in_=ds["k_hbm"][64 * h:64 * h + 64, :]),
                deps=kv_store_toks + ([t_ksp[h - 2]] if h >= 2 else []))
            war_k = [exp_toks[(h - 2, 7, 31)]] if h >= 2 else []
            tkh = P.step("a", lambda sc, par=par: sc.activation(
                khx[par * 64:par * 64 + 64, :],
                kstage[par * 64:par * 64 + 64, :], ACT.Copy),
                deps=[t_ldk, t_mrow] + war_k)
            t_ksp[h] = P.step("v", lambda v, par=par: v.tensor_tensor(
                klx[par * 64:par * 64 + 64, :],
                kstage[par * 64:par * 64 + 64, :],
                khx[par * 64:par * 64 + 64, :], ALU.subtract),
                deps=[tkh])
            if h == 0:
                t_vf = P.step("v", lambda v: v.memset(vstage[:, :], 1.0),
                              deps=kv_store_toks + [t_mrow])
            t_ldv = P.dma("g", 1, lambda gp, fin, h=h: fin(
                out=vstage[:, :].rearrange("p (a w) -> p a w", w=65)[:, :, 0:64],
                in_=ds["v_hbm"][:, 64 * h:64 * h + 64]
                .rearrange("(a p) d -> p a d", p=128)),
                deps=[t_vf] if h == 0 else [t_vsp[h - 1]])
            war_v = head_users[h - 2][-1:] if h >= 2 else []
            tvh = P.step("a", lambda sc, cb=cb: sc.activation(
                vhx[:, cb * 2080:(cb + 1) * 2080], vstage[:, :], ACT.Copy),
                deps=[t_ldv] + war_v)
            t_vsp[h] = P.step("v", lambda v, cb=cb: v.tensor_tensor(
                vlx[:, cb * 2080:(cb + 1) * 2080], vstage[:, :],
                vhx[:, cb * 2080:(cb + 1) * 2080], ALU.subtract),
                deps=[tvh])
            users = []

            col = (h % 2) * 4 + h // 2
            t_n4 = P.step("t", lambda t, col=col: t.matmul(
                psb[7][0:128, 0:1], ones1[0:1, :], srow[0:1, col:col + 1],
                start=True, stop=True),
                deps=[t_mrow, t_c0] + ([prev_n4] if prev_n4 else []))
            prev_n4 = t_n4
            t_nm = P.step("v", lambda v: v.tensor_copy(
                nmh[:, 0:1], psb[7][0:128, 0:1]),
                deps=[t_n4] + ([aexp_last[h - 1]] if h >= 1 else []))

            for qt in range(8):
                nk = qt * 4 + 4
                obank = psb[2 + qt % 2]
                t_oev_prev = o_store_toks[-1] if o_store_toks else None
                av_toks = []
                for kt in range(nk):
                    sbank = psb[kt % 2]
                    psl = kt % 3
                    t_sc = P.step("t", lambda t, par=par, qh2=qh2, qt=qt,
                                  kt=kt, sbank=sbank: [
                                      t.matmul(
                                          sbank[:, :],
                                          khx[par * 64:par * 64 + 64,
                                              kt * 128:(kt + 1) * 128],
                                          qhx[par * 64:par * 64 + 64,
                                              qh2 + qt * 512:
                                              qh2 + (qt + 1) * 512],
                                          start=True, stop=False),
                                      t.matmul(
                                          sbank[:, :],
                                          klx[par * 64:par * 64 + 64,
                                              kt * 128:(kt + 1) * 128],
                                          qhx[par * 64:par * 64 + 64,
                                              qh2 + qt * 512:
                                              qh2 + (qt + 1) * 512],
                                          start=False, stop=False),
                                      t.matmul(
                                          sbank[:, :],
                                          khx[par * 64:par * 64 + 64,
                                              kt * 128:(kt + 1) * 128],
                                          qlx[par * 64:par * 64 + 64,
                                              qh2 + qt * 512:
                                              qh2 + (qt + 1) * 512],
                                          start=False, stop=True)][-1],
                                  fast=True,
                                  deps=[t_ksp[h], t_qsp[hf], t_nm] +
                                  ([exp_toks[(h, qt, kt - 2)]]
                                   if kt >= 2 else
                                   ([exp_toks[(h, qt - 1, qt * 4 - 1)]]
                                    if qt >= 1 else
                                    ([exp_toks[(h - 1, 7, 31)]]
                                     if h >= 1 else []))))
                    if kt >= qt * 4:
                        t_ex = P.step("a", lambda sc, psl=psl, sbank=sbank:
                                      sc.activation(
                                          ptile[:, psl * 512:(psl + 1) * 512],
                                          sbank[:, :], ACT.Exp,
                                          bias=nmh[:, 0:1], scale=nmh[:, 1:2]),
                                      deps=[t_sc] +
                                      ([av_toks[kt - 3]] if kt >= 3 else []))
                        i = kt - qt * 4
                        dep_p = P.step("v", lambda v, psl=psl, i=i: [
                            v.tensor_tensor(
                                ptile[:, psl * 512:(psl + 1) * 512],
                                ptile[:, psl * 512:(psl + 1) * 512],
                                masks[:, i * 512:(i + 1) * 512], ALU.mult),
                            v.tensor_copy(
                                phx[:, psl * 512:(psl + 1) * 512],
                                ptile[:, psl * 512:(psl + 1) * 512]),
                            v.tensor_tensor(
                                plx[:, psl * 512:(psl + 1) * 512],
                                ptile[:, psl * 512:(psl + 1) * 512],
                                phx[:, psl * 512:(psl + 1) * 512],
                                ALU.subtract)][-1],
                            deps=[t_ex] + mask_toks +
                            ([av_toks[kt - 3]] if kt >= 3 else []))
                    else:
                        t_ex = P.step("a", lambda sc, psl=psl, sbank=sbank: [
                            sc.activation(
                                ptile[:, psl * 512:(psl + 1) * 512],
                                sbank[:, :], ACT.Exp,
                                bias=nmh[:, 0:1], scale=nmh[:, 1:2]),
                            sc.activation(
                                phx[:, psl * 512:(psl + 1) * 512],
                                sbank[:, :], ACT.Exp,
                                bias=nmh[:, 0:1], scale=nmh[:, 1:2])][-1],
                            deps=[t_sc] +
                            ([av_toks[kt - 3]] if kt >= 3 else []))
                        dep_p = P.step("v", lambda v, psl=psl:
                                       v.tensor_tensor(
                                           plx[:, psl * 512:(psl + 1) * 512],
                                           ptile[:, psl * 512:(psl + 1) * 512],
                                           phx[:, psl * 512:(psl + 1) * 512],
                                           ALU.subtract),
                                       deps=[t_ex])
                    exp_toks[(h, qt, kt)] = dep_p
                    aexp_last[h] = t_ex
                    t_av = P.step("t", lambda t, cb=cb, kt=kt, psl=psl,
                                  obank=obank, nk=nk: [
                                      t.matmul(
                                          obank[0:65, :],
                                          vhx[:, cb * 2080 + kt * 65:
                                              cb * 2080 + (kt + 1) * 65],
                                          phx[:, psl * 512:(psl + 1) * 512],
                                          start=(kt == 0), stop=False),
                                      t.matmul(
                                          obank[0:65, :],
                                          vlx[:, cb * 2080 + kt * 65:
                                              cb * 2080 + (kt + 1) * 65],
                                          phx[:, psl * 512:(psl + 1) * 512],
                                          start=False, stop=False),
                                      t.matmul(
                                          obank[0:65, :],
                                          vhx[:, cb * 2080 + kt * 65:
                                              cb * 2080 + (kt + 1) * 65],
                                          plx[:, psl * 512:(psl + 1) * 512],
                                          start=False,
                                          stop=(kt == nk - 1))][-1],
                                  fast=True,
                                  deps=[dep_p, t_vsp[h]] +
                                  ([t_oev_prev] if kt == 0 and t_oev_prev
                                   else []))
                    av_toks.append(t_av)
                users += [av_toks[-1]]
                t_rc = P.step("v", lambda v, obank=obank: v.reciprocal(
                    rrow[:, :], obank[64:65, :]),
                    deps=[av_toks[-1]] + ([prev_bc] if prev_bc else []))
                t_bc = P.step("t", lambda t: t.matmul(
                    psb[6][0:64, :], ones1[0:1, 0:64], rrow[:, :],
                    start=True, stop=True),
                    deps=[t_rc] + (o_store_toks[-1:] if o_store_toks else []))
                prev_bc = t_bc
                t_bce = P.step("v", lambda v: v.tensor_copy(
                    rbc[:, :], psb[6][0:64, :]), deps=[t_bc])
                t_oe = P.step("v", lambda v, obank=obank: v.tensor_tensor(
                    ofh[:, :], obank[0:64, :], rbc[:, :], ALU.mult),
                    deps=[t_bce] + (o_store_toks[-1:] if o_store_toks else []))
                o_store_toks.append(P.dma("g", 1, lambda gp, fin, h=h, qt=qt:
                                          fin(out=ds["o_hbm"][
                                              64 * h:64 * h + 64,
                                              qt * 512:(qt + 1) * 512],
                                              in_=ofh[:, :]),
                                          deps=[t_oe]))
            head_users[h] = users

        ob = reg(0, 4096)
        rstc = reg(4096, 5120)
        xftl = reg(5120, 6144)
        rs_dma_toks = []
        mm_toks_c = {}
        for tau in range(8):
            pp = tau % 2
            hf, tc = tau // 4, tau % 4
            t_ldo = P.dma("sy", 4, lambda sy, fin, tau=tau, pp=pp: [
                fin(out=ob[:, pp * 2048 + dc * 512:pp * 2048 + (dc + 1) * 512],
                    in_=ds["o_hbm"][dc * 128:(dc + 1) * 128,
                                    tau * 512:(tau + 1) * 512])
                for dc in range(4)][-1],
                deps=o_store_toks + [mm_toks_c.get(tau - 2)])
            last_mm = None
            for jm in range(8):
                bank = psb[jm % 2]
                xsl = jm % 2
                t_ldxf = P.dma("g", 1, lambda sy, fin, jm=jm, tau=tau,
                               xsl=xsl: fin(
                                   out=xftl[:, xsl * 512:(xsl + 1) * 512],
                                   in_=ds["xf_full"][jm * 128:(jm + 1) * 128,
                                                     tau * 512:(tau + 1) * 512]),
                               deps=xf_store_toks +
                               (rs_dma_toks[-2:-1] if len(rs_dma_toks) >= 2
                                else []))
                def c_mm(t, jm=jm, pp=pp, bank=bank):
                    for dc in range(4):
                        t.matmul(bank[:, :],
                                 wbuf[:, 12288 + dc * 1024 + jm * 128:
                                      12288 + dc * 1024 + (jm + 1) * 128],
                                 ob[:, pp * 2048 + dc * 512:
                                    pp * 2048 + (dc + 1) * 512],
                                 start=(dc == 0), stop=(dc == 3))
                t_mm = P.step("t", c_mm,
                              deps=[t_ldo, t_w, aexp_last[7]] +
                              (rs_dma_toks[-2:-1] if len(rs_dma_toks) >= 2
                               else []))
                last_mm = t_mm
                rsl = jm % 2
                t_ev = P.step("v", lambda v, bank=bank, xsl=xsl, rsl=rsl:
                              v.scalar_tensor_tensor(
                                  rstc[:, rsl * 512:(rsl + 1) * 512],
                                  xftl[:, xsl * 512:(xsl + 1) * 512], 0.5,
                                  bank[:, :], ALU.mult, ALU.add),
                              deps=[t_mm, t_ldxf] +
                              (rs_dma_toks[-2:-1] if len(rs_dma_toks) >= 2
                               else []))
                rs_dma_toks.append(P.dma("g", 1,
                                         lambda gp, fin, hf=hf, jm=jm, tc=tc,
                                         rsl=rsl: fin(
                                             out=ds["rs_in"][
                                                 hf, jm * 128:(jm + 1) * 128,
                                                 tc * 512:(tc + 1) * 512],
                                             in_=rstc[:, rsl * 512:
                                                      (rsl + 1) * 512]),
                                         deps=[t_ev]))
            mm_toks_c[tau] = last_mm

        if debug:
            P.dma("g", 1, lambda gp, fin: fin(out=ds["rsin_d"][:, :, :],
                                              in_=ds["rs_in"][:, :, :]),
                  deps=rs_dma_toks)
        t_rs_cc = P.cc("ReduceScatter", ALU.add, CC_PAIRS,
                       [ds["rs_in"].flatten()], [ds["rs_out"].flatten()],
                       deps=rs_dma_toks)
        if debug:
            P.dma("g", 1, lambda gp, fin: fin(out=ds["rs_out_d"][:, :],
                                              in_=ds["rs_out"][:, :]),
                  deps=[t_rs_cc])

        x2t = reg(0, 1024)
        xtm4 = reg(1024, 5120)
        btm4 = reg(5120, 9216)
        lrz = reg(9216, 11264)[0:1, :]
        lgt = reg(11264, 11776)[0:32, :]
        lgtm = reg(11776, 12288)
        ohm = reg(12288, 14336)[0:32, :]
        ccm = reg(14336, 16384)[0:32, :]
        slotf_r = reg(1024, 3072)[0:1, :]
        accf_r = reg(3072, 5120)[0:1, :]
        wrow = reg(5120, 7680)[0:1, :]
        ohs = reg(7680, 8192)[0:32, :]
        sct = reg(1024, 3072)
        slot_sb = reg(3072, 3200)
        zb = reg(3200, 4224)

        def lr(i):
            return lrz[0:1, i * 512:(i + 1) * 512]

        def wr_(i):
            return wrow[0:1, i * 512:(i + 1) * 512]

        agx_toks = []
        base_toks = []
        x2_done = {}
        prev_x2 = []
        lg_evs = []
        xt_toks = {}
        for to in range(4):
            mm_m = []
            mm_s = []
            for hc in range(8):
                sl = hc % 2
                t_ld = P.dma("g", 1, lambda sy, fin, hc=hc, to=to, sl=sl: fin(
                    out=x2t[:, sl * 512:(sl + 1) * 512],
                    in_=ds["rs_out"][hc * 128:(hc + 1) * 128,
                                     to * 512:(to + 1) * 512]),
                    deps=[t_rs_cc] + (mm_m[hc - 2:hc - 1] if hc >= 2
                                      else prev_x2))
                t_x2 = P.step("v", lambda v, hc=hc, sl=sl: v.tensor_scalar(
                    x2t[:, sl * 512:(sl + 1) * 512],
                    x2t[:, sl * 512:(sl + 1) * 512],
                    bo_sb[:, hc:hc + 1], None, ALU.add), deps=[t_ld])
                if debug:
                    t_x2 = P.dma("g", 1, lambda gp, fin, hc=hc, to=to, sl=sl:
                                 fin(out=ds["x2_d"][hc * 128:(hc + 1) * 128,
                                                    to * 512:(to + 1) * 512],
                                     in_=x2t[:, sl * 512:(sl + 1) * 512]),
                                 deps=[t_x2])
                tq = P.step("a", lambda sc, sl=sl: sc.activation(
                    sqt[:, 0:512], x2t[:, sl * 512:(sl + 1) * 512],
                    ACT.Square), deps=[t_x2] + mm_s[-1:])
                mm_m.append(P.step("t", lambda t, hc=hc, sl=sl: t.matmul(
                    psb[0][0:1, :], onesc[:, :],
                    x2t[:, sl * 512:(sl + 1) * 512],
                    start=(hc == 0), stop=(hc == 7)),
                    deps=[t_x2] + (lg_evs[-1:] if hc == 0 else [])))
                mm_s.append(P.step("t", lambda t, hc=hc: t.matmul(
                    psb[1][0:1, :], onesc[:, :], sqt[:, 0:512],
                    start=(hc == 0), stop=(hc == 7)), deps=[tq]))

            def d_stats(v):
                v.tensor_scalar_mul(lr(0), psb[0][0:1, :], 1.0 / 1024)
                v.tensor_scalar_mul(lr(1), psb[1][0:1, :], 1.0 / 1024)
                v.tensor_tensor(lr(2), lr(0), lr(0), ALU.mult)
                v.tensor_tensor(lr(2), lr(1), lr(2), ALU.subtract)
                return v.tensor_scalar_add(lr(2), lr(2), EPS)
            t_st = P.step("v", d_stats, deps=[mm_m[-1], mm_s[-1]])
            t_sq2 = P.step("a", lambda sc: sc.activation(
                lr(2), lr(2), ACT.Sqrt), deps=[t_st])

            def d_rstd(v):
                v.reciprocal(lr(2), lr(2))
                return v.scalar_tensor_tensor(lr(3), lr(0), -1.0,
                                              lr(2), ALU.mult, ALU.mult)
            t_rst = P.step("v", d_rstd, deps=[t_sq2])
            if debug:
                t_rst = P.dma("g", 1, lambda gp, fin, to=to: fin(
                    out=ds["lr_d"][0:1, to * 2048:(to + 1) * 2048] if True else None,
                    in_=lrz[0:1, :]), deps=[t_rst])
            t_b1 = P.step("t", lambda t: t.matmul(
                psb[2][:, :], ones1[0:1, :], lr(2),
                start=True, stop=True), deps=[t_rst])
            t_b2 = P.step("t", lambda t: t.matmul(
                psb[3][:, :], ones1[0:1, :], lr(3),
                start=True, stop=True), deps=[t_rst])

            ev_xb = []
            for hc in range(8):
                sl = hc % 2
                t_ld2 = P.dma("g", 1, lambda sy, fin, hc=hc, to=to, sl=sl:
                              fin(out=x2t[:, sl * 512:(sl + 1) * 512],
                                  in_=ds["rs_out"][hc * 128:(hc + 1) * 128,
                                                   to * 512:(to + 1) * 512]),
                              deps=[mm_m[-1], mm_s[-1]] +
                              ([x2_done[(hc - 2, to)]] if hc >= 2 else []))
                t_x2 = P.step("v", lambda v, hc=hc, sl=sl: v.tensor_scalar(
                    x2t[:, sl * 512:(sl + 1) * 512],
                    x2t[:, sl * 512:(sl + 1) * 512],
                    bo_sb[:, hc:hc + 1], None, ALU.add), deps=[t_ld2])
                if debug:
                    t_x2 = P.dma("g", 1, lambda gp, fin, hc=hc, to=to, sl=sl:
                                 fin(out=ds["x2b_d"][hc * 128:(hc + 1) * 128,
                                                     to * 512:(to + 1) * 512],
                                     in_=x2t[:, sl * 512:(sl + 1) * 512]),
                                 deps=[t_x2])
                t_xt = P.step("v", lambda v, hc=hc, to=to, sl=sl: [
                    v.tensor_tensor(big1[:, hc * 2048 + to * 512:
                                         hc * 2048 + (to + 1) * 512],
                                    x2t[:, sl * 512:(sl + 1) * 512],
                                    psb[2][:, :], ALU.mult),
                    v.tensor_tensor(big1[:, hc * 2048 + to * 512:
                                         hc * 2048 + (to + 1) * 512],
                                    big1[:, hc * 2048 + to * 512:
                                         hc * 2048 + (to + 1) * 512],
                                    psb[3][:, :], ALU.add)][-1],
                    deps=[t_b1, t_b2, t_x2])
                xt_toks[(hc, to)] = t_xt
                for sub in range(4):
                    bank = psb[4]
                    bank2 = psb[5]
                    ttr = P.step("t", lambda t, hc=hc, to=to, sub=sub,
                                 bank=bank: t.transpose(
                                     bank[:, :128],
                                     big1[:, hc * 2048 + to * 512 + sub * 128:
                                          hc * 2048 + to * 512 +
                                          (sub + 1) * 128],
                                     ident[:, :]),
                                 deps=[t_xt] +
                                 ([ev_xb[-2]] if len(ev_xb) >= 2 else prev_x2))
                    tev = P.step("v", lambda v, hc=hc, sub=sub, bank=bank:
                                 v.tensor_copy(
                                     xtm4[:, sub * 1024 + hc * 128:
                                          sub * 1024 + (hc + 1) * 128],
                                     bank[:, :128]),
                                 deps=[ttr] + agx_toks[-4:-3])
                    ev_xb.append(tev)
                    t_ba = P.step("v", lambda v, hc=hc, to=to, sub=sub, sl=sl:
                                  v.tensor_tensor(
                                      bfa[:, :],
                                      x2t[:, sl * 512 + sub * 128:
                                          sl * 512 + (sub + 1) * 128],
                                      big1[:, hc * 2048 + to * 512 + sub * 128:
                                           hc * 2048 + to * 512 +
                                           (sub + 1) * 128],
                                      ALU.add), deps=[t_xt] + ev_xb[-1:])
                    ttr2 = P.step("t", lambda t, bank2=bank2: t.transpose(
                        bank2[:, :128], bfa[:, :], ident[:, :]),
                        deps=[t_ba] + ([ev_xb[-2]] if len(ev_xb) >= 2 else []))
                    tev2 = P.step("v", lambda v, hc=hc, sub=sub, bank2=bank2:
                                  v.tensor_copy(
                                      btm4[:, sub * 1024 + hc * 128:
                                           sub * 1024 + (hc + 1) * 128],
                                      bank2[:, :128]),
                                  deps=[ttr2] + base_toks[-4:-3])
                    ev_xb.append(tev2)
                x2_done[(hc, to)] = ev_xb[-1]
            for sub in range(4):
                agx_toks.append(P.dma("g", 1, lambda gp, fin, to=to, sub=sub:
                                      fin(out=ds["agx_in"][
                                          to * 512 + sub * 128:
                                          to * 512 + (sub + 1) * 128, :],
                                          in_=xtm4[:, sub * 1024:
                                                   (sub + 1) * 1024]),
                                      deps=ev_xb))
                base_toks.append(P.dma("g", 1, lambda gp, fin, to=to, sub=sub:
                                       fin(out=ds["base_hbm"][
                                           to * 512 + sub * 128:
                                           to * 512 + (sub + 1) * 128, :],
                                           in_=btm4[:, sub * 1024:
                                                    (sub + 1) * 1024]),
                                       deps=ev_xb))
            prev_x2 = [x2_done[(7, to)]]

            t_lg = P.step("t", lambda t, to=to: [
                t.matmul(psb[6][0:32, :], wr_sb[:, hc * 32:(hc + 1) * 32],
                         big1[:, hc * 2048 + to * 512:
                              hc * 2048 + (to + 1) * 512],
                         start=(hc == 0), stop=(hc == 7))
                for hc in range(8)][-1],
                deps=[xt_toks[(hc, to)] for hc in range(8)] + [t_w] +
                lg_evs[-1:])
            t_le = P.step("v", lambda v: v.tensor_copy(
                lgt[:, :], psb[6][0:32, :]), deps=[t_lg])
            for s_ in range(4):
                ttr = P.step("t", lambda t, s_=s_: t.transpose(
                    psb[6][0:128, 0:32], lgt[:, s_ * 128:(s_ + 1) * 128],
                    ident[0:32, 0:32]),
                    deps=[t_le] + lg_evs[-1:])
                tev = P.step("v", lambda v, to=to, s_=s_: v.tensor_copy(
                    lgtm[:, (to * 4 + s_) * 32:(to * 4 + s_ + 1) * 32],
                    psb[6][0:128, 0:32]), deps=[ttr])
                lg_evs.append(tev)

        t_cc_xt = P.cc("AllGather", ALU.bypass, CC_ALL,
                       [ds["agx_in"].flatten()], [ds["xt_full"].flatten()],
                       deps=agx_toks)
        if debug:
            P.dma("g", 2, lambda gp, fin: [
                fin(out=ds["agx_in_d"][:, :], in_=ds["agx_in"][:, :]),
                fin(out=ds["xt_full_d"][:, :], in_=ds["xt_full"][:, :])]
                and None, deps=[t_cc_xt])
        if debug:
            P.dma("g", 1, lambda gp, fin: fin(out=ds["lg_dbg"][:, :],
                                              in_=lgtm[:, :]), deps=lg_evs)

        oh_toks = []
        rt_toks = []
        prev_tr = None
        prev_r2 = None
        for jp in range(8):
            js = (2 * jp, 2 * jp + 1)

            def d_rt(v, js=js):
                for i, j in enumerate(js):
                    v.reduce_max(scr[:, i * 8:i * 8 + 1],
                                 lgtm[:, j * 32:(j + 1) * 32], AX)
                for i in range(2):
                    v.tensor_scalar_mul(scr[:, i * 8 + 1:i * 8 + 2],
                                        scr[:, i * 8:i * 8 + 1], -1.0)
            t_r1 = P.step("v", d_rt,
                          deps=lg_evs + oh_toks[-1:] +
                          ([prev_r2] if prev_r2 else []))
            t_r2 = P.step("a", lambda sc, js=js: [
                sc.activation(probs[:, i * 32:(i + 1) * 32],
                              lgtm[:, j * 32:(j + 1) * 32], ACT.Exp,
                              bias=scr[:, i * 8 + 1:i * 8 + 2],
                              accum_out=scr[:, i * 8 + 2:i * 8 + 3])
                for i, j in enumerate(js)][-1], deps=[t_r1])
            prev_r2 = t_r2

            def d_rt2(v, js=js):
                for i in range(2):
                    v.reciprocal(scr[:, i * 8 + 3:i * 8 + 4],
                                 scr[:, i * 8 + 2:i * 8 + 3])
                for i in range(2):
                    p_ = probs[:, i * 32:(i + 1) * 32]
                    v.tensor_scalar(p_, p_, scr[:, i * 8 + 3:i * 8 + 4],
                                    None, ALU.mult)
                for i, j in enumerate(js):
                    v.reduce_max(rt[:, j * 8 + 1:j * 8 + 2],
                                 probs[:, i * 32:(i + 1) * 32], AX)
                for i, j in enumerate(js):
                    p_ = probs[:, i * 32:(i + 1) * 32]
                    v.tensor_scalar(p_, p_, rt[:, j * 8 + 1:j * 8 + 2],
                                    None, ALU.is_equal)
                for i in range(2):
                    p_ = probs[:, i * 32:(i + 1) * 32]
                    v.tensor_tensor(p_, p_, w32[:, :], ALU.mult)
                for i in range(2):
                    v.reduce_max(scr[:, i * 8 + 4:i * 8 + 5],
                                 probs[:, i * 32:(i + 1) * 32], AX)
                for i, j in enumerate(js):
                    v.scalar_tensor_tensor(rt[:, j * 8:j * 8 + 1],
                                           scr[:, i * 8 + 4:i * 8 + 5],
                                           -1.0, onesc[:, :],
                                           ALU.mult, ALU.bypass)
                for j in js:
                    v.tensor_scalar_add(rt[:, j * 8:j * 8 + 1],
                                        rt[:, j * 8:j * 8 + 1], 32.0)
                last = None
                for i, j in enumerate(js):
                    last = v.tensor_scalar(probs[:, i * 32:(i + 1) * 32],
                                           iotaE[:, :],
                                           rt[:, j * 8:j * 8 + 1],
                                           None, ALU.is_equal)
                return last
            t_r3 = P.step("v", d_rt2, deps=[t_r2])
            rt_toks.append(t_r3)
            for i, j in enumerate(js):
                ttr = P.step("t", lambda t, i=i: t.transpose(
                    psb[5][0:32, :128], probs[:, i * 32:(i + 1) * 32],
                    ident[:, :]),
                    deps=[t_r3] + ([prev_tr] if prev_tr else []))
                tev = P.step("v", lambda v, j=j: v.tensor_copy(
                    ohm[:, j * 128:(j + 1) * 128], psb[5][0:32, :128]),
                    deps=[ttr])
                prev_tr = tev
                oh_toks.append(tev)

        t_scan = P.step("v", lambda v: v.tensor_tensor_scan(
            ccm[:, :], ohm[:, :], ohm[:, :], 0.0, ALU.add, ALU.bypass),
            deps=oh_toks)
        t_cntd = P.dma("g", 1, lambda gp, fin: fin(
            out=ds["cnt_in"][:].unsqueeze(1), in_=ccm[:, 2047:2048]),
            deps=[t_scan])
        t_cc_cnt = P.cc("AllGather", ALU.bypass, CC_ALL,
                        [ds["cnt_in"][:]], [ds["cnt_out"][:]], deps=[t_cntd])
        t_ld8 = P.dma("g", 1, lambda sy, fin: fin(
            out=cnt8[:, :], in_=ds["cnt_out"].rearrange("(a b) -> a b", b=32)),
            deps=[t_cc_cnt])
        t_off = P.step("t", lambda t: t.matmul(
            psb[7][0:32, 0:1], cnt8[:, :], mrow[:, :], start=True, stop=True),
            deps=[t_ld8, t_w, prev_n4])
        t_ofe = P.step("v", lambda v: v.tensor_copy(offc[:, :],
                                                    psb[7][0:32, 0:1]),
                       deps=[t_off])

        row_toks = []
        rows_guard = agx_toks + base_toks
        for to in range(4):
            sl_ = slice(to * 512, (to + 1) * 512)
            t_o1 = P.step("v", lambda v, sl_=sl_: v.tensor_tensor(
                ohs[:, :], ohm[:, sl_], ccm[:, sl_], ALU.mult),
                deps=[t_scan] + row_toks[-1:] + rows_guard)
            t_dbg_o = None
            if debug and to == 0:
                t_dbg_o = P.dma("g", 1, lambda gp, fin: fin(
                    out=ds["rows_dbg"][5:6, 0:512],
                    in_=ohs[7:8, :]), deps=[t_o1])
            t_m1 = P.step("t", lambda t: t.matmul(
                psb[6][0:1, :], onesc[0:32, :], ohs[:, :],
                start=True, stop=True), deps=[t_o1])
            t_e1 = P.step("v", lambda v: v.tensor_copy(
                wr_(0), psb[6][0:1, :]), deps=[t_m1] + rows_guard)
            if debug and to == 0:
                P.dma("g", 1, lambda gp, fin: fin(
                    out=ds["rows_dbg"][6:7, 0:512],
                    in_=wr_(0)), deps=[t_e1])
                t_e1 = (t_e1, t_dbg_o)[0]
            t_o2 = P.step("v", lambda v, sl_=sl_: v.tensor_scalar(
                ohs[:, :], ohm[:, sl_], offc[:, 0:1], None, ALU.mult),
                deps=[t_e1, t_ofe] + ([t_dbg_o] if t_dbg_o else []))
            t_m2 = P.step("t", lambda t: t.matmul(
                psb[6][0:1, :], onesc[0:32, :], ohs[:, :],
                start=True, stop=True), deps=[t_o2])
            t_e2 = P.step("v", lambda v: v.tensor_copy(
                wr_(1), psb[6][0:1, :]), deps=[t_m2])
            t_o3 = P.step("v", lambda v, sl_=sl_: v.tensor_scalar(
                ohs[:, :], ohm[:, sl_], pcol[0:32, 0:1], None, ALU.mult),
                deps=[t_e2])
            t_m3 = P.step("t", lambda t: t.matmul(
                psb[6][0:1, :], onesc[0:32, :], ohs[:, :],
                start=True, stop=True), deps=[t_o3])
            t_e3 = P.step("v", lambda v: v.tensor_copy(
                wr_(2), psb[6][0:1, :]), deps=[t_m3])

            def d_rows(v, sl_=sl_):
                acc_b = accf_r[0:1, sl_]
                slf_b = slotf_r[0:1, sl_]
                v.tensor_tensor(wr_(3), wr_(0), wr_(1), ALU.add)
                v.tensor_scalar_add(wr_(3), wr_(3), -1.0)
                v.tensor_single_scalar(acc_b, wr_(3), float(C), ALU.is_lt)
                v.scalar_tensor_tensor(wr_(4), wr_(2), float(C), wr_(3),
                                       ALU.mult, ALU.add)
                v.tensor_single_scalar(wr_(3), wr_(3), float(C), ALU.is_ge)
                return v.scalar_tensor_tensor(slf_b, wr_(3), BIG, wr_(4),
                                              ALU.mult, ALU.add)
            row_toks.append(P.step("v", d_rows, deps=[t_e3]))
            if debug and to == 0:
                t_dbg_s = P.dma("g", 1, lambda gp, fin: fin(
                    out=ds["rows_dbg"][7:8, 0:512],
                    in_=slotf_r[0:1, 0:512]), deps=row_toks[-1:])
                row_toks[-1] = t_dbg_s
            if debug:
                P.dma("g", 3, lambda gp, fin, to=to: [
                    fin(out=ds["rows_dbg"][2:3, to * 512:(to + 1) * 512],
                        in_=wr_(0)),
                    fin(out=ds["rows_dbg"][3:4, to * 512:(to + 1) * 512],
                        in_=wr_(1)),
                    fin(out=ds["rows_dbg"][4:5, to * 512:(to + 1) * 512],
                        in_=wr_(2))][-1], deps=row_toks[-1:])
        if debug:
            P.dma("g", 4, lambda gp, fin: [
                fin(out=ds["ohm_dbg"][:, :], in_=ohm[:, :]),
                fin(out=ds["ccm_dbg"][:, :], in_=ccm[:, :]),
                fin(out=ds["rows_dbg"][0:1, :], in_=slotf_r),
                fin(out=ds["rows_dbg"][1:2, :], in_=accf_r)][-1],
                deps=row_toks + [t_scan])

        prev_tr2 = None
        for j in range(16):
            ttr = P.step("t", lambda t, j=j: [
                t.matmul(psb[5][0:128, 0:1],
                         slotf_r[0:1, j * 128:(j + 1) * 128],
                         onesc[0:1, 0:1], start=True, stop=True),
                t.matmul(psb[4][0:128, 0:1],
                         accf_r[0:1, j * 128:(j + 1) * 128],
                         onesc[0:1, 0:1], start=True, stop=True)][-1],
                deps=row_toks + ([prev_tr2] if prev_tr2 else []))
            tev = P.step("v", lambda v: [
                v.tensor_copy(r8t[:, 0:1], psb[5][0:128, 0:1]),
                v.tensor_copy(r8t[:, 5:6], psb[4][0:128, 0:1])][-1],
                deps=[ttr])

            def d_rt3(v, j=j):
                v.tensor_copy(rt[:, j * 8 + 2:j * 8 + 3], r8t[:, 0:1])
                v.tensor_tensor(rt[:, j * 8 + 3:j * 8 + 4],
                                rt[:, j * 8 + 1:j * 8 + 2], r8t[:, 5:6],
                                ALU.mult)
                return v.tensor_scalar_min(rt[:, j * 8 + 4:j * 8 + 5],
                                           r8t[:, 0:1], float(EC - 1))
            prev_tr2 = P.step("v", d_rt3, deps=[tev])
            rt_toks.append(prev_tr2)
        if debug:
            P.dma("g", 1, lambda gp, fin: fin(out=ds["rt_dbg"][:, :],
                                              in_=rt[:, :]), deps=rt_toks)

        t_slotd = P.dma("g", 1, lambda gp, fin: fin(
            out=ds["slot_in"][:].unsqueeze(0), in_=slotf_r),
            deps=row_toks)
        t_cc_slot = P.cc("AllGather", ALU.bypass, CC_ALL,
                         [ds["slot_in"][:]], [ds["slot_out"][:]],
                         deps=[t_slotd])
        if debug:
            P.dma("g", 1, lambda gp, fin: fin(out=ds["slot_out_d"][:],
                                              in_=ds["slot_out"][:]),
                  deps=[t_cc_slot])

        t_zb = P.step("v", lambda v: v.memset(zb[:, :], 0.0),
                      deps=base_toks + rt_toks)
        t_zero = P.dma("g", 20, lambda gp, fin: [
            fin(out=ds["buf"][i * 128:(i + 1) * 128, :], in_=zb[:, :])
            for i in range(20)][-1], deps=[t_zb])
        t_lds0 = P.dma("g", 1, lambda sy, fin: fin(
            out=slot_sb[0:128, :],
            in_=ds["slot_out"].rearrange("(p a) -> p a", a=128)),
            deps=[t_cc_slot] + base_toks + rt_toks)
        t_lds1 = P.step("t", lambda t: t.transpose(
            psb[5][:, :128], slot_sb[0:128, :], ident[:, :]),
            deps=[t_lds0, prev_tr2])
        t_lds = P.step("v", lambda v: v.tensor_copy(
            slot_sb[0:128, :], psb[5][:, :128]), deps=[t_lds1])
        scat_toks = []
        for p2 in range(64):
            ja, jb = 2 * p2, 2 * p2 + 1
            t_la = P.dma("g", 1, lambda sy, fin, ja=ja: fin(
                out=sct[:, 0:1024],
                in_=ds["xt_full"][ja * 128:(ja + 1) * 128, :]),
                deps=[t_cc_xt, t_cc_slot] + rows_guard + rt_toks +
                scat_toks[-2:])
            t_lbx = P.dma("g", 1, lambda sy, fin, jb=jb: fin(
                out=sct[:, 1024:2048],
                in_=ds["xt_full"][jb * 128:(jb + 1) * 128, :]),
                deps=scat_toks[-2:])

            def d_idx(v, ja=ja, jb=jb):
                for i, j2 in enumerate((ja, jb)):
                    v.tensor_tensor(idxt[:, i * 4:i * 4 + 1],
                                    slot_sb[0:128, j2:j2 + 1],
                                    meta_sb[:, 0:1], ALU.subtract)
                for i in range(2):
                    v.tensor_single_scalar(idxt[:, i * 4 + 1:i * 4 + 2],
                                           idxt[:, i * 4:i * 4 + 1], 0.0,
                                           ALU.is_lt)
                for i in range(2):
                    v.scalar_tensor_tensor(idxt[:, i * 4 + 2:i * 4 + 3],
                                           idxt[:, i * 4 + 1:i * 4 + 2], BIG,
                                           idxt[:, i * 4:i * 4 + 1],
                                           ALU.mult, ALU.add)
                for i in range(2):
                    v.tensor_scalar_min(idxt[:, i * 4 + 2:i * 4 + 3],
                                        idxt[:, i * 4 + 2:i * 4 + 3], 4096.0)
                last = None
                for i in range(2):
                    last = v.tensor_copy(idxi[:, i:i + 1],
                                         idxt[:, i * 4 + 2:i * 4 + 3])
                return last
            t_ix = P.step("v", d_idx,
                          deps=[t_lds, t_w] + scat_toks[-2:])
            for i in range(2):
                scat_toks.append(P.dma("g", 1, lambda gp, fin, i=i: fin(
                    nc.gpsimd.indirect_dma_start(
                        out=ds["buf"][:, :],
                        out_offset=IndirectOffsetOnAxis(idxi[:, i:i + 1], 0),
                        in_=sct[:, i * 1024:(i + 1) * 1024],
                        in_offset=None,
                        bounds_check=bc_scat,
                        oob_is_err=False)),
                    deps=[t_la, t_lbx, t_ix, t_zero]))

        bfm = reg(0, 5120)
        hgm = reg(5120, 10240)
        yfm = reg(10240, 15360)
        ytm = reg(15360, 16384)

        agy_toks = []
        prev_gu = []
        prev_dn = []
        for le in range(ELOC):
            t_wg = P.dma("sy", 8, lambda sy, fin, le=le: [
                fin(out=wbuf[:, hc * 1024:(hc + 1) * 1024],
                    in_=din["wg"][le, hc * 128:(hc + 1) * 128, :])
                for hc in range(8)][-1],
                deps=[mm_toks_c[7]] + prev_gu + prev_dn)
            t_wu = P.dma("sy", 8, lambda sy, fin, le=le: [
                fin(out=wbuf[:, 8192 + hc * 1024:8192 + (hc + 1) * 1024],
                    in_=din["wu"][le, hc * 128:(hc + 1) * 128, :])
                for hc in range(8)][-1],
                deps=[mm_toks_c[7]] + prev_gu + prev_dn)
            prev_gu, prev_dn = [], []
            bf_toks = []
            bt_prev = []
            for bt in range(5):
                t_lb = P.dma("sy", 1, lambda sy, fin, le=le, bt=bt:
                             fin(out=ytm[:, :],
                                 in_=ds["buf"][le * C + bt * 128:
                                               le * C + (bt + 1) * 128, :]),
                             deps=scat_toks + bt_prev + agy_toks[-1:])
                evs = []
                for hc in range(8):
                    bank = psb[6 + hc % 2]
                    ttr = P.step("t", lambda t, hc=hc, bank=bank:
                                 t.transpose(bank[:, :128],
                                             ytm[:, hc * 128:(hc + 1) * 128],
                                             ident[:, :]),
                                 deps=[t_lb] +
                                 ([evs[-2]] if len(evs) >= 2 else
                                  (bf_toks[-1:] if bf_toks
                                   else agy_toks[-1:])))
                    tev = P.step("v", lambda v, hc=hc, bt=bt, bank=bank:
                                 v.tensor_copy(bfm[:, hc * 640 + bt * 128:
                                                   hc * 640 + (bt + 1) * 128],
                                               bank[:, :128]), deps=[ttr])
                    evs.append(tev)
                bf_toks += evs
                bt_prev = [evs[-1]]
            hg_toks = []
            for half in range(2):
                for fj in range(8):
                    gb = psb[(fj % 2) * 2]
                    ub = psb[(fj % 2) * 2 + 1]
                    t_g = P.step("t", lambda t, half=half, fj=fj, gb=gb: [
                        t.matmul(gb[:, 0:320],
                                 wbuf[:, hc * 1024 + fj * 128:
                                      hc * 1024 + (fj + 1) * 128],
                                 bfm[:, hc * 640 + half * 320:
                                     hc * 640 + half * 320 + 320],
                                 start=(hc == 0), stop=(hc == 7))
                        for hc in range(8)][-1],
                        deps=bf_toks + [t_wg] + hg_toks[-2:-1])
                    t_u = P.step("t", lambda t, half=half, fj=fj, ub=ub: [
                        t.matmul(ub[:, 0:320],
                                 wbuf[:, 8192 + hc * 1024 + fj * 128:
                                      8192 + hc * 1024 + (fj + 1) * 128],
                                 bfm[:, hc * 640 + half * 320:
                                     hc * 640 + half * 320 + 320],
                                 start=(hc == 0), stop=(hc == 7))
                        for hc in range(8)][-1],
                        deps=[t_wu] + hg_toks[-2:-1])
                    prev_gu += [t_g, t_u]
                    t_si = P.step("a", lambda sc, gb=gb: sc.activation(
                        ytm[:, 0:320], gb[:, 0:320], ACT.Silu),
                        deps=[t_g] + hg_toks[-1:])
                    t_hg = P.step("v", lambda v, half=half, fj=fj, ub=ub:
                                  v.tensor_tensor(
                                      hgm[:, fj * 640 + half * 320:
                                          fj * 640 + half * 320 + 320],
                                      ytm[:, 0:320], ub[:, 0:320],
                                      ALU.mult),
                                  deps=[t_si, t_u])
                    hg_toks.append(t_hg)
            t_wd = P.dma("sy", 8, lambda sy, fin, le=le: [
                fin(out=wbuf[:, hc * 1024:(hc + 1) * 1024],
                    in_=din["wd"][le, hc * 128:(hc + 1) * 128, :])
                for hc in range(8)][-1], deps=prev_gu)
            y_toks = []
            for half in range(2):
                for hj in range(8):
                    yb = psb[4 + (hj % 2)]
                    t_y = P.step("t", lambda t, half=half, hj=hj, yb=yb: [
                        t.matmul(yb[:, 0:320],
                                 wbuf[:, fc * 1024 + hj * 128:
                                      fc * 1024 + (hj + 1) * 128],
                                 hgm[:, fc * 640 + half * 320:
                                     fc * 640 + half * 320 + 320],
                                 start=(fc == 0), stop=(fc == 7))
                        for fc in range(8)][-1],
                        deps=hg_toks + [t_wd] + y_toks[-2:-1])
                    prev_dn.append(t_y)
                    t_ye = P.step("v", lambda v, half=half, hj=hj, yb=yb:
                                  v.tensor_copy(
                                      yfm[:, hj * 640 + half * 320:
                                          hj * 640 + half * 320 + 320],
                                      yb[:, 0:320]), deps=[t_y])
                    y_toks.append(t_ye)
            for bt in range(5):
                evs = []
                for hc in range(8):
                    bank = psb[6 + hc % 2]
                    ttr = P.step("t", lambda t, bt=bt, hc=hc, bank=bank:
                                 t.transpose(bank[:, :128],
                                             yfm[:, hc * 640 + bt * 128:
                                                 hc * 640 + (bt + 1) * 128],
                                             ident[:, :]),
                                 deps=y_toks + ([evs[-2]] if len(evs) >= 2
                                                else agy_toks[-1:] +
                                                hg_toks[-1:]))
                    tev = P.step("v", lambda v, hc=hc, bank=bank:
                                 v.tensor_copy(ytm[:, hc * 128:(hc + 1) * 128],
                                               bank[:, :128]),
                                 deps=[ttr] + agy_toks[-1:] + bt_prev)
                    evs.append(tev)
                agy_toks.append(P.dma("g", 1, lambda gp, fin, le=le, bt=bt:
                                      fin(out=ds["agy_in"][
                                          le * C + bt * 128:
                                          le * C + (bt + 1) * 128, :],
                                          in_=ytm[:, :]), deps=evs))

        t_cc_y = P.cc("AllGather", ALU.bypass, CC_ALL,
                      [ds["agy_in"].flatten()], [ds["y_full"].flatten()],
                      deps=agy_toks)
        if debug:
            P.dma("g", 1, lambda gp, fin: fin(out=ds["y_full_d"][:, :],
                                              in_=ds["y_full"][:, :]),
                  deps=[t_cc_y])

        gat = reg(0, 2048)
        bstl = reg(2048, 4096)
        outt = reg(4096, 6144)
        out_toks = []
        for j in range(16):
            ssl = j % 2
            t_gi = P.step("v", lambda v, j=j, ssl=ssl: v.tensor_copy(
                gidxi[:, ssl:ssl + 1], rt[:, j * 8 + 4:j * 8 + 5]),
                deps=rt_toks + out_toks[-2:-1])
            t_lb = P.dma("g", 1, lambda sy, fin, j=j, ssl=ssl: fin(
                out=bstl[:, ssl * 1024:(ssl + 1) * 1024],
                in_=ds["base_hbm"][j * 128:(j + 1) * 128, :]),
                deps=base_toks + [t_cc_y] + out_toks[-2:-1])
            t_ga = P.dma("g", 1, lambda gp, fin, ssl=ssl: fin(
                nc.gpsimd.indirect_dma_start(
                    out=gat[:, ssl * 1024:(ssl + 1) * 1024],
                    out_offset=None,
                    in_=ds["y_full"][:, :],
                    in_offset=IndirectOffsetOnAxis(gidxi[:, ssl:ssl + 1], 0),
                    bounds_check=bc_gath,
                    oob_is_err=False)),
                deps=[t_cc_y, t_gi] + out_toks[-2:-1])
            t_fo = P.step("v", lambda v, j=j, ssl=ssl:
                          v.scalar_tensor_tensor(
                              outt[:, ssl * 1024:(ssl + 1) * 1024],
                              gat[:, ssl * 1024:(ssl + 1) * 1024],
                              rt[:, j * 8 + 3:j * 8 + 4],
                              bstl[:, ssl * 1024:(ssl + 1) * 1024],
                              ALU.mult, ALU.add),
                          deps=[t_ga, t_lb] + out_toks[-2:-1])
            out_toks.append(P.dma("g", 1, lambda gp, fin, j=j, ssl=ssl: fin(
                out=out_own[j * 128:(j + 1) * 128, :],
                in_=outt[:, ssl * 1024:(ssl + 1) * 1024]), deps=[t_fo]))

        P.step("g", lambda gp: None, deps=out_toks)

        block = es.enter_context(nc.Block())
        P.finalize(block)
    return nc, din, ds, out_own


_CACHE = {}


def _get_program(debug=False):
    key = bool(debug)
    if key not in _CACHE:
        _CACHE[key] = build_program(debug=debug)
    return _CACHE[key]


def make_in_maps(inputs):
    x = np.asarray(inputs["L0_attn_x"], dtype=np.float32)
    Wq = np.asarray(inputs["L0_attn_Wq"], dtype=np.float32)
    Wk = np.asarray(inputs["L0_attn_Wk"], dtype=np.float32)
    Wv = np.asarray(inputs["L0_attn_Wv"], dtype=np.float32)
    Wo = np.asarray(inputs["L0_attn_Wo"], dtype=np.float32)
    bq = np.asarray(inputs["L0_attn_bq"], dtype=np.float32)
    bk = np.asarray(inputs["L0_attn_bk"], dtype=np.float32)
    bv = np.asarray(inputs["L0_attn_bv"], dtype=np.float32)
    bo = np.asarray(inputs["L0_attn_bo"], dtype=np.float32)
    wr = np.asarray(inputs["L0_ffn_w_router"], dtype=np.float32)
    wg = np.asarray(inputs["L0_ffn_w_gate"], dtype=np.float32)
    wu = np.asarray(inputs["L0_ffn_w_up"], dtype=np.float32)
    wd = np.asarray(inputs["L0_ffn_w_down"], dtype=np.float32)
    maps = []
    for c in range(NCORE):
        b, hh = c // 2, c % 2
        sl = slice(hh * 512, (hh + 1) * 512)
        meta = np.zeros((128, 4), np.float32)
        meta[:, 0] = c * ELOC * C
        mr = np.zeros(NCORE, np.float32)
        mr[:c] = 1.0
        maps.append({
            "x_b": np.ascontiguousarray(x[b]),
            "wqT": np.ascontiguousarray(Wq[sl, :].T),
            "wkT": np.ascontiguousarray(Wk[sl, :].T),
            "wvT": np.ascontiguousarray(Wv[sl, :].T),
            "bqs": np.ascontiguousarray(bq[sl]),
            "bks": np.ascontiguousarray(bk[sl]),
            "bvs": np.ascontiguousarray(bv[sl]),
            "woT": np.ascontiguousarray(Wo[:, sl].T),
            "bo": bo,
            "w_router": wr,
            "wg": np.ascontiguousarray(wg[c * ELOC:(c + 1) * ELOC]),
            "wu": np.ascontiguousarray(wu[c * ELOC:(c + 1) * ELOC]),
            "wd": np.ascontiguousarray(wd[c * ELOC:(c + 1) * ELOC]),
            "meta": meta,
            "maskrow": mr,
        })
    return maps


def kernel(**inputs):
    nc, din, ds, out_own = _get_program(debug=False)
    maps = make_in_maps(inputs)
    res = run_bass_kernel_spmd(nc, maps, core_ids=list(range(NCORE)))
    out = np.empty((B, S, H), np.float32)
    for c in range(NCORE):
        b, hh = c // 2, c % 2
        out[b, hh * TOWN:(hh + 1) * TOWN, :] = np.asarray(
            res.results[c]["out_own"])
    return out


# version 06
# speedup vs baseline: 1.2703x; 1.2703x over previous
import contextlib

import numpy as np

import concourse.bass as bass
import concourse.mybir as mybir
from concourse.bass_core_rust import IndirectOffsetOnAxis
from concourse.bass_utils import run_bass_kernel_spmd

F32 = mybir.dt.float32
BF16 = mybir.dt.bfloat16
I32 = mybir.dt.int32
ALU = mybir.AluOpType
ACT = mybir.ActivationFunctionType
AX = mybir.AxisListType.X

B, S, H = 4, 4096, 1024
E, C, FF = 32, 640, 1024
TOWN = 2048
EC = E * C
ELOC = 4
BIG = float(1 << 26)
EPS = 1e-5
NCORE = 8
CC_ALL = [[0, 1, 2, 3, 4, 5, 6, 7]]
CC_PAIRS = [[0, 1], [2, 3], [4, 5], [6, 7]]


class Prog:

    BODIES = ("t", "v", "a", "g", "sy")

    def __init__(self, nc, es):
        self.nc = nc
        names = ("t", "v", "a", "g", "gd", "sy", "cc")
        self.sem = {n: es.enter_context(nc.semaphore("sem_" + n)) for n in names}
        self.cnt = {n: 0 for n in names}
        self.waited = {b: {} for b in self.BODIES}
        self.stream = {b: [] for b in self.BODIES}

    def _waits(self, body, deps):
        own = {"t": "t", "v": "v", "a": "a", "g": "g", "sy": None}[body]
        need = {}
        for d in deps:
            if d is None:
                continue
            dn, dv = d
            if dn == own:
                continue
            if dv > self.waited[body].get(dn, 0):
                need[dn] = max(need.get(dn, 0), dv)
        out = []
        for dn, dv in need.items():
            self.waited[body][dn] = dv
            out.append((self.sem[dn], dv))
        return out

    def step(self, body, fn, deps=(), fast=False):
        waits = self._waits(body, list(deps))
        self.cnt[body] += 1
        tok = (body, self.cnt[body])
        sem = self.sem[body]
        drain = body == "t" and not fast

        def emit(eng, waits=waits, fn=fn, drain=drain):
            for s_, v_ in waits:
                eng.wait_ge(s_, v_)
            res = fn(eng)
            if drain:
                res = eng.drain()
            if res is not None and hasattr(res, "then_inc"):
                res.then_inc(sem, 1)
            else:
                eng.sem_inc(sem, 1)
        self.stream[body].append(emit)
        return tok

    def dma(self, body, n, fn, deps=()):
        sname = "gd" if body == "g" else "sy"
        waits = self._waits(body, list(deps))
        self.cnt[sname] += 16 * n
        tok = (sname, self.cnt[sname])
        sem = self.sem[sname]
        issued = [0]

        def emit(eng, waits=waits, fn=fn, n=n):
            for s_, v_ in waits:
                eng.wait_ge(s_, v_)

            def fin(instr=None, **kw):
                if instr is None:
                    instr = eng.dma_start(**kw)
                issued[0] += 1
                return instr.then_inc(sem, 16)
            fn(eng, fin)
            assert issued[0] == n, f"dma count {issued[0]} != {n}"
        self.stream[body].append(emit)
        return tok

    def cc(self, kind, op, groups, ins, outs, deps=()):
        waits = self._waits("g", list(deps))
        self.cnt["cc"] += 1
        tok = ("cc", self.cnt["cc"])
        sem = self.sem["cc"]

        def emit(eng, waits=waits):
            for s_, v_ in waits:
                eng.wait_ge(s_, v_)
            eng.collective_compute(kind, op, replica_groups=groups,
                                   ins=ins, outs=outs).then_inc(sem, 1)
        self.stream["g"].append(emit)
        return tok

    def finalize(self, block):
        hosts = {"t": block.tensor, "v": block.vector, "a": block.scalar,
                 "g": block.gpsimd, "sy": block.sync}
        for body in self.BODIES:
            emitters = self.stream[body]

            def run_all(eng, emitters=emitters):
                for em in emitters:
                    em(eng)
            hosts[body](run_all)


def build_program(debug=False):
    nc = bass.Bass("TRN2", target_bir_lowering=False, num_devices=NCORE,
                   detect_race_conditions=False)

    din = {}
    for nm, shp in [
        ("x_b", [S, H]),
        ("wqT", [H, 512]), ("wkT", [H, 512]), ("wvT", [H, 512]),
        ("bqs", [512]), ("bks", [512]), ("bvs", [512]),
        ("woT", [512, H]), ("bo", [H]),
        ("w_router", [H, E]),
        ("wg", [ELOC, H, FF]), ("wu", [ELOC, H, FF]), ("wd", [ELOC, FF, H]),
        ("meta", [128, 4]),
        ("maskrow", [NCORE]),
    ]:
        din[nm] = nc.dram_tensor(nm, shp, F32, kind="ExternalInput").ap()

    out_own = nc.dram_tensor("out_own", [TOWN, H], F32, kind="ExternalOutput").ap()

    dkind = "ExternalOutput" if debug else "Internal"
    ds = {}
    for nm, shp in [
        ("k_hbm", [512, S]), ("v_hbm", [S, 512]), ("o_hbm", [512, S]),
        ("xf_full", [H, S]),
        ("base_hbm", [TOWN, H]),
        ("buf", [ELOC * C, H]),
    ]:
        ds[nm] = nc.dram_tensor(nm, shp, F32, kind=dkind).ap()
    for nm, shp in [
        ("rs_in", [2, H, TOWN]), ("rs_out", [H, TOWN]),
        ("cnt_in", [E]), ("cnt_out", [NCORE * E]),
        ("slot_in", [TOWN]), ("slot_out", [NCORE * TOWN]),
        ("agx_in", [TOWN, H]), ("xt_full", [NCORE * TOWN, H]),
        ("agy_in", [ELOC * C, H]), ("y_full", [EC, H]),
    ]:
        ds[nm] = nc.dram_tensor(nm, shp, F32, kind="Internal").ap()
    if debug:
        for nm, shp in [("rs_out_d", [H, TOWN]), ("agx_in_d", [TOWN, H]),
                        ("xt_full_d", [NCORE * TOWN, H]),
                        ("y_full_d", [EC, H]),
                        ("slot_out_d", [NCORE * TOWN])]:
            ds[nm] = nc.dram_tensor(nm, shp, F32, kind="ExternalOutput").ap()
    if debug:
        for nm, shp in [("q_dbg", [128, 16384]), ("rt_dbg", [128, 16 * 8]),
                        ("ln_dbg", [128, 4096]), ("st_dbg", [128, 16]),
                        ("rsin_d", [2, H, TOWN]), ("x2_d", [H, TOWN]),
                        ("x2b_d", [H, TOWN]),
                        ("lr_d", [4, 4 * 2048]),
                        ("lg_dbg", [128, 512]), ("ohm_dbg", [32, TOWN]),
                        ("ccm_dbg", [32, TOWN]), ("rows_dbg", [8, TOWN]),
                        ("m_dbg", [1, 16])]:
            ds[nm] = nc.dram_tensor(nm, shp, F32, kind="ExternalOutput").ap()

    es = contextlib.ExitStack()
    with es:
        def sbuf(name, shp, dt_=F32):
            return es.enter_context(nc.sbuf_tensor(name, shp, dt_))

        wbuf = sbuf("wbuf", [128, 16384])
        big1 = sbuf("big1", [128, 16384])
        scp = sbuf("scp", [128, 16384])

        def reg(a, b):
            return scp[:, a:b]

        ident = sbuf("ident", [128, 128])
        identi = sbuf("identi", [128, 128], I32)
        pcol = sbuf("pcol", [128, 1])
        e2ind = sbuf("e2ind", [128, 2])
        sel01 = sbuf("sel01", [128, 2])
        ones1 = sbuf("ones1", [1, 128])
        onesc = sbuf("onesc", [128, 1])
        bia = sbuf("bia", [128, 12])
        bo_sb = sbuf("bo_sb", [128, 8])
        wr_sb = sbuf("wr_sb", [128, 8 * 32])
        iotaE = sbuf("iotaE", [128, 32])
        iotaEi = sbuf("iotaEi", [128, 32], I32)
        w32 = sbuf("w32", [128, 32])
        w32i = sbuf("w32i", [128, 32], I32)
        probs = sbuf("probs", [128, 64])
        scr = sbuf("scr", [128, 16])
        rt = sbuf("rt", [128, 16 * 8])
        meta_sb = sbuf("meta_sb", [128, 4])
        mrow = sbuf("mrow", [8, 1])
        offc = sbuf("offc", [32, 1])
        cnt8 = sbuf("cnt8", [8, 32])
        r8t = sbuf("r8t", [128, 8])
        idxt = sbuf("idxt", [128, 8])
        idxi = sbuf("idxi", [128, 2], I32)
        gidxi = sbuf("gidxi", [128, 2], I32)
        nmh = sbuf("nmh", [128, 2])
        srow = sbuf("srow", [1, 16])
        stat = sbuf("stat", [128, 16])
        kn2m = sbuf("kn2m", [2, 4])
        qn2m = sbuf("qn2m", [2, 4])
        masks = sbuf("masks", [128, 4 * 512])
        maski = sbuf("maski", [128, 512], I32)
        bfa = sbuf("bfa", [128, 128])

        psb = [es.enter_context(nc.psum_tensor(f"psb{i}", [128, 512], F32))
               for i in range(8)]

        P = Prog(nc, es)
        bc_scat = nc.gpsimd.alloc_register("bc_scat")
        bc_gath = nc.gpsimd.alloc_register("bc_gath")

        mask_toks = []
        t_iota0 = P.step("g", lambda gp: [
            gp.reg_mov(bc_scat, ELOC * C - 1),
            gp.reg_mov(bc_gath, EC - 1),
            gp.iota(identi[:, :], pattern=[[-1, 128]], base=0,
                    channel_multiplier=1),
            gp.iota(iotaEi[:, :], pattern=[[1, 32]], base=0,
                    channel_multiplier=0),
            gp.iota(w32i[:, :], pattern=[[-1, 32]], base=32,
                    channel_multiplier=0)][-1])
        t_c0 = P.step("v", lambda v: [
            v.tensor_single_scalar(ident[:, :], identi[:, :], 0, ALU.is_equal),
            v.tensor_copy(pcol[:, :], identi[:, 0:1]),
            v.tensor_single_scalar(e2ind[:, 0:1], identi[:, 0:1], 64, ALU.is_lt),
            v.tensor_single_scalar(e2ind[:, 1:2], identi[:, 0:1], 63, ALU.is_gt),
            v.tensor_single_scalar(sel01[:, 0:1], identi[:, 0:1], 0, ALU.is_equal),
            v.tensor_single_scalar(sel01[:, 1:2], identi[:, 0:1], 1, ALU.is_equal),
            v.tensor_copy(iotaE[:, :], iotaEi[:, :]),
            v.tensor_copy(w32[:, :], w32i[:, :]),
            v.memset(ones1[:, :], 1.0),
            v.memset(onesc[:, :], 1.0),
            v.memset(nmh[:, 1:2], 0.125),
            v.memset(kn2m[:, :], 0.0),
            v.memset(qn2m[:, :], 0.0),
            v.memset(rt[:, :], 0.0),
            v.memset(probs[:, :], 0.0),
            v.memset(scr[:, :], 0.0),
            v.memset(srow[:, :], 0.0),
            v.memset(stat[:, :], 0.0)][-1], deps=[t_iota0])
        for i, dlt in enumerate([0, -128, -256, -384]):
            tm1 = P.step("g", lambda gp, dlt=dlt: gp.iota(
                maski[:, :], pattern=[[1, 512]], base=dlt,
                channel_multiplier=-1), deps=mask_toks[-1:] or [t_iota0])
            mask_toks.append(P.step("v", lambda v, i=i: v.tensor_single_scalar(
                masks[:, i * 512:(i + 1) * 512], maski[:, :], -1, ALU.is_gt),
                deps=[tm1]))

        def w_loads(sy, fin):
            for i, nm in enumerate(["wqT", "wkT", "wvT"]):
                for hc in range(8):
                    fin(out=wbuf[:, i * 4096 + hc * 512:i * 4096 + (hc + 1) * 512],
                        in_=din[nm][hc * 128:(hc + 1) * 128, :])
            for dc in range(4):
                fin(out=wbuf[:, 12288 + dc * 1024:12288 + (dc + 1) * 1024],
                    in_=din["woT"][dc * 128:(dc + 1) * 128, :])
            for i, nm in enumerate(["bqs", "bks", "bvs"]):
                for mj in range(4):
                    fin(out=bia[:, i * 4 + mj:i * 4 + mj + 1],
                        in_=din[nm][mj * 128:(mj + 1) * 128].unsqueeze(1))
            for hc in range(8):
                fin(out=bo_sb[:, hc:hc + 1],
                    in_=din["bo"][hc * 128:(hc + 1) * 128].unsqueeze(1))
            for hc in range(8):
                fin(out=wr_sb[:, hc * 32:(hc + 1) * 32],
                    in_=din["w_router"][hc * 128:(hc + 1) * 128, :])
            fin(out=meta_sb[:, :], in_=din["meta"][:, :])
            fin(out=mrow[:, :], in_=din["maskrow"][:].unsqueeze(1))
        t_w = P.dma("sy", 28 + 12 + 8 + 8 + 2, w_loads)

        xa = reg(0, 4096)
        lnta = reg(4096, 8192)
        lnf = reg(8192, 12288)
        sqt = reg(12288, 13312)
        ktmp = reg(13312, 13824)
        vtmp = reg(13824, 14336)
        ksq = reg(14336, 14848)
        evt = reg(14848, 15360)

        xa_readers, lnf_readers, lnta_guard = [], [], []
        prev_evac = {}
        kv_store_toks = []
        xf_store_toks = []
        q_evac_toks = []
        t_qn2_last = None
        t_kn2_last = None
        xf_ev_m2 = None

        for tau in range(8):
            t0 = tau * 512
            t_ldx = P.dma("g", 4, lambda sy, fin, t0=t0: [
                fin(out=xa[:, s4 * 1024:(s4 + 1) * 1024],
                    in_=din["x_b"][t0 + s4 * 128:t0 + (s4 + 1) * 128, :])
                for s4 in range(4)][-1],
                deps=xa_readers + [t_c0])
            xa_readers = []

            t_sums = P.step("v", lambda v: [
                v.reduce_sum(stat[:, s4 * 4:s4 * 4 + 1],
                             xa[:, s4 * 1024:(s4 + 1) * 1024], AX)
                for s4 in range(4)][-1], deps=[t_ldx])
            t_sq = P.step("a", lambda sc: [
                sc.activation(lnta[:, s4 * 1024:(s4 + 1) * 1024],
                              xa[:, s4 * 1024:(s4 + 1) * 1024], ACT.Square,
                              accum_out=stat[:, s4 * 4 + 1:s4 * 4 + 2])
                for s4 in range(4)][-1], deps=[t_ldx] + lnta_guard)
            xa_readers += [t_sums, t_sq]

            def fin_stats(v):
                for s4 in range(4):
                    c0 = s4 * 4
                    v.tensor_scalar_mul(stat[:, c0:c0 + 1],
                                        stat[:, c0:c0 + 1], 1.0 / 1024)
                for s4 in range(4):
                    c0 = s4 * 4
                    v.tensor_scalar_mul(stat[:, c0 + 1:c0 + 2],
                                        stat[:, c0 + 1:c0 + 2], 1.0 / 1024)
                for s4 in range(4):
                    c0 = s4 * 4
                    v.tensor_tensor(stat[:, c0 + 2:c0 + 3], stat[:, c0:c0 + 1],
                                    stat[:, c0:c0 + 1], ALU.mult)
                for s4 in range(4):
                    c0 = s4 * 4
                    v.tensor_tensor(stat[:, c0 + 2:c0 + 3],
                                    stat[:, c0 + 1:c0 + 2],
                                    stat[:, c0 + 2:c0 + 3], ALU.subtract)
                for s4 in range(4):
                    c0 = s4 * 4
                    last = v.tensor_scalar_add(stat[:, c0 + 2:c0 + 3],
                                               stat[:, c0 + 2:c0 + 3], EPS)
                return last
            t_fs = P.step("v", fin_stats, deps=[t_sums, t_sq])
            t_sqr = P.step("a", lambda sc: [
                sc.activation(stat[:, s4 * 4 + 3:s4 * 4 + 4],
                              stat[:, s4 * 4 + 2:s4 * 4 + 3], ACT.Sqrt)
                for s4 in range(4)][-1], deps=[t_fs])

            def fin_rstd(v):
                for s4 in range(4):
                    c0 = s4 * 4
                    v.reciprocal(stat[:, c0 + 2:c0 + 3],
                                 stat[:, c0 + 3:c0 + 4])
                for s4 in range(4):
                    c0 = s4 * 4
                    last = v.scalar_tensor_tensor(stat[:, c0 + 3:c0 + 4],
                                                  stat[:, c0:c0 + 1], -1.0,
                                                  stat[:, c0 + 2:c0 + 3],
                                                  ALU.mult, ALU.mult)
                return last
            t_rs = P.step("v", fin_rstd, deps=[t_sqr])
            t_ln = P.step("a", lambda sc: [
                sc.activation(lnta[:, s4 * 1024:(s4 + 1) * 1024],
                              xa[:, s4 * 1024:(s4 + 1) * 1024], ACT.Identity,
                              bias=stat[:, s4 * 4 + 3:s4 * 4 + 4],
                              scale=stat[:, s4 * 4 + 2:s4 * 4 + 3])
                for s4 in range(4)][-1], deps=[t_rs] + lnta_guard)
            xa_readers.append(t_ln)

            ev_prev = []
            for i, (s4, hc) in enumerate((s4, hc) for s4 in range(4)
                                         for hc in range(8)):
                bank = psb[4 + i % 2]
                ttr = P.step("t", lambda t, s4=s4, hc=hc, bank=bank:
                             t.transpose(bank[:, :128],
                                         lnta[:, s4 * 1024 + hc * 128:
                                              s4 * 1024 + (hc + 1) * 128],
                                         ident[:, :]),
                             deps=[t_ln, t_c0] +
                             (lnf_readers if i == 0 else []) +
                             ([ev_prev[i - 2]] if i >= 2 else []))
                tev = P.step("v", lambda v, s4=s4, hc=hc, bank=bank:
                             v.tensor_copy(lnf[:, hc * 512 + s4 * 128:
                                               hc * 512 + (s4 + 1) * 128],
                                           bank[:, :128]), deps=[ttr])
                ev_prev.append(tev)
            lnf_ready = list(ev_prev)
            lnta_guard = ev_prev[-2:]
            lnf_readers = []
            if debug and tau == 0:
                tdbg = P.dma("g", 2, lambda gp, fin: [
                    fin(out=ds["ln_dbg"][:, :], in_=lnf[:, :]),
                    fin(out=ds["st_dbg"][:, :], in_=stat[:, :])][-1],
                    deps=lnf_ready + [t_rs])
                lnf_readers.append(tdbg)

            for hc in range(8):
                evs = []
                for s4 in range(4):
                    i = hc * 4 + s4
                    bank = psb[6 + i % 2]
                    dep_dma = [xf_store_toks[-2]] if len(xf_store_toks) >= 2 \
                        else []
                    ttr = P.step("t", lambda t, s4=s4, hc=hc, bank=bank:
                                 t.transpose(bank[:, :128],
                                             xa[:, s4 * 1024 + hc * 128:
                                                s4 * 1024 + (hc + 1) * 128],
                                             ident[:, :]),
                                 deps=[t_ldx, t_c0] + dep_dma +
                                 ([evs[-2]] if len(evs) >= 2 else
                                  ([xf_ev_m2] if xf_ev_m2 else [])))
                    tev = P.step("v", lambda v, s4=s4, bank=bank:
                                 v.tensor_copy(evt[:, s4 * 128:(s4 + 1) * 128],
                                               bank[:, :128]),
                                 deps=[ttr] + xf_store_toks[-1:] +
                                 kv_store_toks[-1:])
                    evs.append(tev)
                xf_ev_m2 = evs[-2]
                xa_readers.append(evs[-1])
                xf_store_toks.append(P.dma("g", 1, lambda gp, fin, hc=hc, t0=t0:
                                           fin(out=ds["xf_full"][
                                               hc * 128:(hc + 1) * 128,
                                               t0:t0 + 512],
                                               in_=evt[:, :]),
                                           deps=[evs[-1]]))

            for mat in range(3):
                for mj in range(4):
                    bank = psb[mj % 2]
                    tmm = P.step("t", lambda t, mat=mat, mj=mj, bank=bank: [
                        t.matmul(bank[:, :],
                                 wbuf[:, mat * 4096 + hc * 512 + mj * 128:
                                      mat * 4096 + hc * 512 + (mj + 1) * 128],
                                 lnf[:, hc * 512:(hc + 1) * 512],
                                 start=(hc == 0), stop=(hc == 7))
                        for hc in range(8)][-1],
                        deps=lnf_ready + [t_w] + prev_evac.get((mat, mj), []))
                    lnf_readers.append(tmm)
                    if mat == 0:
                        tev = P.step("v", lambda v, mj=mj, bank=bank, t0=t0:
                                     v.tensor_scalar(
                                         big1[:, mj * 4096 + t0:
                                              mj * 4096 + t0 + 512],
                                         bank[:, :], bia[:, mj:mj + 1], None,
                                         ALU.add), deps=[tmm])
                        q_evac_toks.append(tev)
                        tsq = P.step("a", lambda sc, mj=mj, t0=t0:
                                     sc.activation(ksq[:, :],
                                                   big1[:, mj * 4096 + t0:
                                                        mj * 4096 + t0 + 512],
                                                   ACT.Square),
                                     deps=[tev, t_qn2_last, t_kn2_last])
                        tnm = P.step("t", lambda t:
                                     t.matmul(psb[3][0:2, :], e2ind[:, :],
                                              ksq[:, :], start=True, stop=True),
                                     deps=[tsq, t_qn2_last])
                        t_qn2_last = P.step("v", lambda v, mj=mj: [
                            v.reduce_max(scr[0:2, 4:5], psb[3][0:2, :], AX),
                            v.tensor_scalar_add(scr[:, 7:8], scr[:, 7:8], 0.0),
                            v.tensor_tensor(qn2m[:, mj:mj + 1],
                                            qn2m[:, mj:mj + 1],
                                            scr[0:2, 4:5], ALU.max)][-1],
                            deps=[tnm])
                        prev_evac[(mat, mj)] = [tev, tsq]
                    elif mat == 1:
                        tev = P.step("v", lambda v, mj=mj, bank=bank:
                                     v.tensor_scalar(ktmp[:, :], bank[:, :],
                                                     bia[:, 4 + mj:5 + mj],
                                                     None, ALU.add),
                                     deps=[tmm] + prev_evac.get(("kst", 0), []))
                        tdm = P.dma("g", 1, lambda gp, fin, mj=mj, t0=t0:
                                    fin(out=ds["k_hbm"][mj * 128:(mj + 1) * 128,
                                                        t0:t0 + 512],
                                        in_=ktmp[:, :]), deps=[tev])
                        kv_store_toks.append(tdm)
                        tsq = P.step("a", lambda sc: sc.activation(
                            ksq[:, :], ktmp[:, :], ACT.Square),
                            deps=[tev, t_kn2_last, t_qn2_last])
                        tnm = P.step("t", lambda t:
                                     t.matmul(psb[3][0:2, :], e2ind[:, :],
                                              ksq[:, :], start=True, stop=True),
                                     deps=[tsq, t_kn2_last])
                        t_kn2_last = P.step("v", lambda v, mj=mj: [
                            v.reduce_max(scr[0:2, 4:5], psb[3][0:2, :], AX),
                            v.tensor_scalar_add(scr[:, 7:8], scr[:, 7:8], 0.0),
                            v.tensor_tensor(kn2m[:, mj:mj + 1],
                                            kn2m[:, mj:mj + 1],
                                            scr[0:2, 4:5], ALU.max)][-1],
                            deps=[tnm])
                        prev_evac[(mat, mj)] = [tdm, tsq]
                        prev_evac[("kst", 0)] = [tdm, tsq]
                    else:
                        tev = P.step("v", lambda v, mj=mj, bank=bank:
                                     v.tensor_scalar(vtmp[:, :], bank[:, :],
                                                     bia[:, 8 + mj:9 + mj],
                                                     None, ALU.add),
                                     deps=[tmm] + prev_evac.get(("vst", 0), []))
                        evs = []
                        for s4 in range(4):
                            ttr = P.step("t", lambda t, s4=s4:
                                         t.transpose(psb[2][:, :128],
                                                     vtmp[:, s4 * 128:
                                                          (s4 + 1) * 128],
                                                     ident[:, :]),
                                         deps=[tev] + (evs[-1:] if evs else
                                                       prev_evac.get(("vtr", 0),
                                                                     [])))
                            tev2 = P.step("v", lambda v, s4=s4:
                                          v.tensor_copy(
                                              evt[:, s4 * 128:(s4 + 1) * 128],
                                              psb[2][:, :128]),
                                          deps=[ttr] + xf_store_toks[-1:] +
                                          kv_store_toks[-1:])
                            evs.append(tev2)
                        tdm = P.dma("g", 1, lambda gp, fin, mj=mj, t0=t0: fin(
                            out=ds["v_hbm"][t0:t0 + 512,
                                            mj * 128:(mj + 1) * 128]
                            .rearrange("(a p) d -> p a d", p=128),
                            in_=evt[:, :].rearrange("p (a d) -> p a d", d=128)),
                            deps=[evs[-1]])
                        kv_store_toks.append(tdm)
                        prev_evac[(mat, mj)] = [tdm]
                        prev_evac[("vst", 0)] = [tdm]
                        prev_evac[("vtr", 0)] = [evs[-1]]

        t_msq = P.step("a", lambda sc: [
            sc.activation(qn2m[:, :], qn2m[:, :], ACT.Sqrt),
            sc.activation(kn2m[:, :], kn2m[:, :], ACT.Sqrt)][-1],
            deps=[t_qn2_last, t_kn2_last])
        t_mm2 = P.step("v", lambda v: [
            v.tensor_tensor(kn2m[:, :], kn2m[:, :], qn2m[:, :], ALU.mult),
            v.tensor_scalar_add(scr[:, 7:8], scr[:, 7:8], 0.0),
            v.tensor_scalar_mul(kn2m[:, :], kn2m[:, :], -0.125)][-1],
            deps=[t_msq])
        t_se0 = P.step("t", lambda t: t.matmul(
            psb[3][0:1, 0:4], sel01[0:2, 0:1], kn2m[:, :],
            start=True, stop=True), deps=[t_mm2, t_qn2_last, t_kn2_last])
        t_sv0 = P.step("v", lambda v: v.tensor_copy(
            srow[0:1, 0:4], psb[3][0:1, 0:4]), deps=[t_se0])
        t_se1 = P.step("t", lambda t: t.matmul(
            psb[3][0:1, 0:4], sel01[0:2, 1:2], kn2m[:, :],
            start=True, stop=True), deps=[t_sv0])
        t_mrow = P.step("v", lambda v: v.tensor_copy(
            srow[0:1, 4:8], psb[3][0:1, 0:4]), deps=[t_se1])
        if debug:
            P.dma("g", 8, lambda gp, fin: [
                fin(out=ds["q_dbg"][:, i * 2048:(i + 1) * 2048],
                    in_=big1[:, i * 2048:(i + 1) * 2048]) for i in range(8)]
                and None, deps=q_evac_toks)
            P.dma("g", 1, lambda gp, fin: fin(out=ds["m_dbg"][:, :],
                                              in_=srow[:, :].unsqueeze(0)
                                              if False else srow[:, :]),
                  deps=[t_mrow])

        kstage = reg(0, 4096)
        vhx = reg(4096, 6176).bitcast(BF16)
        vlx = reg(6176, 8256).bitcast(BF16)
        vstage = reg(8256, 10336)
        ptile = reg(10336, 11872)
        phx = reg(11872, 12640).bitcast(BF16)
        plx = reg(12640, 13408).bitcast(BF16)
        rrow = reg(13408, 13920)[0:1, :]
        ofh = reg(13920, 14432)[0:64, :]
        rbc = reg(14432, 14944)[0:64, :]
        khx = wbuf[:, 0:2048].bitcast(BF16)
        klx = wbuf[:, 2048:4096].bitcast(BF16)
        qhx = wbuf[:, 4096:8192].bitcast(BF16)
        qlx = wbuf[:, 8192:12288].bitcast(BF16)

        o_store_toks = []
        head_users = {}
        exp_toks = {}
        aexp_last = {}
        prev_n4 = None
        prev_bc = None
        t_ksp = {}
        t_vsp = {}
        t_qsp = {}
        t_vf = None
        for h in range(8):
            par = h % 2
            cb = par
            hf = h // 2
            qh2 = (hf % 2) * 4096
            if h % 2 == 0:
                war_q = [exp_toks[(h - 3, 7, 31)]] if h >= 4 else []
                tqh = P.step("a", lambda sc, hf=hf, qh2=qh2: sc.activation(
                    qhx[:, qh2:qh2 + 4096],
                    big1[:, hf * 4096:(hf + 1) * 4096], ACT.Copy),
                    deps=[q_evac_toks[-1], t_mrow] + war_q)
                t_qsp[hf] = P.step("v", lambda v, hf=hf, qh2=qh2:
                                   v.tensor_tensor(
                                       qlx[:, qh2:qh2 + 4096],
                                       big1[:, hf * 4096:(hf + 1) * 4096],
                                       qhx[:, qh2:qh2 + 4096], ALU.subtract),
                                   deps=[tqh])
            t_ldk = P.dma("g", 1, lambda gp, fin, h=h, par=par: fin(
                out=kstage[par * 64:par * 64 + 64, :],
                in_=ds["k_hbm"][64 * h:64 * h + 64, :]),
                deps=kv_store_toks + ([t_ksp[h - 2]] if h >= 2 else []))
            war_k = [exp_toks[(h - 2, 7, 31)]] if h >= 2 else []
            tkh = P.step("a", lambda sc, par=par: sc.activation(
                khx[par * 64:par * 64 + 64, :],
                kstage[par * 64:par * 64 + 64, :], ACT.Copy),
                deps=[t_ldk, t_mrow] + war_k)
            t_ksp[h] = P.step("v", lambda v, par=par: v.tensor_tensor(
                klx[par * 64:par * 64 + 64, :],
                kstage[par * 64:par * 64 + 64, :],
                khx[par * 64:par * 64 + 64, :], ALU.subtract),
                deps=[tkh])
            if h == 0:
                t_vf = P.step("v", lambda v: v.memset(vstage[:, :], 1.0),
                              deps=kv_store_toks + [t_mrow])
            t_ldv = P.dma("g", 1, lambda gp, fin, h=h: fin(
                out=vstage[:, :].rearrange("p (a w) -> p a w", w=65)[:, :, 0:64],
                in_=ds["v_hbm"][:, 64 * h:64 * h + 64]
                .rearrange("(a p) d -> p a d", p=128)),
                deps=[t_vf] if h == 0 else [t_vsp[h - 1]])
            war_v = head_users[h - 2][-1:] if h >= 2 else []
            tvh = P.step("a", lambda sc, cb=cb: sc.activation(
                vhx[:, cb * 2080:(cb + 1) * 2080], vstage[:, :], ACT.Copy),
                deps=[t_ldv] + war_v)
            t_vsp[h] = P.step("v", lambda v, cb=cb: v.tensor_tensor(
                vlx[:, cb * 2080:(cb + 1) * 2080], vstage[:, :],
                vhx[:, cb * 2080:(cb + 1) * 2080], ALU.subtract),
                deps=[tvh])
            users = []

            col = (h % 2) * 4 + h // 2
            t_n4 = P.step("t", lambda t, col=col: t.matmul(
                psb[7][0:128, 0:1], ones1[0:1, :], srow[0:1, col:col + 1],
                start=True, stop=True),
                deps=[t_mrow, t_c0] + ([prev_n4] if prev_n4 else []))
            prev_n4 = t_n4
            t_nm = P.step("v", lambda v: v.tensor_copy(
                nmh[:, 0:1], psb[7][0:128, 0:1]),
                deps=[t_n4] + ([aexp_last[h - 1]] if h >= 1 else []))

            for qt in range(8):
                nk = qt * 4 + 4
                obank = psb[2 + qt % 2]
                t_oev_prev = o_store_toks[-1] if o_store_toks else None
                av_toks = []
                pdeps = {}

                def emit_sc(kt, qt=qt):
                    sbank = psb[kt % 2]
                    psl = kt % 3
                    t_sc = P.step("t", lambda t, par=par, qh2=qh2, qt=qt,
                                  kt=kt, sbank=sbank: [
                                      t.matmul(
                                          sbank[:, :],
                                          khx[par * 64:par * 64 + 64,
                                              kt * 128:(kt + 1) * 128],
                                          qhx[par * 64:par * 64 + 64,
                                              qh2 + qt * 512:
                                              qh2 + (qt + 1) * 512],
                                          start=True, stop=False),
                                      t.matmul(
                                          sbank[:, :],
                                          klx[par * 64:par * 64 + 64,
                                              kt * 128:(kt + 1) * 128],
                                          qhx[par * 64:par * 64 + 64,
                                              qh2 + qt * 512:
                                              qh2 + (qt + 1) * 512],
                                          start=False, stop=False),
                                      t.matmul(
                                          sbank[:, :],
                                          khx[par * 64:par * 64 + 64,
                                              kt * 128:(kt + 1) * 128],
                                          qlx[par * 64:par * 64 + 64,
                                              qh2 + qt * 512:
                                              qh2 + (qt + 1) * 512],
                                          start=False, stop=True)][-1],
                                  fast=True,
                                  deps=[t_ksp[h], t_qsp[hf], t_nm] +
                                  ([exp_toks[(h, qt, kt - 2)]]
                                   if kt >= 2 else
                                   ([exp_toks[(h, qt - 1, qt * 4 - 1)]]
                                    if qt >= 1 else
                                    ([exp_toks[(h - 1, 7, 31)]]
                                     if h >= 1 else []))))
                    if kt >= qt * 4:
                        t_ex = P.step("a", lambda sc, psl=psl, sbank=sbank:
                                      sc.activation(
                                          ptile[:, psl * 512:(psl + 1) * 512],
                                          sbank[:, :], ACT.Exp,
                                          bias=nmh[:, 0:1], scale=nmh[:, 1:2]),
                                      deps=[t_sc] +
                                      ([av_toks[kt - 3]] if kt >= 3 else []))
                        i = kt - qt * 4
                        dep_p = P.step("v", lambda v, psl=psl, i=i: [
                            v.tensor_tensor(
                                ptile[:, psl * 512:(psl + 1) * 512],
                                ptile[:, psl * 512:(psl + 1) * 512],
                                masks[:, i * 512:(i + 1) * 512], ALU.mult),
                            v.tensor_copy(
                                phx[:, psl * 512:(psl + 1) * 512],
                                ptile[:, psl * 512:(psl + 1) * 512]),
                            v.tensor_tensor(
                                plx[:, psl * 512:(psl + 1) * 512],
                                ptile[:, psl * 512:(psl + 1) * 512],
                                phx[:, psl * 512:(psl + 1) * 512],
                                ALU.subtract)][-1],
                            deps=[t_ex] + mask_toks +
                            ([av_toks[kt - 3]] if kt >= 3 else []))
                    else:
                        t_ex = P.step("a", lambda sc, psl=psl, sbank=sbank: [
                            sc.activation(
                                ptile[:, psl * 512:(psl + 1) * 512],
                                sbank[:, :], ACT.Exp,
                                bias=nmh[:, 0:1], scale=nmh[:, 1:2]),
                            sc.activation(
                                phx[:, psl * 512:(psl + 1) * 512],
                                sbank[:, :], ACT.Exp,
                                bias=nmh[:, 0:1], scale=nmh[:, 1:2])][-1],
                            deps=[t_sc] +
                            ([av_toks[kt - 3]] if kt >= 3 else []))
                        dep_p = P.step("v", lambda v, psl=psl:
                                       v.tensor_tensor(
                                           plx[:, psl * 512:(psl + 1) * 512],
                                           ptile[:, psl * 512:(psl + 1) * 512],
                                           phx[:, psl * 512:(psl + 1) * 512],
                                           ALU.subtract),
                                       deps=[t_ex])
                    exp_toks[(h, qt, kt)] = dep_p
                    pdeps[kt] = dep_p
                    aexp_last[h] = t_ex

                def emit_av(kt, qt=qt, obank=obank, nk=nk,
                            t_oev_prev=t_oev_prev):
                    psl = kt % 3
                    dep_p = pdeps[kt]
                    t_av = P.step("t", lambda t, cb=cb, kt=kt, psl=psl,
                                  obank=obank, nk=nk: [
                                      t.matmul(
                                          obank[0:65, :],
                                          vhx[:, cb * 2080 + kt * 65:
                                              cb * 2080 + (kt + 1) * 65],
                                          phx[:, psl * 512:(psl + 1) * 512],
                                          start=(kt == 0), stop=False),
                                      t.matmul(
                                          obank[0:65, :],
                                          vlx[:, cb * 2080 + kt * 65:
                                              cb * 2080 + (kt + 1) * 65],
                                          phx[:, psl * 512:(psl + 1) * 512],
                                          start=False, stop=False),
                                      t.matmul(
                                          obank[0:65, :],
                                          vhx[:, cb * 2080 + kt * 65:
                                              cb * 2080 + (kt + 1) * 65],
                                          plx[:, psl * 512:(psl + 1) * 512],
                                          start=False,
                                          stop=(kt == nk - 1))][-1],
                                  fast=True,
                                  deps=[dep_p, t_vsp[h]] +
                                  ([t_oev_prev] if kt == 0 and t_oev_prev
                                   else []))
                    av_toks.append(t_av)

                for kt in range(nk):
                    emit_sc(kt)
                    if kt >= 1:
                        emit_av(kt - 1)
                emit_av(nk - 1)
                users += [av_toks[-1]]
                t_rc = P.step("v", lambda v, obank=obank: v.reciprocal(
                    rrow[:, :], obank[64:65, :]),
                    deps=[av_toks[-1]] + ([prev_bc] if prev_bc else []))
                t_bc = P.step("t", lambda t: t.matmul(
                    psb[6][0:64, :], ones1[0:1, 0:64], rrow[:, :],
                    start=True, stop=True),
                    deps=[t_rc] + (o_store_toks[-1:] if o_store_toks else []))
                prev_bc = t_bc
                t_bce = P.step("v", lambda v: v.tensor_copy(
                    rbc[:, :], psb[6][0:64, :]), deps=[t_bc])
                t_oe = P.step("v", lambda v, obank=obank: v.tensor_tensor(
                    ofh[:, :], obank[0:64, :], rbc[:, :], ALU.mult),
                    deps=[t_bce] + (o_store_toks[-1:] if o_store_toks else []))
                o_store_toks.append(P.dma("g", 1, lambda gp, fin, h=h, qt=qt:
                                          fin(out=ds["o_hbm"][
                                              64 * h:64 * h + 64,
                                              qt * 512:(qt + 1) * 512],
                                              in_=ofh[:, :]),
                                          deps=[t_oe]))
            head_users[h] = users

        ob = reg(0, 4096)
        rstc = reg(4096, 5120)
        xftl = reg(5120, 6144)
        rs_dma_toks = []
        mm_toks_c = {}
        for tau in range(8):
            pp = tau % 2
            hf, tc = tau // 4, tau % 4
            t_ldo = P.dma("sy", 4, lambda sy, fin, tau=tau, pp=pp: [
                fin(out=ob[:, pp * 2048 + dc * 512:pp * 2048 + (dc + 1) * 512],
                    in_=ds["o_hbm"][dc * 128:(dc + 1) * 128,
                                    tau * 512:(tau + 1) * 512])
                for dc in range(4)][-1],
                deps=o_store_toks + [mm_toks_c.get(tau - 2)])
            last_mm = None
            for jm in range(8):
                bank = psb[jm % 2]
                xsl = jm % 2
                t_ldxf = P.dma("g", 1, lambda sy, fin, jm=jm, tau=tau,
                               xsl=xsl: fin(
                                   out=xftl[:, xsl * 512:(xsl + 1) * 512],
                                   in_=ds["xf_full"][jm * 128:(jm + 1) * 128,
                                                     tau * 512:(tau + 1) * 512]),
                               deps=xf_store_toks +
                               (rs_dma_toks[-2:-1] if len(rs_dma_toks) >= 2
                                else []))
                def c_mm(t, jm=jm, pp=pp, bank=bank):
                    for dc in range(4):
                        t.matmul(bank[:, :],
                                 wbuf[:, 12288 + dc * 1024 + jm * 128:
                                      12288 + dc * 1024 + (jm + 1) * 128],
                                 ob[:, pp * 2048 + dc * 512:
                                    pp * 2048 + (dc + 1) * 512],
                                 start=(dc == 0), stop=(dc == 3))
                t_mm = P.step("t", c_mm,
                              deps=[t_ldo, t_w, aexp_last[7]] +
                              (rs_dma_toks[-2:-1] if len(rs_dma_toks) >= 2
                               else []))
                last_mm = t_mm
                rsl = jm % 2
                t_ev = P.step("v", lambda v, bank=bank, xsl=xsl, rsl=rsl:
                              v.scalar_tensor_tensor(
                                  rstc[:, rsl * 512:(rsl + 1) * 512],
                                  xftl[:, xsl * 512:(xsl + 1) * 512], 0.5,
                                  bank[:, :], ALU.mult, ALU.add),
                              deps=[t_mm, t_ldxf] +
                              (rs_dma_toks[-2:-1] if len(rs_dma_toks) >= 2
                               else []))
                rs_dma_toks.append(P.dma("g", 1,
                                         lambda gp, fin, hf=hf, jm=jm, tc=tc,
                                         rsl=rsl: fin(
                                             out=ds["rs_in"][
                                                 hf, jm * 128:(jm + 1) * 128,
                                                 tc * 512:(tc + 1) * 512],
                                             in_=rstc[:, rsl * 512:
                                                      (rsl + 1) * 512]),
                                         deps=[t_ev]))
            mm_toks_c[tau] = last_mm

        if debug:
            P.dma("g", 1, lambda gp, fin: fin(out=ds["rsin_d"][:, :, :],
                                              in_=ds["rs_in"][:, :, :]),
                  deps=rs_dma_toks)
        t_rs_cc = P.cc("ReduceScatter", ALU.add, CC_PAIRS,
                       [ds["rs_in"].flatten()], [ds["rs_out"].flatten()],
                       deps=rs_dma_toks)
        if debug:
            P.dma("g", 1, lambda gp, fin: fin(out=ds["rs_out_d"][:, :],
                                              in_=ds["rs_out"][:, :]),
                  deps=[t_rs_cc])

        x2t = reg(0, 1024)
        xtm4 = reg(1024, 5120)
        btm4 = reg(5120, 9216)
        lrz = reg(9216, 11264)[0:1, :]
        lgt = reg(11264, 11776)[0:32, :]
        lgtm = reg(11776, 12288)
        ohm = reg(12288, 14336)[0:32, :]
        ccm = reg(14336, 16384)[0:32, :]
        slotf_r = reg(1024, 3072)[0:1, :]
        accf_r = reg(3072, 5120)[0:1, :]
        wrow = reg(5120, 7680)[0:1, :]
        ohs = reg(7680, 8192)[0:32, :]
        sct = reg(1024, 3072)
        slot_sb = reg(3072, 3200)
        zb = reg(3200, 4224)

        def lr(i):
            return lrz[0:1, i * 512:(i + 1) * 512]

        def wr_(i):
            return wrow[0:1, i * 512:(i + 1) * 512]

        agx_toks = []
        base_toks = []
        x2_done = {}
        prev_x2 = []
        lg_evs = []
        xt_toks = {}
        for to in range(4):
            mm_m = []
            mm_s = []
            for hc in range(8):
                sl = hc % 2
                t_ld = P.dma("g", 1, lambda sy, fin, hc=hc, to=to, sl=sl: fin(
                    out=x2t[:, sl * 512:(sl + 1) * 512],
                    in_=ds["rs_out"][hc * 128:(hc + 1) * 128,
                                     to * 512:(to + 1) * 512]),
                    deps=[t_rs_cc] + (mm_m[hc - 2:hc - 1] if hc >= 2
                                      else prev_x2))
                t_x2 = P.step("v", lambda v, hc=hc, sl=sl: v.tensor_scalar(
                    x2t[:, sl * 512:(sl + 1) * 512],
                    x2t[:, sl * 512:(sl + 1) * 512],
                    bo_sb[:, hc:hc + 1], None, ALU.add), deps=[t_ld])
                if debug:
                    t_x2 = P.dma("g", 1, lambda gp, fin, hc=hc, to=to, sl=sl:
                                 fin(out=ds["x2_d"][hc * 128:(hc + 1) * 128,
                                                    to * 512:(to + 1) * 512],
                                     in_=x2t[:, sl * 512:(sl + 1) * 512]),
                                 deps=[t_x2])
                tq = P.step("a", lambda sc, sl=sl: sc.activation(
                    sqt[:, 0:512], x2t[:, sl * 512:(sl + 1) * 512],
                    ACT.Square), deps=[t_x2] + mm_s[-1:])
                mm_m.append(P.step("t", lambda t, hc=hc, sl=sl: t.matmul(
                    psb[0][0:1, :], onesc[:, :],
                    x2t[:, sl * 512:(sl + 1) * 512],
                    start=(hc == 0), stop=(hc == 7)),
                    deps=[t_x2] + (lg_evs[-1:] if hc == 0 else [])))
                mm_s.append(P.step("t", lambda t, hc=hc: t.matmul(
                    psb[1][0:1, :], onesc[:, :], sqt[:, 0:512],
                    start=(hc == 0), stop=(hc == 7)), deps=[tq]))

            def d_stats(v):
                v.tensor_scalar_mul(lr(0), psb[0][0:1, :], 1.0 / 1024)
                v.tensor_scalar_mul(lr(1), psb[1][0:1, :], 1.0 / 1024)
                v.tensor_tensor(lr(2), lr(0), lr(0), ALU.mult)
                v.tensor_tensor(lr(2), lr(1), lr(2), ALU.subtract)
                return v.tensor_scalar_add(lr(2), lr(2), EPS)
            t_st = P.step("v", d_stats, deps=[mm_m[-1], mm_s[-1]])
            t_sq2 = P.step("a", lambda sc: sc.activation(
                lr(2), lr(2), ACT.Sqrt), deps=[t_st])

            def d_rstd(v):
                v.reciprocal(lr(2), lr(2))
                return v.scalar_tensor_tensor(lr(3), lr(0), -1.0,
                                              lr(2), ALU.mult, ALU.mult)
            t_rst = P.step("v", d_rstd, deps=[t_sq2])
            if debug:
                t_rst = P.dma("g", 1, lambda gp, fin, to=to: fin(
                    out=ds["lr_d"][0:1, to * 2048:(to + 1) * 2048] if True else None,
                    in_=lrz[0:1, :]), deps=[t_rst])
            t_b1 = P.step("t", lambda t: t.matmul(
                psb[2][:, :], ones1[0:1, :], lr(2),
                start=True, stop=True), deps=[t_rst])
            t_b2 = P.step("t", lambda t: t.matmul(
                psb[3][:, :], ones1[0:1, :], lr(3),
                start=True, stop=True), deps=[t_rst])

            ev_xb = []
            for hc in range(8):
                sl = hc % 2
                t_ld2 = P.dma("g", 1, lambda sy, fin, hc=hc, to=to, sl=sl:
                              fin(out=x2t[:, sl * 512:(sl + 1) * 512],
                                  in_=ds["rs_out"][hc * 128:(hc + 1) * 128,
                                                   to * 512:(to + 1) * 512]),
                              deps=[mm_m[-1], mm_s[-1]] +
                              ([x2_done[(hc - 2, to)]] if hc >= 2 else []))
                t_x2 = P.step("v", lambda v, hc=hc, sl=sl: v.tensor_scalar(
                    x2t[:, sl * 512:(sl + 1) * 512],
                    x2t[:, sl * 512:(sl + 1) * 512],
                    bo_sb[:, hc:hc + 1], None, ALU.add), deps=[t_ld2])
                if debug:
                    t_x2 = P.dma("g", 1, lambda gp, fin, hc=hc, to=to, sl=sl:
                                 fin(out=ds["x2b_d"][hc * 128:(hc + 1) * 128,
                                                     to * 512:(to + 1) * 512],
                                     in_=x2t[:, sl * 512:(sl + 1) * 512]),
                                 deps=[t_x2])
                t_xt = P.step("v", lambda v, hc=hc, to=to, sl=sl: [
                    v.tensor_tensor(big1[:, hc * 2048 + to * 512:
                                         hc * 2048 + (to + 1) * 512],
                                    x2t[:, sl * 512:(sl + 1) * 512],
                                    psb[2][:, :], ALU.mult),
                    v.tensor_tensor(big1[:, hc * 2048 + to * 512:
                                         hc * 2048 + (to + 1) * 512],
                                    big1[:, hc * 2048 + to * 512:
                                         hc * 2048 + (to + 1) * 512],
                                    psb[3][:, :], ALU.add)][-1],
                    deps=[t_b1, t_b2, t_x2])
                xt_toks[(hc, to)] = t_xt
                for sub in range(4):
                    bank = psb[4]
                    bank2 = psb[5]
                    ttr = P.step("t", lambda t, hc=hc, to=to, sub=sub,
                                 bank=bank: t.transpose(
                                     bank[:, :128],
                                     big1[:, hc * 2048 + to * 512 + sub * 128:
                                          hc * 2048 + to * 512 +
                                          (sub + 1) * 128],
                                     ident[:, :]),
                                 deps=[t_xt] +
                                 ([ev_xb[-2]] if len(ev_xb) >= 2 else prev_x2))
                    tev = P.step("v", lambda v, hc=hc, sub=sub, bank=bank:
                                 v.tensor_copy(
                                     xtm4[:, sub * 1024 + hc * 128:
                                          sub * 1024 + (hc + 1) * 128],
                                     bank[:, :128]),
                                 deps=[ttr] + agx_toks[-4:-3])
                    ev_xb.append(tev)
                    t_ba = P.step("v", lambda v, hc=hc, to=to, sub=sub, sl=sl:
                                  v.tensor_tensor(
                                      bfa[:, :],
                                      x2t[:, sl * 512 + sub * 128:
                                          sl * 512 + (sub + 1) * 128],
                                      big1[:, hc * 2048 + to * 512 + sub * 128:
                                           hc * 2048 + to * 512 +
                                           (sub + 1) * 128],
                                      ALU.add), deps=[t_xt] + ev_xb[-1:])
                    ttr2 = P.step("t", lambda t, bank2=bank2: t.transpose(
                        bank2[:, :128], bfa[:, :], ident[:, :]),
                        deps=[t_ba] + ([ev_xb[-2]] if len(ev_xb) >= 2 else []))
                    tev2 = P.step("v", lambda v, hc=hc, sub=sub, bank2=bank2:
                                  v.tensor_copy(
                                      btm4[:, sub * 1024 + hc * 128:
                                           sub * 1024 + (hc + 1) * 128],
                                      bank2[:, :128]),
                                  deps=[ttr2] + base_toks[-4:-3])
                    ev_xb.append(tev2)
                x2_done[(hc, to)] = ev_xb[-1]
            for sub in range(4):
                agx_toks.append(P.dma("g", 1, lambda gp, fin, to=to, sub=sub:
                                      fin(out=ds["agx_in"][
                                          to * 512 + sub * 128:
                                          to * 512 + (sub + 1) * 128, :],
                                          in_=xtm4[:, sub * 1024:
                                                   (sub + 1) * 1024]),
                                      deps=ev_xb))
                base_toks.append(P.dma("g", 1, lambda gp, fin, to=to, sub=sub:
                                       fin(out=ds["base_hbm"][
                                           to * 512 + sub * 128:
                                           to * 512 + (sub + 1) * 128, :],
                                           in_=btm4[:, sub * 1024:
                                                    (sub + 1) * 1024]),
                                       deps=ev_xb))
            prev_x2 = [x2_done[(7, to)]]

            t_lg = P.step("t", lambda t, to=to: [
                t.matmul(psb[6][0:32, :], wr_sb[:, hc * 32:(hc + 1) * 32],
                         big1[:, hc * 2048 + to * 512:
                              hc * 2048 + (to + 1) * 512],
                         start=(hc == 0), stop=(hc == 7))
                for hc in range(8)][-1],
                deps=[xt_toks[(hc, to)] for hc in range(8)] + [t_w] +
                lg_evs[-1:])
            t_le = P.step("v", lambda v: v.tensor_copy(
                lgt[:, :], psb[6][0:32, :]), deps=[t_lg])
            for s_ in range(4):
                ttr = P.step("t", lambda t, s_=s_: t.transpose(
                    psb[6][0:128, 0:32], lgt[:, s_ * 128:(s_ + 1) * 128],
                    ident[0:32, 0:32]),
                    deps=[t_le] + lg_evs[-1:])
                tev = P.step("v", lambda v, to=to, s_=s_: v.tensor_copy(
                    lgtm[:, (to * 4 + s_) * 32:(to * 4 + s_ + 1) * 32],
                    psb[6][0:128, 0:32]), deps=[ttr])
                lg_evs.append(tev)

        t_cc_xt = P.cc("AllGather", ALU.bypass, CC_ALL,
                       [ds["agx_in"].flatten()], [ds["xt_full"].flatten()],
                       deps=agx_toks)
        if debug:
            P.dma("g", 2, lambda gp, fin: [
                fin(out=ds["agx_in_d"][:, :], in_=ds["agx_in"][:, :]),
                fin(out=ds["xt_full_d"][:, :], in_=ds["xt_full"][:, :])]
                and None, deps=[t_cc_xt])
        if debug:
            P.dma("g", 1, lambda gp, fin: fin(out=ds["lg_dbg"][:, :],
                                              in_=lgtm[:, :]), deps=lg_evs)

        oh_toks = []
        rt_toks = []
        prev_tr = None
        prev_r2 = None
        for jp in range(8):
            js = (2 * jp, 2 * jp + 1)

            def d_rt(v, js=js):
                for i, j in enumerate(js):
                    v.reduce_max(scr[:, i * 8:i * 8 + 1],
                                 lgtm[:, j * 32:(j + 1) * 32], AX)
                for i in range(2):
                    v.tensor_scalar_mul(scr[:, i * 8 + 1:i * 8 + 2],
                                        scr[:, i * 8:i * 8 + 1], -1.0)
            t_r1 = P.step("v", d_rt,
                          deps=lg_evs + oh_toks[-1:] +
                          ([prev_r2] if prev_r2 else []))
            t_r2 = P.step("a", lambda sc, js=js: [
                sc.activation(probs[:, i * 32:(i + 1) * 32],
                              lgtm[:, j * 32:(j + 1) * 32], ACT.Exp,
                              bias=scr[:, i * 8 + 1:i * 8 + 2],
                              accum_out=scr[:, i * 8 + 2:i * 8 + 3])
                for i, j in enumerate(js)][-1], deps=[t_r1])
            prev_r2 = t_r2

            def d_rt2(v, js=js):
                for i in range(2):
                    v.reciprocal(scr[:, i * 8 + 3:i * 8 + 4],
                                 scr[:, i * 8 + 2:i * 8 + 3])
                for i in range(2):
                    p_ = probs[:, i * 32:(i + 1) * 32]
                    v.tensor_scalar(p_, p_, scr[:, i * 8 + 3:i * 8 + 4],
                                    None, ALU.mult)
                for i, j in enumerate(js):
                    v.reduce_max(rt[:, j * 8 + 1:j * 8 + 2],
                                 probs[:, i * 32:(i + 1) * 32], AX)
                for i, j in enumerate(js):
                    p_ = probs[:, i * 32:(i + 1) * 32]
                    v.tensor_scalar(p_, p_, rt[:, j * 8 + 1:j * 8 + 2],
                                    None, ALU.is_equal)
                for i in range(2):
                    p_ = probs[:, i * 32:(i + 1) * 32]
                    v.tensor_tensor(p_, p_, w32[:, :], ALU.mult)
                for i in range(2):
                    v.reduce_max(scr[:, i * 8 + 4:i * 8 + 5],
                                 probs[:, i * 32:(i + 1) * 32], AX)
                for i, j in enumerate(js):
                    v.scalar_tensor_tensor(rt[:, j * 8:j * 8 + 1],
                                           scr[:, i * 8 + 4:i * 8 + 5],
                                           -1.0, onesc[:, :],
                                           ALU.mult, ALU.bypass)
                for j in js:
                    v.tensor_scalar_add(rt[:, j * 8:j * 8 + 1],
                                        rt[:, j * 8:j * 8 + 1], 32.0)
                last = None
                for i, j in enumerate(js):
                    last = v.tensor_scalar(probs[:, i * 32:(i + 1) * 32],
                                           iotaE[:, :],
                                           rt[:, j * 8:j * 8 + 1],
                                           None, ALU.is_equal)
                return last
            t_r3 = P.step("v", d_rt2, deps=[t_r2])
            rt_toks.append(t_r3)
            for i, j in enumerate(js):
                ttr = P.step("t", lambda t, i=i: t.transpose(
                    psb[5][0:32, :128], probs[:, i * 32:(i + 1) * 32],
                    ident[:, :]),
                    deps=[t_r3] + ([prev_tr] if prev_tr else []))
                tev = P.step("v", lambda v, j=j: v.tensor_copy(
                    ohm[:, j * 128:(j + 1) * 128], psb[5][0:32, :128]),
                    deps=[ttr])
                prev_tr = tev
                oh_toks.append(tev)

        t_scan = P.step("v", lambda v: v.tensor_tensor_scan(
            ccm[:, :], ohm[:, :], ohm[:, :], 0.0, ALU.add, ALU.bypass),
            deps=oh_toks)
        t_cntd = P.dma("g", 1, lambda gp, fin: fin(
            out=ds["cnt_in"][:].unsqueeze(1), in_=ccm[:, 2047:2048]),
            deps=[t_scan])
        t_cc_cnt = P.cc("AllGather", ALU.bypass, CC_ALL,
                        [ds["cnt_in"][:]], [ds["cnt_out"][:]], deps=[t_cntd])
        t_ld8 = P.dma("g", 1, lambda sy, fin: fin(
            out=cnt8[:, :], in_=ds["cnt_out"].rearrange("(a b) -> a b", b=32)),
            deps=[t_cc_cnt])
        t_off = P.step("t", lambda t: t.matmul(
            psb[7][0:32, 0:1], cnt8[:, :], mrow[:, :], start=True, stop=True),
            deps=[t_ld8, t_w, prev_n4])
        t_ofe = P.step("v", lambda v: v.tensor_copy(offc[:, :],
                                                    psb[7][0:32, 0:1]),
                       deps=[t_off])

        row_toks = []
        rows_guard = agx_toks + base_toks
        for to in range(4):
            sl_ = slice(to * 512, (to + 1) * 512)
            t_o1 = P.step("v", lambda v, sl_=sl_: v.tensor_tensor(
                ohs[:, :], ohm[:, sl_], ccm[:, sl_], ALU.mult),
                deps=[t_scan] + row_toks[-1:] + rows_guard)
            t_dbg_o = None
            if debug and to == 0:
                t_dbg_o = P.dma("g", 1, lambda gp, fin: fin(
                    out=ds["rows_dbg"][5:6, 0:512],
                    in_=ohs[7:8, :]), deps=[t_o1])
            t_m1 = P.step("t", lambda t: t.matmul(
                psb[6][0:1, :], onesc[0:32, :], ohs[:, :],
                start=True, stop=True), deps=[t_o1])
            t_e1 = P.step("v", lambda v: v.tensor_copy(
                wr_(0), psb[6][0:1, :]), deps=[t_m1] + rows_guard)
            if debug and to == 0:
                P.dma("g", 1, lambda gp, fin: fin(
                    out=ds["rows_dbg"][6:7, 0:512],
                    in_=wr_(0)), deps=[t_e1])
                t_e1 = (t_e1, t_dbg_o)[0]
            t_o2 = P.step("v", lambda v, sl_=sl_: v.tensor_scalar(
                ohs[:, :], ohm[:, sl_], offc[:, 0:1], None, ALU.mult),
                deps=[t_e1, t_ofe] + ([t_dbg_o] if t_dbg_o else []))
            t_m2 = P.step("t", lambda t: t.matmul(
                psb[6][0:1, :], onesc[0:32, :], ohs[:, :],
                start=True, stop=True), deps=[t_o2])
            t_e2 = P.step("v", lambda v: v.tensor_copy(
                wr_(1), psb[6][0:1, :]), deps=[t_m2])
            t_o3 = P.step("v", lambda v, sl_=sl_: v.tensor_scalar(
                ohs[:, :], ohm[:, sl_], pcol[0:32, 0:1], None, ALU.mult),
                deps=[t_e2])
            t_m3 = P.step("t", lambda t: t.matmul(
                psb[6][0:1, :], onesc[0:32, :], ohs[:, :],
                start=True, stop=True), deps=[t_o3])
            t_e3 = P.step("v", lambda v: v.tensor_copy(
                wr_(2), psb[6][0:1, :]), deps=[t_m3])

            def d_rows(v, sl_=sl_):
                acc_b = accf_r[0:1, sl_]
                slf_b = slotf_r[0:1, sl_]
                v.tensor_tensor(wr_(3), wr_(0), wr_(1), ALU.add)
                v.tensor_scalar_add(wr_(3), wr_(3), -1.0)
                v.tensor_single_scalar(acc_b, wr_(3), float(C), ALU.is_lt)
                v.scalar_tensor_tensor(wr_(4), wr_(2), float(C), wr_(3),
                                       ALU.mult, ALU.add)
                v.tensor_single_scalar(wr_(3), wr_(3), float(C), ALU.is_ge)
                return v.scalar_tensor_tensor(slf_b, wr_(3), BIG, wr_(4),
                                              ALU.mult, ALU.add)
            row_toks.append(P.step("v", d_rows, deps=[t_e3]))
            if debug and to == 0:
                t_dbg_s = P.dma("g", 1, lambda gp, fin: fin(
                    out=ds["rows_dbg"][7:8, 0:512],
                    in_=slotf_r[0:1, 0:512]), deps=row_toks[-1:])
                row_toks[-1] = t_dbg_s
            if debug:
                P.dma("g", 3, lambda gp, fin, to=to: [
                    fin(out=ds["rows_dbg"][2:3, to * 512:(to + 1) * 512],
                        in_=wr_(0)),
                    fin(out=ds["rows_dbg"][3:4, to * 512:(to + 1) * 512],
                        in_=wr_(1)),
                    fin(out=ds["rows_dbg"][4:5, to * 512:(to + 1) * 512],
                        in_=wr_(2))][-1], deps=row_toks[-1:])
        if debug:
            P.dma("g", 4, lambda gp, fin: [
                fin(out=ds["ohm_dbg"][:, :], in_=ohm[:, :]),
                fin(out=ds["ccm_dbg"][:, :], in_=ccm[:, :]),
                fin(out=ds["rows_dbg"][0:1, :], in_=slotf_r),
                fin(out=ds["rows_dbg"][1:2, :], in_=accf_r)][-1],
                deps=row_toks + [t_scan])

        prev_tr2 = None
        for j in range(16):
            ttr = P.step("t", lambda t, j=j: [
                t.matmul(psb[5][0:128, 0:1],
                         slotf_r[0:1, j * 128:(j + 1) * 128],
                         onesc[0:1, 0:1], start=True, stop=True),
                t.matmul(psb[4][0:128, 0:1],
                         accf_r[0:1, j * 128:(j + 1) * 128],
                         onesc[0:1, 0:1], start=True, stop=True)][-1],
                deps=row_toks + ([prev_tr2] if prev_tr2 else []))
            tev = P.step("v", lambda v: [
                v.tensor_copy(r8t[:, 0:1], psb[5][0:128, 0:1]),
                v.tensor_copy(r8t[:, 5:6], psb[4][0:128, 0:1])][-1],
                deps=[ttr])

            def d_rt3(v, j=j):
                v.tensor_copy(rt[:, j * 8 + 2:j * 8 + 3], r8t[:, 0:1])
                v.tensor_tensor(rt[:, j * 8 + 3:j * 8 + 4],
                                rt[:, j * 8 + 1:j * 8 + 2], r8t[:, 5:6],
                                ALU.mult)
                return v.tensor_scalar_min(rt[:, j * 8 + 4:j * 8 + 5],
                                           r8t[:, 0:1], float(EC - 1))
            prev_tr2 = P.step("v", d_rt3, deps=[tev])
            rt_toks.append(prev_tr2)
        if debug:
            P.dma("g", 1, lambda gp, fin: fin(out=ds["rt_dbg"][:, :],
                                              in_=rt[:, :]), deps=rt_toks)

        t_slotd = P.dma("g", 1, lambda gp, fin: fin(
            out=ds["slot_in"][:].unsqueeze(0), in_=slotf_r),
            deps=row_toks)
        t_cc_slot = P.cc("AllGather", ALU.bypass, CC_ALL,
                         [ds["slot_in"][:]], [ds["slot_out"][:]],
                         deps=[t_slotd])
        if debug:
            P.dma("g", 1, lambda gp, fin: fin(out=ds["slot_out_d"][:],
                                              in_=ds["slot_out"][:]),
                  deps=[t_cc_slot])

        t_zb = P.step("v", lambda v: v.memset(zb[:, :], 0.0),
                      deps=base_toks + rt_toks)
        t_zero = P.dma("g", 20, lambda gp, fin: [
            fin(out=ds["buf"][i * 128:(i + 1) * 128, :], in_=zb[:, :])
            for i in range(20)][-1], deps=[t_zb])
        t_lds0 = P.dma("g", 1, lambda sy, fin: fin(
            out=slot_sb[0:128, :],
            in_=ds["slot_out"].rearrange("(p a) -> p a", a=128)),
            deps=[t_cc_slot] + base_toks + rt_toks)
        t_lds1 = P.step("t", lambda t: t.transpose(
            psb[5][:, :128], slot_sb[0:128, :], ident[:, :]),
            deps=[t_lds0, prev_tr2])
        t_lds = P.step("v", lambda v: v.tensor_copy(
            slot_sb[0:128, :], psb[5][:, :128]), deps=[t_lds1])
        scat_toks = []
        for p2 in range(64):
            ja, jb = 2 * p2, 2 * p2 + 1
            t_la = P.dma("g", 1, lambda sy, fin, ja=ja: fin(
                out=sct[:, 0:1024],
                in_=ds["xt_full"][ja * 128:(ja + 1) * 128, :]),
                deps=[t_cc_xt, t_cc_slot] + rows_guard + rt_toks +
                scat_toks[-2:])
            t_lbx = P.dma("g", 1, lambda sy, fin, jb=jb: fin(
                out=sct[:, 1024:2048],
                in_=ds["xt_full"][jb * 128:(jb + 1) * 128, :]),
                deps=scat_toks[-2:])

            def d_idx(v, ja=ja, jb=jb):
                for i, j2 in enumerate((ja, jb)):
                    v.tensor_tensor(idxt[:, i * 4:i * 4 + 1],
                                    slot_sb[0:128, j2:j2 + 1],
                                    meta_sb[:, 0:1], ALU.subtract)
                for i in range(2):
                    v.tensor_single_scalar(idxt[:, i * 4 + 1:i * 4 + 2],
                                           idxt[:, i * 4:i * 4 + 1], 0.0,
                                           ALU.is_lt)
                for i in range(2):
                    v.scalar_tensor_tensor(idxt[:, i * 4 + 2:i * 4 + 3],
                                           idxt[:, i * 4 + 1:i * 4 + 2], BIG,
                                           idxt[:, i * 4:i * 4 + 1],
                                           ALU.mult, ALU.add)
                for i in range(2):
                    v.tensor_scalar_min(idxt[:, i * 4 + 2:i * 4 + 3],
                                        idxt[:, i * 4 + 2:i * 4 + 3], 4096.0)
                last = None
                for i in range(2):
                    last = v.tensor_copy(idxi[:, i:i + 1],
                                         idxt[:, i * 4 + 2:i * 4 + 3])
                return last
            t_ix = P.step("v", d_idx,
                          deps=[t_lds, t_w] + scat_toks[-2:])
            for i in range(2):
                scat_toks.append(P.dma("g", 1, lambda gp, fin, i=i: fin(
                    nc.gpsimd.indirect_dma_start(
                        out=ds["buf"][:, :],
                        out_offset=IndirectOffsetOnAxis(idxi[:, i:i + 1], 0),
                        in_=sct[:, i * 1024:(i + 1) * 1024],
                        in_offset=None,
                        bounds_check=bc_scat,
                        oob_is_err=False)),
                    deps=[t_la, t_lbx, t_ix, t_zero]))

        bfm = reg(0, 5120)
        hgm = reg(5120, 10240)
        yfm = reg(10240, 15360)
        ytm = reg(15360, 16384)

        agy_toks = []
        prev_gu = []
        prev_dn = []
        for le in range(ELOC):
            t_wg = P.dma("sy", 8, lambda sy, fin, le=le: [
                fin(out=wbuf[:, hc * 1024:(hc + 1) * 1024],
                    in_=din["wg"][le, hc * 128:(hc + 1) * 128, :])
                for hc in range(8)][-1],
                deps=[mm_toks_c[7]] + prev_gu + prev_dn)
            t_wu = P.dma("sy", 8, lambda sy, fin, le=le: [
                fin(out=wbuf[:, 8192 + hc * 1024:8192 + (hc + 1) * 1024],
                    in_=din["wu"][le, hc * 128:(hc + 1) * 128, :])
                for hc in range(8)][-1],
                deps=[mm_toks_c[7]] + prev_gu + prev_dn)
            prev_gu, prev_dn = [], []
            bf_toks = []
            bt_prev = []
            for bt in range(5):
                t_lb = P.dma("sy", 1, lambda sy, fin, le=le, bt=bt:
                             fin(out=ytm[:, :],
                                 in_=ds["buf"][le * C + bt * 128:
                                               le * C + (bt + 1) * 128, :]),
                             deps=scat_toks + bt_prev + agy_toks[-1:])
                evs = []
                for hc in range(8):
                    bank = psb[6 + hc % 2]
                    ttr = P.step("t", lambda t, hc=hc, bank=bank:
                                 t.transpose(bank[:, :128],
                                             ytm[:, hc * 128:(hc + 1) * 128],
                                             ident[:, :]),
                                 deps=[t_lb] +
                                 ([evs[-2]] if len(evs) >= 2 else
                                  (bf_toks[-1:] if bf_toks
                                   else agy_toks[-1:])))
                    tev = P.step("v", lambda v, hc=hc, bt=bt, bank=bank:
                                 v.tensor_copy(bfm[:, hc * 640 + bt * 128:
                                                   hc * 640 + (bt + 1) * 128],
                                               bank[:, :128]), deps=[ttr])
                    evs.append(tev)
                bf_toks += evs
                bt_prev = [evs[-1]]
            hg_toks = []
            for half in range(2):
                for fj in range(8):
                    gb = psb[(fj % 2) * 2]
                    ub = psb[(fj % 2) * 2 + 1]
                    t_g = P.step("t", lambda t, half=half, fj=fj, gb=gb: [
                        t.matmul(gb[:, 0:320],
                                 wbuf[:, hc * 1024 + fj * 128:
                                      hc * 1024 + (fj + 1) * 128],
                                 bfm[:, hc * 640 + half * 320:
                                     hc * 640 + half * 320 + 320],
                                 start=(hc == 0), stop=(hc == 7))
                        for hc in range(8)][-1],
                        deps=bf_toks + [t_wg] + hg_toks[-2:-1])
                    t_u = P.step("t", lambda t, half=half, fj=fj, ub=ub: [
                        t.matmul(ub[:, 0:320],
                                 wbuf[:, 8192 + hc * 1024 + fj * 128:
                                      8192 + hc * 1024 + (fj + 1) * 128],
                                 bfm[:, hc * 640 + half * 320:
                                     hc * 640 + half * 320 + 320],
                                 start=(hc == 0), stop=(hc == 7))
                        for hc in range(8)][-1],
                        deps=[t_wu] + hg_toks[-2:-1])
                    prev_gu += [t_g, t_u]
                    t_si = P.step("a", lambda sc, gb=gb: sc.activation(
                        ytm[:, 0:320], gb[:, 0:320], ACT.Silu),
                        deps=[t_g] + hg_toks[-1:])
                    t_hg = P.step("v", lambda v, half=half, fj=fj, ub=ub:
                                  v.tensor_tensor(
                                      hgm[:, fj * 640 + half * 320:
                                          fj * 640 + half * 320 + 320],
                                      ytm[:, 0:320], ub[:, 0:320],
                                      ALU.mult),
                                  deps=[t_si, t_u])
                    hg_toks.append(t_hg)
            t_wd = P.dma("sy", 8, lambda sy, fin, le=le: [
                fin(out=wbuf[:, hc * 1024:(hc + 1) * 1024],
                    in_=din["wd"][le, hc * 128:(hc + 1) * 128, :])
                for hc in range(8)][-1], deps=prev_gu)
            y_toks = []
            for half in range(2):
                for hj in range(8):
                    yb = psb[4 + (hj % 2)]
                    t_y = P.step("t", lambda t, half=half, hj=hj, yb=yb: [
                        t.matmul(yb[:, 0:320],
                                 wbuf[:, fc * 1024 + hj * 128:
                                      fc * 1024 + (hj + 1) * 128],
                                 hgm[:, fc * 640 + half * 320:
                                     fc * 640 + half * 320 + 320],
                                 start=(fc == 0), stop=(fc == 7))
                        for fc in range(8)][-1],
                        deps=hg_toks + [t_wd] + y_toks[-2:-1])
                    prev_dn.append(t_y)
                    t_ye = P.step("v", lambda v, half=half, hj=hj, yb=yb:
                                  v.tensor_copy(
                                      yfm[:, hj * 640 + half * 320:
                                          hj * 640 + half * 320 + 320],
                                      yb[:, 0:320]), deps=[t_y])
                    y_toks.append(t_ye)
            for bt in range(5):
                evs = []
                for hc in range(8):
                    bank = psb[6 + hc % 2]
                    ttr = P.step("t", lambda t, bt=bt, hc=hc, bank=bank:
                                 t.transpose(bank[:, :128],
                                             yfm[:, hc * 640 + bt * 128:
                                                 hc * 640 + (bt + 1) * 128],
                                             ident[:, :]),
                                 deps=y_toks + ([evs[-2]] if len(evs) >= 2
                                                else agy_toks[-1:] +
                                                hg_toks[-1:]))
                    tev = P.step("v", lambda v, hc=hc, bank=bank:
                                 v.tensor_copy(ytm[:, hc * 128:(hc + 1) * 128],
                                               bank[:, :128]),
                                 deps=[ttr] + agy_toks[-1:] + bt_prev)
                    evs.append(tev)
                agy_toks.append(P.dma("g", 1, lambda gp, fin, le=le, bt=bt:
                                      fin(out=ds["agy_in"][
                                          le * C + bt * 128:
                                          le * C + (bt + 1) * 128, :],
                                          in_=ytm[:, :]), deps=evs))

        t_cc_y = P.cc("AllGather", ALU.bypass, CC_ALL,
                      [ds["agy_in"].flatten()], [ds["y_full"].flatten()],
                      deps=agy_toks)
        if debug:
            P.dma("g", 1, lambda gp, fin: fin(out=ds["y_full_d"][:, :],
                                              in_=ds["y_full"][:, :]),
                  deps=[t_cc_y])

        gat = reg(0, 2048)
        bstl = reg(2048, 4096)
        outt = reg(4096, 6144)
        out_toks = []
        for j in range(16):
            ssl = j % 2
            t_gi = P.step("v", lambda v, j=j, ssl=ssl: v.tensor_copy(
                gidxi[:, ssl:ssl + 1], rt[:, j * 8 + 4:j * 8 + 5]),
                deps=rt_toks + out_toks[-2:-1])
            t_lb = P.dma("g", 1, lambda sy, fin, j=j, ssl=ssl: fin(
                out=bstl[:, ssl * 1024:(ssl + 1) * 1024],
                in_=ds["base_hbm"][j * 128:(j + 1) * 128, :]),
                deps=base_toks + [t_cc_y] + out_toks[-2:-1])
            t_ga = P.dma("g", 1, lambda gp, fin, ssl=ssl: fin(
                nc.gpsimd.indirect_dma_start(
                    out=gat[:, ssl * 1024:(ssl + 1) * 1024],
                    out_offset=None,
                    in_=ds["y_full"][:, :],
                    in_offset=IndirectOffsetOnAxis(gidxi[:, ssl:ssl + 1], 0),
                    bounds_check=bc_gath,
                    oob_is_err=False)),
                deps=[t_cc_y, t_gi] + out_toks[-2:-1])
            t_fo = P.step("v", lambda v, j=j, ssl=ssl:
                          v.scalar_tensor_tensor(
                              outt[:, ssl * 1024:(ssl + 1) * 1024],
                              gat[:, ssl * 1024:(ssl + 1) * 1024],
                              rt[:, j * 8 + 3:j * 8 + 4],
                              bstl[:, ssl * 1024:(ssl + 1) * 1024],
                              ALU.mult, ALU.add),
                          deps=[t_ga, t_lb] + out_toks[-2:-1])
            out_toks.append(P.dma("g", 1, lambda gp, fin, j=j, ssl=ssl: fin(
                out=out_own[j * 128:(j + 1) * 128, :],
                in_=outt[:, ssl * 1024:(ssl + 1) * 1024]), deps=[t_fo]))

        P.step("g", lambda gp: None, deps=out_toks)

        block = es.enter_context(nc.Block())
        P.finalize(block)
    return nc, din, ds, out_own


_CACHE = {}


def _get_program(debug=False):
    key = bool(debug)
    if key not in _CACHE:
        _CACHE[key] = build_program(debug=debug)
    return _CACHE[key]


def make_in_maps(inputs):
    x = np.asarray(inputs["L0_attn_x"], dtype=np.float32)
    Wq = np.asarray(inputs["L0_attn_Wq"], dtype=np.float32)
    Wk = np.asarray(inputs["L0_attn_Wk"], dtype=np.float32)
    Wv = np.asarray(inputs["L0_attn_Wv"], dtype=np.float32)
    Wo = np.asarray(inputs["L0_attn_Wo"], dtype=np.float32)
    bq = np.asarray(inputs["L0_attn_bq"], dtype=np.float32)
    bk = np.asarray(inputs["L0_attn_bk"], dtype=np.float32)
    bv = np.asarray(inputs["L0_attn_bv"], dtype=np.float32)
    bo = np.asarray(inputs["L0_attn_bo"], dtype=np.float32)
    wr = np.asarray(inputs["L0_ffn_w_router"], dtype=np.float32)
    wg = np.asarray(inputs["L0_ffn_w_gate"], dtype=np.float32)
    wu = np.asarray(inputs["L0_ffn_w_up"], dtype=np.float32)
    wd = np.asarray(inputs["L0_ffn_w_down"], dtype=np.float32)
    maps = []
    for c in range(NCORE):
        b, hh = c // 2, c % 2
        sl = slice(hh * 512, (hh + 1) * 512)
        meta = np.zeros((128, 4), np.float32)
        meta[:, 0] = c * ELOC * C
        mr = np.zeros(NCORE, np.float32)
        mr[:c] = 1.0
        maps.append({
            "x_b": np.ascontiguousarray(x[b]),
            "wqT": np.ascontiguousarray(Wq[sl, :].T),
            "wkT": np.ascontiguousarray(Wk[sl, :].T),
            "wvT": np.ascontiguousarray(Wv[sl, :].T),
            "bqs": np.ascontiguousarray(bq[sl]),
            "bks": np.ascontiguousarray(bk[sl]),
            "bvs": np.ascontiguousarray(bv[sl]),
            "woT": np.ascontiguousarray(Wo[:, sl].T),
            "bo": bo,
            "w_router": wr,
            "wg": np.ascontiguousarray(wg[c * ELOC:(c + 1) * ELOC]),
            "wu": np.ascontiguousarray(wu[c * ELOC:(c + 1) * ELOC]),
            "wd": np.ascontiguousarray(wd[c * ELOC:(c + 1) * ELOC]),
            "meta": meta,
            "maskrow": mr,
        })
    return maps


def kernel(**inputs):
    nc, din, ds, out_own = _get_program(debug=False)
    maps = make_in_maps(inputs)
    res = run_bass_kernel_spmd(nc, maps, core_ids=list(range(NCORE)))
    out = np.empty((B, S, H), np.float32)
    for c in range(NCORE):
        b, hh = c // 2, c % 2
        out[b, hh * TOWN:(hh + 1) * TOWN, :] = np.asarray(
            res.results[c]["out_own"])
    return out


# version 07
# speedup vs baseline: 1.2746x; 1.0034x over previous
import contextlib

import numpy as np

import concourse.bass as bass
import concourse.mybir as mybir
from concourse.bass_core_rust import IndirectOffsetOnAxis
from concourse.bass_utils import run_bass_kernel_spmd

F32 = mybir.dt.float32
BF16 = mybir.dt.bfloat16
I32 = mybir.dt.int32
ALU = mybir.AluOpType
ACT = mybir.ActivationFunctionType
AX = mybir.AxisListType.X

B, S, H = 4, 4096, 1024
E, C, FF = 32, 640, 1024
TOWN = 2048
EC = E * C
ELOC = 4
BIG = float(1 << 26)
EPS = 1e-5
NCORE = 8
CC_ALL = [[0, 1, 2, 3, 4, 5, 6, 7]]
CC_PAIRS = [[0, 1], [2, 3], [4, 5], [6, 7]]


class Prog:

    BODIES = ("t", "v", "a", "g", "sy")

    def __init__(self, nc, es):
        self.nc = nc
        names = ("t", "v", "a", "g", "gd", "sy", "cc")
        self.sem = {n: es.enter_context(nc.semaphore("sem_" + n)) for n in names}
        self.cnt = {n: 0 for n in names}
        self.waited = {b: {} for b in self.BODIES}
        self.stream = {b: [] for b in self.BODIES}

    def _waits(self, body, deps):
        own = {"t": "t", "v": "v", "a": "a", "g": "g", "sy": None}[body]
        need = {}
        for d in deps:
            if d is None:
                continue
            dn, dv = d
            if dn == own:
                continue
            if dv > self.waited[body].get(dn, 0):
                need[dn] = max(need.get(dn, 0), dv)
        out = []
        for dn, dv in need.items():
            self.waited[body][dn] = dv
            out.append((self.sem[dn], dv))
        return out

    def step(self, body, fn, deps=(), fast=False):
        waits = self._waits(body, list(deps))
        self.cnt[body] += 1
        tok = (body, self.cnt[body])
        sem = self.sem[body]
        drain = body == "t" and not fast

        def emit(eng, waits=waits, fn=fn, drain=drain):
            for s_, v_ in waits:
                eng.wait_ge(s_, v_)
            res = fn(eng)
            if drain:
                res = eng.drain()
            if res is not None and hasattr(res, "then_inc"):
                res.then_inc(sem, 1)
            else:
                eng.sem_inc(sem, 1)
        self.stream[body].append(emit)
        return tok

    def dma(self, body, n, fn, deps=()):
        sname = "gd" if body == "g" else "sy"
        waits = self._waits(body, list(deps))
        self.cnt[sname] += 16 * n
        tok = (sname, self.cnt[sname])
        sem = self.sem[sname]
        issued = [0]

        def emit(eng, waits=waits, fn=fn, n=n):
            for s_, v_ in waits:
                eng.wait_ge(s_, v_)

            def fin(instr=None, **kw):
                if instr is None:
                    instr = eng.dma_start(**kw)
                issued[0] += 1
                return instr.then_inc(sem, 16)
            fn(eng, fin)
            assert issued[0] == n, f"dma count {issued[0]} != {n}"
        self.stream[body].append(emit)
        return tok

    def cc(self, kind, op, groups, ins, outs, deps=()):
        waits = self._waits("g", list(deps))
        self.cnt["cc"] += 1
        tok = ("cc", self.cnt["cc"])
        sem = self.sem["cc"]

        def emit(eng, waits=waits):
            for s_, v_ in waits:
                eng.wait_ge(s_, v_)
            eng.collective_compute(kind, op, replica_groups=groups,
                                   ins=ins, outs=outs).then_inc(sem, 1)
        self.stream["g"].append(emit)
        return tok

    def finalize(self, block):
        hosts = {"t": block.tensor, "v": block.vector, "a": block.scalar,
                 "g": block.gpsimd, "sy": block.sync}
        for body in self.BODIES:
            emitters = self.stream[body]

            def run_all(eng, emitters=emitters):
                for em in emitters:
                    em(eng)
            hosts[body](run_all)


def build_program(debug=False):
    nc = bass.Bass("TRN2", target_bir_lowering=False, num_devices=NCORE,
                   detect_race_conditions=False)

    din = {}
    for nm, shp in [
        ("x_b", [S, H]),
        ("wqT", [H, 512]), ("wkT", [H, 512]), ("wvT", [H, 512]),
        ("bqs", [512]), ("bks", [512]), ("bvs", [512]),
        ("woT", [512, H]), ("bo", [H]),
        ("w_router", [H, E]),
        ("wg", [ELOC, H, FF]), ("wu", [ELOC, H, FF]), ("wd", [ELOC, FF, H]),
        ("meta", [128, 4]),
        ("maskrow", [NCORE]),
    ]:
        din[nm] = nc.dram_tensor(nm, shp, F32, kind="ExternalInput").ap()

    out_own = nc.dram_tensor("out_own", [TOWN, H], F32, kind="ExternalOutput").ap()

    dkind = "ExternalOutput" if debug else "Internal"
    ds = {}
    for nm, shp in [
        ("k_hbm", [512, S]), ("v_hbm", [S, 512]), ("o_hbm", [512, S]),
        ("xf_full", [H, S]),
        ("base_hbm", [TOWN, H]),
        ("buf", [ELOC * C, H]),
    ]:
        ds[nm] = nc.dram_tensor(nm, shp, F32, kind=dkind).ap()
    for nm, shp in [
        ("rs_in", [2, H, TOWN]), ("rs_out", [H, TOWN]),
        ("cnt_in", [E]), ("cnt_out", [NCORE * E]),
        ("slot_in", [TOWN]), ("slot_out", [NCORE * TOWN]),
        ("agx_in", [TOWN, H]), ("xt_full", [NCORE * TOWN, H]),
        ("agy_in", [ELOC * C, H]), ("y_full", [EC, H]),
    ]:
        ds[nm] = nc.dram_tensor(nm, shp, F32, kind="Internal").ap()
    if debug:
        for nm, shp in [("rs_out_d", [H, TOWN]), ("agx_in_d", [TOWN, H]),
                        ("xt_full_d", [NCORE * TOWN, H]),
                        ("y_full_d", [EC, H]),
                        ("slot_out_d", [NCORE * TOWN])]:
            ds[nm] = nc.dram_tensor(nm, shp, F32, kind="ExternalOutput").ap()
    if debug:
        for nm, shp in [("q_dbg", [128, 16384]), ("rt_dbg", [128, 16 * 8]),
                        ("ln_dbg", [128, 4096]), ("st_dbg", [128, 16]),
                        ("rsin_d", [2, H, TOWN]), ("x2_d", [H, TOWN]),
                        ("x2b_d", [H, TOWN]),
                        ("lr_d", [4, 4 * 2048]),
                        ("lg_dbg", [128, 512]), ("ohm_dbg", [32, TOWN]),
                        ("ccm_dbg", [32, TOWN]), ("rows_dbg", [8, TOWN]),
                        ("m_dbg", [1, 16])]:
            ds[nm] = nc.dram_tensor(nm, shp, F32, kind="ExternalOutput").ap()

    es = contextlib.ExitStack()
    with es:
        def sbuf(name, shp, dt_=F32):
            return es.enter_context(nc.sbuf_tensor(name, shp, dt_))

        wbuf = sbuf("wbuf", [128, 16384])
        big1 = sbuf("big1", [128, 16384])
        scp = sbuf("scp", [128, 16384])

        def reg(a, b):
            return scp[:, a:b]

        ident = sbuf("ident", [128, 128])
        identi = sbuf("identi", [128, 128], I32)
        pcol = sbuf("pcol", [128, 1])
        e2ind = sbuf("e2ind", [128, 2])
        sel01 = sbuf("sel01", [128, 2])
        ones1 = sbuf("ones1", [1, 128])
        onesc = sbuf("onesc", [128, 1])
        bia = sbuf("bia", [128, 12])
        bo_sb = sbuf("bo_sb", [128, 8])
        wr_sb = sbuf("wr_sb", [128, 8 * 32])
        iotaE = sbuf("iotaE", [128, 32])
        iotaEi = sbuf("iotaEi", [128, 32], I32)
        w32 = sbuf("w32", [128, 32])
        w32i = sbuf("w32i", [128, 32], I32)
        probs = sbuf("probs", [128, 64])
        scr = sbuf("scr", [128, 16])
        rt = sbuf("rt", [128, 16 * 8])
        meta_sb = sbuf("meta_sb", [128, 4])
        mrow = sbuf("mrow", [8, 1])
        offc = sbuf("offc", [32, 1])
        cnt8 = sbuf("cnt8", [8, 32])
        r8t = sbuf("r8t", [128, 8])
        idxt = sbuf("idxt", [128, 8])
        idxi = sbuf("idxi", [128, 2], I32)
        gidxi = sbuf("gidxi", [128, 2], I32)
        nmh = sbuf("nmh", [128, 2])
        srow = sbuf("srow", [1, 16])
        stat = sbuf("stat", [128, 16])
        kn2m = sbuf("kn2m", [2, 4])
        qn2m = sbuf("qn2m", [2, 4])
        masks = sbuf("masks", [128, 4 * 512])
        maski = sbuf("maski", [128, 512], I32)
        bfa = sbuf("bfa", [128, 128])

        psb = [es.enter_context(nc.psum_tensor(f"psb{i}", [128, 512], F32))
               for i in range(8)]

        P = Prog(nc, es)
        bc_scat = nc.gpsimd.alloc_register("bc_scat")
        bc_gath = nc.gpsimd.alloc_register("bc_gath")

        mask_toks = []
        t_iota0 = P.step("g", lambda gp: [
            gp.reg_mov(bc_scat, ELOC * C - 1),
            gp.reg_mov(bc_gath, EC - 1),
            gp.iota(identi[:, :], pattern=[[-1, 128]], base=0,
                    channel_multiplier=1),
            gp.iota(iotaEi[:, :], pattern=[[1, 32]], base=0,
                    channel_multiplier=0),
            gp.iota(w32i[:, :], pattern=[[-1, 32]], base=32,
                    channel_multiplier=0)][-1])
        t_c0 = P.step("v", lambda v: [
            v.tensor_single_scalar(ident[:, :], identi[:, :], 0, ALU.is_equal),
            v.tensor_copy(pcol[:, :], identi[:, 0:1]),
            v.tensor_single_scalar(e2ind[:, 0:1], identi[:, 0:1], 64, ALU.is_lt),
            v.tensor_single_scalar(e2ind[:, 1:2], identi[:, 0:1], 63, ALU.is_gt),
            v.tensor_single_scalar(sel01[:, 0:1], identi[:, 0:1], 0, ALU.is_equal),
            v.tensor_single_scalar(sel01[:, 1:2], identi[:, 0:1], 1, ALU.is_equal),
            v.tensor_copy(iotaE[:, :], iotaEi[:, :]),
            v.tensor_copy(w32[:, :], w32i[:, :]),
            v.memset(ones1[:, :], 1.0),
            v.memset(onesc[:, :], 1.0),
            v.memset(nmh[:, 1:2], 0.125),
            v.memset(kn2m[:, :], 0.0),
            v.memset(qn2m[:, :], 0.0),
            v.memset(rt[:, :], 0.0),
            v.memset(probs[:, :], 0.0),
            v.memset(scr[:, :], 0.0),
            v.memset(srow[:, :], 0.0),
            v.memset(stat[:, :], 0.0)][-1], deps=[t_iota0])
        for i, dlt in enumerate([0, -128, -256, -384]):
            tm1 = P.step("g", lambda gp, dlt=dlt: gp.iota(
                maski[:, :], pattern=[[1, 512]], base=dlt,
                channel_multiplier=-1), deps=mask_toks[-1:] or [t_iota0])
            mask_toks.append(P.step("v", lambda v, i=i: v.tensor_single_scalar(
                masks[:, i * 512:(i + 1) * 512], maski[:, :], -1, ALU.is_gt),
                deps=[tm1]))

        def w_loads(sy, fin):
            for i, nm in enumerate(["wqT", "wkT", "wvT"]):
                for hc in range(8):
                    fin(out=wbuf[:, i * 4096 + hc * 512:i * 4096 + (hc + 1) * 512],
                        in_=din[nm][hc * 128:(hc + 1) * 128, :])
            for dc in range(4):
                fin(out=wbuf[:, 12288 + dc * 1024:12288 + (dc + 1) * 1024],
                    in_=din["woT"][dc * 128:(dc + 1) * 128, :])
            for i, nm in enumerate(["bqs", "bks", "bvs"]):
                for mj in range(4):
                    fin(out=bia[:, i * 4 + mj:i * 4 + mj + 1],
                        in_=din[nm][mj * 128:(mj + 1) * 128].unsqueeze(1))
            for hc in range(8):
                fin(out=bo_sb[:, hc:hc + 1],
                    in_=din["bo"][hc * 128:(hc + 1) * 128].unsqueeze(1))
            for hc in range(8):
                fin(out=wr_sb[:, hc * 32:(hc + 1) * 32],
                    in_=din["w_router"][hc * 128:(hc + 1) * 128, :])
            fin(out=meta_sb[:, :], in_=din["meta"][:, :])
            fin(out=mrow[:, :], in_=din["maskrow"][:].unsqueeze(1))
        t_w = P.dma("sy", 28 + 12 + 8 + 8 + 2, w_loads)

        xa = reg(0, 4096)
        lnta = reg(4096, 8192)
        lnf = reg(8192, 12288)
        sqt = reg(12288, 13312)
        ktmp = reg(13312, 13824)
        vtmp = reg(13824, 14336)
        ksq = reg(14336, 14848)
        evt = reg(14848, 15360)

        xa_readers, lnf_readers, lnta_guard = [], [], []
        prev_evac = {}
        kv_store_toks = []
        xf_store_toks = []
        q_evac_toks = []
        t_qn2_last = None
        t_kn2_last = None
        xf_ev_m2 = None

        for tau in range(8):
            t0 = tau * 512
            t_ldx = P.dma("g", 4, lambda sy, fin, t0=t0: [
                fin(out=xa[:, s4 * 1024:(s4 + 1) * 1024],
                    in_=din["x_b"][t0 + s4 * 128:t0 + (s4 + 1) * 128, :])
                for s4 in range(4)][-1],
                deps=xa_readers + [t_c0])
            xa_readers = []

            t_sums = P.step("v", lambda v: [
                v.reduce_sum(stat[:, s4 * 4:s4 * 4 + 1],
                             xa[:, s4 * 1024:(s4 + 1) * 1024], AX)
                for s4 in range(4)][-1], deps=[t_ldx])
            t_sq = P.step("a", lambda sc: [
                sc.activation(lnta[:, s4 * 1024:(s4 + 1) * 1024],
                              xa[:, s4 * 1024:(s4 + 1) * 1024], ACT.Square,
                              accum_out=stat[:, s4 * 4 + 1:s4 * 4 + 2])
                for s4 in range(4)][-1], deps=[t_ldx] + lnta_guard)
            xa_readers += [t_sums, t_sq]

            def fin_stats(v):
                for s4 in range(4):
                    c0 = s4 * 4
                    v.tensor_scalar_mul(stat[:, c0:c0 + 1],
                                        stat[:, c0:c0 + 1], 1.0 / 1024)
                for s4 in range(4):
                    c0 = s4 * 4
                    v.tensor_scalar_mul(stat[:, c0 + 1:c0 + 2],
                                        stat[:, c0 + 1:c0 + 2], 1.0 / 1024)
                for s4 in range(4):
                    c0 = s4 * 4
                    v.tensor_tensor(stat[:, c0 + 2:c0 + 3], stat[:, c0:c0 + 1],
                                    stat[:, c0:c0 + 1], ALU.mult)
                for s4 in range(4):
                    c0 = s4 * 4
                    v.tensor_tensor(stat[:, c0 + 2:c0 + 3],
                                    stat[:, c0 + 1:c0 + 2],
                                    stat[:, c0 + 2:c0 + 3], ALU.subtract)
                for s4 in range(4):
                    c0 = s4 * 4
                    last = v.tensor_scalar_add(stat[:, c0 + 2:c0 + 3],
                                               stat[:, c0 + 2:c0 + 3], EPS)
                return last
            t_fs = P.step("v", fin_stats, deps=[t_sums, t_sq])
            t_sqr = P.step("a", lambda sc: [
                sc.activation(stat[:, s4 * 4 + 3:s4 * 4 + 4],
                              stat[:, s4 * 4 + 2:s4 * 4 + 3], ACT.Sqrt)
                for s4 in range(4)][-1], deps=[t_fs])

            def fin_rstd(v):
                for s4 in range(4):
                    c0 = s4 * 4
                    v.reciprocal(stat[:, c0 + 2:c0 + 3],
                                 stat[:, c0 + 3:c0 + 4])
                for s4 in range(4):
                    c0 = s4 * 4
                    last = v.scalar_tensor_tensor(stat[:, c0 + 3:c0 + 4],
                                                  stat[:, c0:c0 + 1], -1.0,
                                                  stat[:, c0 + 2:c0 + 3],
                                                  ALU.mult, ALU.mult)
                return last
            t_rs = P.step("v", fin_rstd, deps=[t_sqr])
            t_ln = P.step("a", lambda sc: [
                sc.activation(lnta[:, s4 * 1024:(s4 + 1) * 1024],
                              xa[:, s4 * 1024:(s4 + 1) * 1024], ACT.Identity,
                              bias=stat[:, s4 * 4 + 3:s4 * 4 + 4],
                              scale=stat[:, s4 * 4 + 2:s4 * 4 + 3])
                for s4 in range(4)][-1], deps=[t_rs] + lnta_guard)
            xa_readers.append(t_ln)

            ev_prev = []
            for i, (s4, hc) in enumerate((s4, hc) for s4 in range(4)
                                         for hc in range(8)):
                bank = psb[4 + i % 2]
                ttr = P.step("t", lambda t, s4=s4, hc=hc, bank=bank:
                             t.transpose(bank[:, :128],
                                         lnta[:, s4 * 1024 + hc * 128:
                                              s4 * 1024 + (hc + 1) * 128],
                                         ident[:, :]),
                             deps=[t_ln, t_c0] +
                             (lnf_readers if i == 0 else []) +
                             ([ev_prev[i - 2]] if i >= 2 else []))
                tev = P.step("v", lambda v, s4=s4, hc=hc, bank=bank:
                             v.tensor_copy(lnf[:, hc * 512 + s4 * 128:
                                               hc * 512 + (s4 + 1) * 128],
                                           bank[:, :128]), deps=[ttr])
                ev_prev.append(tev)
            lnf_ready = list(ev_prev)
            lnta_guard = ev_prev[-2:]
            lnf_readers = []
            if debug and tau == 0:
                tdbg = P.dma("g", 2, lambda gp, fin: [
                    fin(out=ds["ln_dbg"][:, :], in_=lnf[:, :]),
                    fin(out=ds["st_dbg"][:, :], in_=stat[:, :])][-1],
                    deps=lnf_ready + [t_rs])
                lnf_readers.append(tdbg)

            for hc in range(8):
                evs = []
                for s4 in range(4):
                    i = hc * 4 + s4
                    bank = psb[6 + i % 2]
                    dep_dma = [xf_store_toks[-2]] if len(xf_store_toks) >= 2 \
                        else []
                    ttr = P.step("t", lambda t, s4=s4, hc=hc, bank=bank:
                                 t.transpose(bank[:, :128],
                                             xa[:, s4 * 1024 + hc * 128:
                                                s4 * 1024 + (hc + 1) * 128],
                                             ident[:, :]),
                                 deps=[t_ldx, t_c0] + dep_dma +
                                 ([evs[-2]] if len(evs) >= 2 else
                                  ([xf_ev_m2] if xf_ev_m2 else [])))
                    tev = P.step("v", lambda v, s4=s4, bank=bank:
                                 v.tensor_copy(evt[:, s4 * 128:(s4 + 1) * 128],
                                               bank[:, :128]),
                                 deps=[ttr] + xf_store_toks[-1:] +
                                 kv_store_toks[-1:])
                    evs.append(tev)
                xf_ev_m2 = evs[-2]
                xa_readers.append(evs[-1])
                xf_store_toks.append(P.dma("g", 1, lambda gp, fin, hc=hc, t0=t0:
                                           fin(out=ds["xf_full"][
                                               hc * 128:(hc + 1) * 128,
                                               t0:t0 + 512],
                                               in_=evt[:, :]),
                                           deps=[evs[-1]]))

            for mat in range(3):
                for mj in range(4):
                    bank = psb[mj % 2]
                    tmm = P.step("t", lambda t, mat=mat, mj=mj, bank=bank: [
                        t.matmul(bank[:, :],
                                 wbuf[:, mat * 4096 + hc * 512 + mj * 128:
                                      mat * 4096 + hc * 512 + (mj + 1) * 128],
                                 lnf[:, hc * 512:(hc + 1) * 512],
                                 start=(hc == 0), stop=(hc == 7))
                        for hc in range(8)][-1],
                        deps=lnf_ready + [t_w] + prev_evac.get((mat, mj), []))
                    lnf_readers.append(tmm)
                    if mat == 0:
                        tev = P.step("v", lambda v, mj=mj, bank=bank, t0=t0:
                                     v.tensor_scalar(
                                         big1[:, mj * 4096 + t0:
                                              mj * 4096 + t0 + 512],
                                         bank[:, :], bia[:, mj:mj + 1], None,
                                         ALU.add), deps=[tmm])
                        q_evac_toks.append(tev)
                        tsq = P.step("a", lambda sc, mj=mj, t0=t0:
                                     sc.activation(ksq[:, :],
                                                   big1[:, mj * 4096 + t0:
                                                        mj * 4096 + t0 + 512],
                                                   ACT.Square),
                                     deps=[tev, t_qn2_last, t_kn2_last])
                        tnm = P.step("t", lambda t:
                                     t.matmul(psb[3][0:2, :], e2ind[:, :],
                                              ksq[:, :], start=True, stop=True),
                                     deps=[tsq, t_qn2_last])
                        t_qn2_last = P.step("v", lambda v, mj=mj: [
                            v.reduce_max(scr[0:2, 4:5], psb[3][0:2, :], AX),
                            v.tensor_scalar_add(scr[:, 7:8], scr[:, 7:8], 0.0),
                            v.tensor_tensor(qn2m[:, mj:mj + 1],
                                            qn2m[:, mj:mj + 1],
                                            scr[0:2, 4:5], ALU.max)][-1],
                            deps=[tnm])
                        prev_evac[(mat, mj)] = [tev, tsq]
                    elif mat == 1:
                        tev = P.step("v", lambda v, mj=mj, bank=bank:
                                     v.tensor_scalar(ktmp[:, :], bank[:, :],
                                                     bia[:, 4 + mj:5 + mj],
                                                     None, ALU.add),
                                     deps=[tmm] + prev_evac.get(("kst", 0), []))
                        tdm = P.dma("g", 1, lambda gp, fin, mj=mj, t0=t0:
                                    fin(out=ds["k_hbm"][mj * 128:(mj + 1) * 128,
                                                        t0:t0 + 512],
                                        in_=ktmp[:, :]), deps=[tev])
                        kv_store_toks.append(tdm)
                        tsq = P.step("a", lambda sc: sc.activation(
                            ksq[:, :], ktmp[:, :], ACT.Square),
                            deps=[tev, t_kn2_last, t_qn2_last])
                        tnm = P.step("t", lambda t:
                                     t.matmul(psb[3][0:2, :], e2ind[:, :],
                                              ksq[:, :], start=True, stop=True),
                                     deps=[tsq, t_kn2_last])
                        t_kn2_last = P.step("v", lambda v, mj=mj: [
                            v.reduce_max(scr[0:2, 4:5], psb[3][0:2, :], AX),
                            v.tensor_scalar_add(scr[:, 7:8], scr[:, 7:8], 0.0),
                            v.tensor_tensor(kn2m[:, mj:mj + 1],
                                            kn2m[:, mj:mj + 1],
                                            scr[0:2, 4:5], ALU.max)][-1],
                            deps=[tnm])
                        prev_evac[(mat, mj)] = [tdm, tsq]
                        prev_evac[("kst", 0)] = [tdm, tsq]
                    else:
                        tev = P.step("v", lambda v, mj=mj, bank=bank:
                                     v.tensor_scalar(vtmp[:, :], bank[:, :],
                                                     bia[:, 8 + mj:9 + mj],
                                                     None, ALU.add),
                                     deps=[tmm] + prev_evac.get(("vst", 0), []))
                        evs = []
                        for s4 in range(4):
                            ttr = P.step("t", lambda t, s4=s4:
                                         t.transpose(psb[2][:, :128],
                                                     vtmp[:, s4 * 128:
                                                          (s4 + 1) * 128],
                                                     ident[:, :]),
                                         deps=[tev] + (evs[-1:] if evs else
                                                       prev_evac.get(("vtr", 0),
                                                                     [])))
                            tev2 = P.step("v", lambda v, s4=s4:
                                          v.tensor_copy(
                                              evt[:, s4 * 128:(s4 + 1) * 128],
                                              psb[2][:, :128]),
                                          deps=[ttr] + xf_store_toks[-1:] +
                                          kv_store_toks[-1:])
                            evs.append(tev2)
                        tdm = P.dma("g", 1, lambda gp, fin, mj=mj, t0=t0: fin(
                            out=ds["v_hbm"][t0:t0 + 512,
                                            mj * 128:(mj + 1) * 128]
                            .rearrange("(a p) d -> p a d", p=128),
                            in_=evt[:, :].rearrange("p (a d) -> p a d", d=128)),
                            deps=[evs[-1]])
                        kv_store_toks.append(tdm)
                        prev_evac[(mat, mj)] = [tdm]
                        prev_evac[("vst", 0)] = [tdm]
                        prev_evac[("vtr", 0)] = [evs[-1]]

        t_msq = P.step("a", lambda sc: [
            sc.activation(qn2m[:, :], qn2m[:, :], ACT.Sqrt),
            sc.activation(kn2m[:, :], kn2m[:, :], ACT.Sqrt)][-1],
            deps=[t_qn2_last, t_kn2_last])
        t_mm2 = P.step("v", lambda v: [
            v.tensor_tensor(kn2m[:, :], kn2m[:, :], qn2m[:, :], ALU.mult),
            v.tensor_scalar_add(scr[:, 7:8], scr[:, 7:8], 0.0),
            v.tensor_scalar_mul(kn2m[:, :], kn2m[:, :], -0.125)][-1],
            deps=[t_msq])
        t_se0 = P.step("t", lambda t: t.matmul(
            psb[3][0:1, 0:4], sel01[0:2, 0:1], kn2m[:, :],
            start=True, stop=True), deps=[t_mm2, t_qn2_last, t_kn2_last])
        t_sv0 = P.step("v", lambda v: v.tensor_copy(
            srow[0:1, 0:4], psb[3][0:1, 0:4]), deps=[t_se0])
        t_se1 = P.step("t", lambda t: t.matmul(
            psb[3][0:1, 0:4], sel01[0:2, 1:2], kn2m[:, :],
            start=True, stop=True), deps=[t_sv0])
        t_mrow = P.step("v", lambda v: v.tensor_copy(
            srow[0:1, 4:8], psb[3][0:1, 0:4]), deps=[t_se1])
        if debug:
            P.dma("g", 8, lambda gp, fin: [
                fin(out=ds["q_dbg"][:, i * 2048:(i + 1) * 2048],
                    in_=big1[:, i * 2048:(i + 1) * 2048]) for i in range(8)]
                and None, deps=q_evac_toks)
            P.dma("g", 1, lambda gp, fin: fin(out=ds["m_dbg"][:, :],
                                              in_=srow[:, :].unsqueeze(0)
                                              if False else srow[:, :]),
                  deps=[t_mrow])

        kstage = reg(0, 4096)
        vhx = reg(4096, 6176).bitcast(BF16)
        vlx = reg(6176, 8256).bitcast(BF16)
        vstage = reg(8256, 10336)
        ptile = reg(10336, 11872)
        phx = reg(11872, 12640).bitcast(BF16)
        plx = reg(12640, 13408).bitcast(BF16)
        rrow = reg(13408, 13920)[0:1, :]
        ofh = reg(13920, 14432)[0:64, :]
        rbc = reg(14432, 14944)[0:64, :]
        khx = wbuf[:, 0:2048].bitcast(BF16)
        klx = wbuf[:, 2048:4096].bitcast(BF16)
        qhx = wbuf[:, 4096:8192].bitcast(BF16)
        qlx = wbuf[:, 8192:12288].bitcast(BF16)

        o_store_toks = []
        head_users = {}
        exp_toks = {}
        aexp_last = {}
        prev_n4 = None
        prev_bc = None
        t_ksp = {}
        t_vsp = {}
        t_qsp = {}
        t_vf = None
        for h in range(8):
            par = h % 2
            cb = par
            hf = h // 2
            qh2 = (hf % 2) * 4096
            if h % 2 == 0:
                war_q = [exp_toks[(h - 3, 7, 31)]] if h >= 4 else []
                tqh = P.step("a", lambda sc, hf=hf, qh2=qh2: sc.activation(
                    qhx[:, qh2:qh2 + 4096],
                    big1[:, hf * 4096:(hf + 1) * 4096], ACT.Copy),
                    deps=[q_evac_toks[-1], t_mrow] + war_q)
                t_qsp[hf] = P.step("v", lambda v, hf=hf, qh2=qh2:
                                   v.tensor_tensor(
                                       qlx[:, qh2:qh2 + 4096],
                                       big1[:, hf * 4096:(hf + 1) * 4096],
                                       qhx[:, qh2:qh2 + 4096], ALU.subtract),
                                   deps=[tqh])
            t_ldk = P.dma("g", 1, lambda gp, fin, h=h, par=par: fin(
                out=kstage[par * 64:par * 64 + 64, :],
                in_=ds["k_hbm"][64 * h:64 * h + 64, :]),
                deps=kv_store_toks + ([t_ksp[h - 2]] if h >= 2 else []))
            war_k = [exp_toks[(h - 2, 7, 31)]] if h >= 2 else []
            tkh = P.step("a", lambda sc, par=par: sc.activation(
                khx[par * 64:par * 64 + 64, :],
                kstage[par * 64:par * 64 + 64, :], ACT.Copy),
                deps=[t_ldk, t_mrow] + war_k)
            t_ksp[h] = P.step("v", lambda v, par=par: v.tensor_tensor(
                klx[par * 64:par * 64 + 64, :],
                kstage[par * 64:par * 64 + 64, :],
                khx[par * 64:par * 64 + 64, :], ALU.subtract),
                deps=[tkh])
            if h == 0:
                t_vf = P.step("v", lambda v: v.memset(vstage[:, :], 1.0),
                              deps=kv_store_toks + [t_mrow])
            t_ldv = P.dma("g", 1, lambda gp, fin, h=h: fin(
                out=vstage[:, :].rearrange("p (a w) -> p a w", w=65)[:, :, 0:64],
                in_=ds["v_hbm"][:, 64 * h:64 * h + 64]
                .rearrange("(a p) d -> p a d", p=128)),
                deps=[t_vf] if h == 0 else [t_vsp[h - 1]])
            war_v = head_users[h - 2][-1:] if h >= 2 else []
            tvh = P.step("a", lambda sc, cb=cb: sc.activation(
                vhx[:, cb * 2080:(cb + 1) * 2080], vstage[:, :], ACT.Copy),
                deps=[t_ldv] + war_v)
            t_vsp[h] = P.step("v", lambda v, cb=cb: v.tensor_tensor(
                vlx[:, cb * 2080:(cb + 1) * 2080], vstage[:, :],
                vhx[:, cb * 2080:(cb + 1) * 2080], ALU.subtract),
                deps=[tvh])
            users = []

            col = (h % 2) * 4 + h // 2
            t_n4 = P.step("t", lambda t, col=col: t.matmul(
                psb[7][0:128, 0:1], ones1[0:1, :], srow[0:1, col:col + 1],
                start=True, stop=True),
                deps=[t_mrow, t_c0] + ([prev_n4] if prev_n4 else []))
            prev_n4 = t_n4
            t_nm = P.step("v", lambda v: v.tensor_copy(
                nmh[:, 0:1], psb[7][0:128, 0:1]),
                deps=[t_n4] + ([aexp_last[h - 1]] if h >= 1 else []))

            for qt in range(8):
                nk = qt * 4 + 4
                obank = psb[2 + qt % 2]
                t_oev_prev = o_store_toks[-1] if o_store_toks else None
                av_toks = []
                pdeps = {}

                def emit_sc(kt, qt=qt):
                    sbank = psb[kt % 2]
                    psl = kt % 3
                    t_sc = P.step("t", lambda t, par=par, qh2=qh2, qt=qt,
                                  kt=kt, sbank=sbank: [
                                      t.matmul(
                                          sbank[:, :],
                                          khx[par * 64:par * 64 + 64,
                                              kt * 128:(kt + 1) * 128],
                                          qhx[par * 64:par * 64 + 64,
                                              qh2 + qt * 512:
                                              qh2 + (qt + 1) * 512],
                                          start=True, stop=False),
                                      t.matmul(
                                          sbank[:, :],
                                          klx[par * 64:par * 64 + 64,
                                              kt * 128:(kt + 1) * 128],
                                          qhx[par * 64:par * 64 + 64,
                                              qh2 + qt * 512:
                                              qh2 + (qt + 1) * 512],
                                          start=False, stop=False),
                                      t.matmul(
                                          sbank[:, :],
                                          khx[par * 64:par * 64 + 64,
                                              kt * 128:(kt + 1) * 128],
                                          qlx[par * 64:par * 64 + 64,
                                              qh2 + qt * 512:
                                              qh2 + (qt + 1) * 512],
                                          start=False, stop=True)][-1],
                                  fast=True,
                                  deps=[t_ksp[h], t_qsp[hf], t_nm] +
                                  ([exp_toks[(h, qt, kt - 2)]]
                                   if kt >= 2 else
                                   ([exp_toks[(h, qt - 1, qt * 4 - 1)]]
                                    if qt >= 1 else
                                    ([exp_toks[(h - 1, 7, 31)]]
                                     if h >= 1 else []))))
                    if kt >= qt * 4:
                        t_ex = P.step("a", lambda sc, psl=psl, sbank=sbank:
                                      sc.activation(
                                          ptile[:, psl * 512:(psl + 1) * 512],
                                          sbank[:, :], ACT.Exp,
                                          bias=nmh[:, 0:1], scale=nmh[:, 1:2]),
                                      deps=[t_sc] +
                                      ([av_toks[kt - 3]] if kt >= 3 else []))
                        i = kt - qt * 4
                        dep_p = P.step("v", lambda v, psl=psl, i=i: [
                            v.tensor_tensor(
                                ptile[:, psl * 512:(psl + 1) * 512],
                                ptile[:, psl * 512:(psl + 1) * 512],
                                masks[:, i * 512:(i + 1) * 512], ALU.mult),
                            v.tensor_copy(
                                phx[:, psl * 512:(psl + 1) * 512],
                                ptile[:, psl * 512:(psl + 1) * 512]),
                            v.tensor_tensor(
                                plx[:, psl * 512:(psl + 1) * 512],
                                ptile[:, psl * 512:(psl + 1) * 512],
                                phx[:, psl * 512:(psl + 1) * 512],
                                ALU.subtract)][-1],
                            deps=[t_ex] + mask_toks +
                            ([av_toks[kt - 3]] if kt >= 3 else []))
                    else:
                        t_ex = P.step("a", lambda sc, psl=psl, sbank=sbank: [
                            sc.activation(
                                ptile[:, psl * 512:(psl + 1) * 512],
                                sbank[:, :], ACT.Exp,
                                bias=nmh[:, 0:1], scale=nmh[:, 1:2]),
                            sc.activation(
                                phx[:, psl * 512:(psl + 1) * 512],
                                sbank[:, :], ACT.Exp,
                                bias=nmh[:, 0:1], scale=nmh[:, 1:2])][-1],
                            deps=[t_sc] +
                            ([av_toks[kt - 3]] if kt >= 3 else []))
                        dep_p = P.step("v", lambda v, psl=psl:
                                       v.tensor_tensor(
                                           plx[:, psl * 512:(psl + 1) * 512],
                                           ptile[:, psl * 512:(psl + 1) * 512],
                                           phx[:, psl * 512:(psl + 1) * 512],
                                           ALU.subtract),
                                       deps=[t_ex])
                    exp_toks[(h, qt, kt)] = dep_p
                    pdeps[kt] = dep_p
                    aexp_last[h] = t_ex

                def emit_av(kt, qt=qt, obank=obank, nk=nk,
                            t_oev_prev=t_oev_prev):
                    psl = kt % 3
                    dep_p = pdeps[kt]
                    t_av = P.step("t", lambda t, cb=cb, kt=kt, psl=psl,
                                  obank=obank, nk=nk: [
                                      t.matmul(
                                          obank[0:65, :],
                                          vhx[:, cb * 2080 + kt * 65:
                                              cb * 2080 + (kt + 1) * 65],
                                          phx[:, psl * 512:(psl + 1) * 512],
                                          start=(kt == 0), stop=False),
                                      t.matmul(
                                          obank[0:65, :],
                                          vlx[:, cb * 2080 + kt * 65:
                                              cb * 2080 + (kt + 1) * 65],
                                          phx[:, psl * 512:(psl + 1) * 512],
                                          start=False, stop=False),
                                      t.matmul(
                                          obank[0:65, :],
                                          vhx[:, cb * 2080 + kt * 65:
                                              cb * 2080 + (kt + 1) * 65],
                                          plx[:, psl * 512:(psl + 1) * 512],
                                          start=False,
                                          stop=(kt == nk - 1))][-1],
                                  fast=True,
                                  deps=[dep_p, t_vsp[h]] +
                                  ([t_oev_prev] if kt == 0 and t_oev_prev
                                   else []))
                    av_toks.append(t_av)

                for kt in range(nk):
                    emit_sc(kt)
                    if kt >= 1:
                        emit_av(kt - 1)
                emit_av(nk - 1)
                users += [av_toks[-1]]
                t_rc = P.step("v", lambda v, obank=obank: v.reciprocal(
                    rrow[:, :], obank[64:65, :]),
                    deps=[av_toks[-1]] + ([prev_bc] if prev_bc else []))
                t_bc = P.step("t", lambda t: t.matmul(
                    psb[6][0:64, :], ones1[0:1, 0:64], rrow[:, :],
                    start=True, stop=True),
                    deps=[t_rc] + (o_store_toks[-1:] if o_store_toks else []))
                prev_bc = t_bc
                t_bce = P.step("v", lambda v: v.tensor_copy(
                    rbc[:, :], psb[6][0:64, :]), deps=[t_bc])
                t_oe = P.step("v", lambda v, obank=obank: v.tensor_tensor(
                    ofh[:, :], obank[0:64, :], rbc[:, :], ALU.mult),
                    deps=[t_bce] + (o_store_toks[-1:] if o_store_toks else []))
                o_store_toks.append(P.dma("g", 1, lambda gp, fin, h=h, qt=qt:
                                          fin(out=ds["o_hbm"][
                                              64 * h:64 * h + 64,
                                              qt * 512:(qt + 1) * 512],
                                              in_=ofh[:, :]),
                                          deps=[t_oe]))
            head_users[h] = users

        ob = reg(0, 4096)
        rstc = reg(4096, 5120)
        xftl = reg(5120, 6144)
        rs_dma_toks = []
        mm_toks_c = {}
        for tau in range(8):
            pp = tau % 2
            hf, tc = tau // 4, tau % 4
            t_ldo = P.dma("sy", 4, lambda sy, fin, tau=tau, pp=pp: [
                fin(out=ob[:, pp * 2048 + dc * 512:pp * 2048 + (dc + 1) * 512],
                    in_=ds["o_hbm"][dc * 128:(dc + 1) * 128,
                                    tau * 512:(tau + 1) * 512])
                for dc in range(4)][-1],
                deps=o_store_toks + [mm_toks_c.get(tau - 2)])
            last_mm = None
            for jm in range(8):
                bank = psb[jm % 2]
                xsl = jm % 2
                t_ldxf = P.dma("g", 1, lambda sy, fin, jm=jm, tau=tau,
                               xsl=xsl: fin(
                                   out=xftl[:, xsl * 512:(xsl + 1) * 512],
                                   in_=ds["xf_full"][jm * 128:(jm + 1) * 128,
                                                     tau * 512:(tau + 1) * 512]),
                               deps=xf_store_toks +
                               (rs_dma_toks[-2:-1] if len(rs_dma_toks) >= 2
                                else []))
                def c_mm(t, jm=jm, pp=pp, bank=bank):
                    for dc in range(4):
                        t.matmul(bank[:, :],
                                 wbuf[:, 12288 + dc * 1024 + jm * 128:
                                      12288 + dc * 1024 + (jm + 1) * 128],
                                 ob[:, pp * 2048 + dc * 512:
                                    pp * 2048 + (dc + 1) * 512],
                                 start=(dc == 0), stop=(dc == 3))
                t_mm = P.step("t", c_mm,
                              deps=[t_ldo, t_w, aexp_last[7]] +
                              (rs_dma_toks[-2:-1] if len(rs_dma_toks) >= 2
                               else []))
                last_mm = t_mm
                rsl = jm % 2
                t_ev = P.step("v", lambda v, bank=bank, xsl=xsl, rsl=rsl:
                              v.scalar_tensor_tensor(
                                  rstc[:, rsl * 512:(rsl + 1) * 512],
                                  xftl[:, xsl * 512:(xsl + 1) * 512], 0.5,
                                  bank[:, :], ALU.mult, ALU.add),
                              deps=[t_mm, t_ldxf] +
                              (rs_dma_toks[-2:-1] if len(rs_dma_toks) >= 2
                               else []))
                rs_dma_toks.append(P.dma("g", 1,
                                         lambda gp, fin, hf=hf, jm=jm, tc=tc,
                                         rsl=rsl: fin(
                                             out=ds["rs_in"][
                                                 hf, jm * 128:(jm + 1) * 128,
                                                 tc * 512:(tc + 1) * 512],
                                             in_=rstc[:, rsl * 512:
                                                      (rsl + 1) * 512]),
                                         deps=[t_ev]))
            mm_toks_c[tau] = last_mm

        if debug:
            P.dma("g", 1, lambda gp, fin: fin(out=ds["rsin_d"][:, :, :],
                                              in_=ds["rs_in"][:, :, :]),
                  deps=rs_dma_toks)
        t_rs_cc = P.cc("ReduceScatter", ALU.add, CC_PAIRS,
                       [ds["rs_in"].flatten()], [ds["rs_out"].flatten()],
                       deps=rs_dma_toks)
        if debug:
            P.dma("g", 1, lambda gp, fin: fin(out=ds["rs_out_d"][:, :],
                                              in_=ds["rs_out"][:, :]),
                  deps=[t_rs_cc])

        x2t = reg(0, 1024)
        xtm4 = reg(1024, 5120)
        btm4 = reg(5120, 9216)
        lrz = reg(9216, 11264)[0:1, :]
        lgt = reg(11264, 11776)[0:32, :]
        lgtm = reg(11776, 12288)
        ohm = reg(12288, 14336)[0:32, :]
        ccm = reg(14336, 16384)[0:32, :]
        slotf_r = reg(1024, 3072)[0:1, :]
        accf_r = reg(3072, 5120)[0:1, :]
        wrow = reg(5120, 7680)[0:1, :]
        ohs = reg(7680, 8192)[0:32, :]
        sct = reg(1024, 3072)
        slot_sb = reg(3072, 3200)
        zb = reg(3200, 4224)

        def lr(i):
            return lrz[0:1, i * 512:(i + 1) * 512]

        def wr_(i):
            return wrow[0:1, i * 512:(i + 1) * 512]

        agx_toks = []
        base_toks = []
        x2_done = {}
        prev_x2 = []
        lg_evs = []
        xt_toks = {}
        for to in range(4):
            mm_m = []
            mm_s = []
            for hc in range(8):
                sl = hc % 2
                t_ld = P.dma("g", 1, lambda sy, fin, hc=hc, to=to, sl=sl: fin(
                    out=x2t[:, sl * 512:(sl + 1) * 512],
                    in_=ds["rs_out"][hc * 128:(hc + 1) * 128,
                                     to * 512:(to + 1) * 512]),
                    deps=[t_rs_cc] + (mm_m[hc - 2:hc - 1] if hc >= 2
                                      else prev_x2))
                t_x2 = P.step("v", lambda v, hc=hc, sl=sl: v.tensor_scalar(
                    x2t[:, sl * 512:(sl + 1) * 512],
                    x2t[:, sl * 512:(sl + 1) * 512],
                    bo_sb[:, hc:hc + 1], None, ALU.add), deps=[t_ld])
                if debug:
                    t_x2 = P.dma("g", 1, lambda gp, fin, hc=hc, to=to, sl=sl:
                                 fin(out=ds["x2_d"][hc * 128:(hc + 1) * 128,
                                                    to * 512:(to + 1) * 512],
                                     in_=x2t[:, sl * 512:(sl + 1) * 512]),
                                 deps=[t_x2])
                tq = P.step("a", lambda sc, sl=sl: sc.activation(
                    sqt[:, 0:512], x2t[:, sl * 512:(sl + 1) * 512],
                    ACT.Square), deps=[t_x2] + mm_s[-1:])
                mm_m.append(P.step("t", lambda t, hc=hc, sl=sl: t.matmul(
                    psb[0][0:1, :], onesc[:, :],
                    x2t[:, sl * 512:(sl + 1) * 512],
                    start=(hc == 0), stop=(hc == 7)),
                    deps=[t_x2] + (lg_evs[-1:] if hc == 0 else [])))
                mm_s.append(P.step("t", lambda t, hc=hc: t.matmul(
                    psb[1][0:1, :], onesc[:, :], sqt[:, 0:512],
                    start=(hc == 0), stop=(hc == 7)), deps=[tq]))

            def d_stats(v):
                v.tensor_scalar_mul(lr(0), psb[0][0:1, :], 1.0 / 1024)
                v.tensor_scalar_mul(lr(1), psb[1][0:1, :], 1.0 / 1024)
                v.tensor_tensor(lr(2), lr(0), lr(0), ALU.mult)
                v.tensor_tensor(lr(2), lr(1), lr(2), ALU.subtract)
                return v.tensor_scalar_add(lr(2), lr(2), EPS)
            t_st = P.step("v", d_stats, deps=[mm_m[-1], mm_s[-1]])
            t_sq2 = P.step("a", lambda sc: sc.activation(
                lr(2), lr(2), ACT.Sqrt), deps=[t_st])

            def d_rstd(v):
                v.reciprocal(lr(2), lr(2))
                return v.scalar_tensor_tensor(lr(3), lr(0), -1.0,
                                              lr(2), ALU.mult, ALU.mult)
            t_rst = P.step("v", d_rstd, deps=[t_sq2])
            if debug:
                t_rst = P.dma("g", 1, lambda gp, fin, to=to: fin(
                    out=ds["lr_d"][0:1, to * 2048:(to + 1) * 2048] if True else None,
                    in_=lrz[0:1, :]), deps=[t_rst])
            t_b1 = P.step("t", lambda t: t.matmul(
                psb[2][:, :], ones1[0:1, :], lr(2),
                start=True, stop=True), deps=[t_rst])
            t_b2 = P.step("t", lambda t: t.matmul(
                psb[3][:, :], ones1[0:1, :], lr(3),
                start=True, stop=True), deps=[t_rst])

            ev_xb = []
            for hc in range(8):
                sl = hc % 2
                t_ld2 = P.dma("g", 1, lambda sy, fin, hc=hc, to=to, sl=sl:
                              fin(out=x2t[:, sl * 512:(sl + 1) * 512],
                                  in_=ds["rs_out"][hc * 128:(hc + 1) * 128,
                                                   to * 512:(to + 1) * 512]),
                              deps=[mm_m[-1], mm_s[-1]] +
                              ([x2_done[(hc - 2, to)]] if hc >= 2 else []))
                t_x2 = P.step("v", lambda v, hc=hc, sl=sl: v.tensor_scalar(
                    x2t[:, sl * 512:(sl + 1) * 512],
                    x2t[:, sl * 512:(sl + 1) * 512],
                    bo_sb[:, hc:hc + 1], None, ALU.add), deps=[t_ld2])
                if debug:
                    t_x2 = P.dma("g", 1, lambda gp, fin, hc=hc, to=to, sl=sl:
                                 fin(out=ds["x2b_d"][hc * 128:(hc + 1) * 128,
                                                     to * 512:(to + 1) * 512],
                                     in_=x2t[:, sl * 512:(sl + 1) * 512]),
                                 deps=[t_x2])
                t_xt = P.step("v", lambda v, hc=hc, to=to, sl=sl: [
                    v.tensor_tensor(big1[:, hc * 2048 + to * 512:
                                         hc * 2048 + (to + 1) * 512],
                                    x2t[:, sl * 512:(sl + 1) * 512],
                                    psb[2][:, :], ALU.mult),
                    v.tensor_tensor(big1[:, hc * 2048 + to * 512:
                                         hc * 2048 + (to + 1) * 512],
                                    big1[:, hc * 2048 + to * 512:
                                         hc * 2048 + (to + 1) * 512],
                                    psb[3][:, :], ALU.add)][-1],
                    deps=[t_b1, t_b2, t_x2])
                xt_toks[(hc, to)] = t_xt
                for sub in range(4):
                    bank = psb[4]
                    bank2 = psb[5]
                    ttr = P.step("t", lambda t, hc=hc, to=to, sub=sub,
                                 bank=bank: t.transpose(
                                     bank[:, :128],
                                     big1[:, hc * 2048 + to * 512 + sub * 128:
                                          hc * 2048 + to * 512 +
                                          (sub + 1) * 128],
                                     ident[:, :]),
                                 deps=[t_xt] +
                                 ([ev_xb[-2]] if len(ev_xb) >= 2 else prev_x2))
                    tev = P.step("v", lambda v, hc=hc, sub=sub, bank=bank:
                                 v.tensor_copy(
                                     xtm4[:, sub * 1024 + hc * 128:
                                          sub * 1024 + (hc + 1) * 128],
                                     bank[:, :128]),
                                 deps=[ttr] + agx_toks[-4:-3])
                    ev_xb.append(tev)
                    t_ba = P.step("v", lambda v, hc=hc, to=to, sub=sub, sl=sl:
                                  v.tensor_tensor(
                                      bfa[:, :],
                                      x2t[:, sl * 512 + sub * 128:
                                          sl * 512 + (sub + 1) * 128],
                                      big1[:, hc * 2048 + to * 512 + sub * 128:
                                           hc * 2048 + to * 512 +
                                           (sub + 1) * 128],
                                      ALU.add), deps=[t_xt] + ev_xb[-1:])
                    ttr2 = P.step("t", lambda t, bank2=bank2: t.transpose(
                        bank2[:, :128], bfa[:, :], ident[:, :]),
                        deps=[t_ba] + ([ev_xb[-2]] if len(ev_xb) >= 2 else []))
                    tev2 = P.step("v", lambda v, hc=hc, sub=sub, bank2=bank2:
                                  v.tensor_copy(
                                      btm4[:, sub * 1024 + hc * 128:
                                           sub * 1024 + (hc + 1) * 128],
                                      bank2[:, :128]),
                                  deps=[ttr2] + base_toks[-4:-3])
                    ev_xb.append(tev2)
                x2_done[(hc, to)] = ev_xb[-1]
            for sub in range(4):
                agx_toks.append(P.dma("g", 1, lambda gp, fin, to=to, sub=sub:
                                      fin(out=ds["agx_in"][
                                          to * 512 + sub * 128:
                                          to * 512 + (sub + 1) * 128, :],
                                          in_=xtm4[:, sub * 1024:
                                                   (sub + 1) * 1024]),
                                      deps=ev_xb))
                base_toks.append(P.dma("g", 1, lambda gp, fin, to=to, sub=sub:
                                       fin(out=ds["base_hbm"][
                                           to * 512 + sub * 128:
                                           to * 512 + (sub + 1) * 128, :],
                                           in_=btm4[:, sub * 1024:
                                                    (sub + 1) * 1024]),
                                       deps=ev_xb))
            prev_x2 = [x2_done[(7, to)]]

            t_lg = P.step("t", lambda t, to=to: [
                t.matmul(psb[6][0:32, :], wr_sb[:, hc * 32:(hc + 1) * 32],
                         big1[:, hc * 2048 + to * 512:
                              hc * 2048 + (to + 1) * 512],
                         start=(hc == 0), stop=(hc == 7))
                for hc in range(8)][-1],
                deps=[xt_toks[(hc, to)] for hc in range(8)] + [t_w] +
                lg_evs[-1:])
            t_le = P.step("v", lambda v: v.tensor_copy(
                lgt[:, :], psb[6][0:32, :]), deps=[t_lg])
            for s_ in range(4):
                ttr = P.step("t", lambda t, s_=s_: t.transpose(
                    psb[6][0:128, 0:32], lgt[:, s_ * 128:(s_ + 1) * 128],
                    ident[0:32, 0:32]),
                    deps=[t_le] + lg_evs[-1:])
                tev = P.step("v", lambda v, to=to, s_=s_: v.tensor_copy(
                    lgtm[:, (to * 4 + s_) * 32:(to * 4 + s_ + 1) * 32],
                    psb[6][0:128, 0:32]), deps=[ttr])
                lg_evs.append(tev)

        if debug:
            P.dma("g", 1, lambda gp, fin: fin(out=ds["lg_dbg"][:, :],
                                              in_=lgtm[:, :]), deps=lg_evs)

        oh_toks = []
        rt_toks = []
        prev_tr = None
        prev_r2 = None
        for jp in range(8):
            js = (2 * jp, 2 * jp + 1)

            def d_rt(v, js=js):
                for i, j in enumerate(js):
                    v.reduce_max(scr[:, i * 8:i * 8 + 1],
                                 lgtm[:, j * 32:(j + 1) * 32], AX)
                for i in range(2):
                    v.tensor_scalar_mul(scr[:, i * 8 + 1:i * 8 + 2],
                                        scr[:, i * 8:i * 8 + 1], -1.0)
            t_r1 = P.step("v", d_rt,
                          deps=lg_evs + oh_toks[-1:] +
                          ([prev_r2] if prev_r2 else []))
            t_r2 = P.step("a", lambda sc, js=js: [
                sc.activation(probs[:, i * 32:(i + 1) * 32],
                              lgtm[:, j * 32:(j + 1) * 32], ACT.Exp,
                              bias=scr[:, i * 8 + 1:i * 8 + 2],
                              accum_out=scr[:, i * 8 + 2:i * 8 + 3])
                for i, j in enumerate(js)][-1], deps=[t_r1])
            prev_r2 = t_r2

            def d_rt2(v, js=js):
                for i in range(2):
                    v.reciprocal(scr[:, i * 8 + 3:i * 8 + 4],
                                 scr[:, i * 8 + 2:i * 8 + 3])
                for i in range(2):
                    p_ = probs[:, i * 32:(i + 1) * 32]
                    v.tensor_scalar(p_, p_, scr[:, i * 8 + 3:i * 8 + 4],
                                    None, ALU.mult)
                for i, j in enumerate(js):
                    v.reduce_max(rt[:, j * 8 + 1:j * 8 + 2],
                                 probs[:, i * 32:(i + 1) * 32], AX)
                for i, j in enumerate(js):
                    p_ = probs[:, i * 32:(i + 1) * 32]
                    v.tensor_scalar(p_, p_, rt[:, j * 8 + 1:j * 8 + 2],
                                    None, ALU.is_equal)
                for i in range(2):
                    p_ = probs[:, i * 32:(i + 1) * 32]
                    v.tensor_tensor(p_, p_, w32[:, :], ALU.mult)
                for i in range(2):
                    v.reduce_max(scr[:, i * 8 + 4:i * 8 + 5],
                                 probs[:, i * 32:(i + 1) * 32], AX)
                for i, j in enumerate(js):
                    v.scalar_tensor_tensor(rt[:, j * 8:j * 8 + 1],
                                           scr[:, i * 8 + 4:i * 8 + 5],
                                           -1.0, onesc[:, :],
                                           ALU.mult, ALU.bypass)
                for j in js:
                    v.tensor_scalar_add(rt[:, j * 8:j * 8 + 1],
                                        rt[:, j * 8:j * 8 + 1], 32.0)
                last = None
                for i, j in enumerate(js):
                    last = v.tensor_scalar(probs[:, i * 32:(i + 1) * 32],
                                           iotaE[:, :],
                                           rt[:, j * 8:j * 8 + 1],
                                           None, ALU.is_equal)
                return last
            t_r3 = P.step("v", d_rt2, deps=[t_r2])
            rt_toks.append(t_r3)
            for i, j in enumerate(js):
                ttr = P.step("t", lambda t, i=i: t.transpose(
                    psb[5][0:32, :128], probs[:, i * 32:(i + 1) * 32],
                    ident[:, :]),
                    deps=[t_r3] + ([prev_tr] if prev_tr else []))
                tev = P.step("v", lambda v, j=j: v.tensor_copy(
                    ohm[:, j * 128:(j + 1) * 128], psb[5][0:32, :128]),
                    deps=[ttr])
                prev_tr = tev
                oh_toks.append(tev)

        t_scan = P.step("v", lambda v: v.tensor_tensor_scan(
            ccm[:, :], ohm[:, :], ohm[:, :], 0.0, ALU.add, ALU.bypass),
            deps=oh_toks)
        t_cntd = P.dma("g", 1, lambda gp, fin: fin(
            out=ds["cnt_in"][:].unsqueeze(1), in_=ccm[:, 2047:2048]),
            deps=[t_scan])
        t_cc_cnt = P.cc("AllGather", ALU.bypass, CC_ALL,
                        [ds["cnt_in"][:]], [ds["cnt_out"][:]], deps=[t_cntd])
        t_cc_xt = P.cc("AllGather", ALU.bypass, CC_ALL,
                       [ds["agx_in"].flatten()], [ds["xt_full"].flatten()],
                       deps=agx_toks)
        if debug:
            P.dma("g", 2, lambda gp, fin: [
                fin(out=ds["agx_in_d"][:, :], in_=ds["agx_in"][:, :]),
                fin(out=ds["xt_full_d"][:, :], in_=ds["xt_full"][:, :])]
                and None, deps=[t_cc_xt])
        t_ld8 = P.dma("g", 1, lambda sy, fin: fin(
            out=cnt8[:, :], in_=ds["cnt_out"].rearrange("(a b) -> a b", b=32)),
            deps=[t_cc_cnt])
        t_off = P.step("t", lambda t: t.matmul(
            psb[7][0:32, 0:1], cnt8[:, :], mrow[:, :], start=True, stop=True),
            deps=[t_ld8, t_w, prev_n4])
        t_ofe = P.step("v", lambda v: v.tensor_copy(offc[:, :],
                                                    psb[7][0:32, 0:1]),
                       deps=[t_off])

        row_toks = []
        rows_guard = agx_toks + base_toks
        for to in range(4):
            sl_ = slice(to * 512, (to + 1) * 512)
            t_o1 = P.step("v", lambda v, sl_=sl_: v.tensor_tensor(
                ohs[:, :], ohm[:, sl_], ccm[:, sl_], ALU.mult),
                deps=[t_scan] + row_toks[-1:] + rows_guard)
            t_dbg_o = None
            if debug and to == 0:
                t_dbg_o = P.dma("g", 1, lambda gp, fin: fin(
                    out=ds["rows_dbg"][5:6, 0:512],
                    in_=ohs[7:8, :]), deps=[t_o1])
            t_m1 = P.step("t", lambda t: t.matmul(
                psb[6][0:1, :], onesc[0:32, :], ohs[:, :],
                start=True, stop=True), deps=[t_o1])
            t_e1 = P.step("v", lambda v: v.tensor_copy(
                wr_(0), psb[6][0:1, :]), deps=[t_m1] + rows_guard)
            if debug and to == 0:
                P.dma("g", 1, lambda gp, fin: fin(
                    out=ds["rows_dbg"][6:7, 0:512],
                    in_=wr_(0)), deps=[t_e1])
                t_e1 = (t_e1, t_dbg_o)[0]
            t_o2 = P.step("v", lambda v, sl_=sl_: v.tensor_scalar(
                ohs[:, :], ohm[:, sl_], offc[:, 0:1], None, ALU.mult),
                deps=[t_e1, t_ofe] + ([t_dbg_o] if t_dbg_o else []))
            t_m2 = P.step("t", lambda t: t.matmul(
                psb[6][0:1, :], onesc[0:32, :], ohs[:, :],
                start=True, stop=True), deps=[t_o2])
            t_e2 = P.step("v", lambda v: v.tensor_copy(
                wr_(1), psb[6][0:1, :]), deps=[t_m2])
            t_o3 = P.step("v", lambda v, sl_=sl_: v.tensor_scalar(
                ohs[:, :], ohm[:, sl_], pcol[0:32, 0:1], None, ALU.mult),
                deps=[t_e2])
            t_m3 = P.step("t", lambda t: t.matmul(
                psb[6][0:1, :], onesc[0:32, :], ohs[:, :],
                start=True, stop=True), deps=[t_o3])
            t_e3 = P.step("v", lambda v: v.tensor_copy(
                wr_(2), psb[6][0:1, :]), deps=[t_m3])

            def d_rows(v, sl_=sl_):
                acc_b = accf_r[0:1, sl_]
                slf_b = slotf_r[0:1, sl_]
                v.tensor_tensor(wr_(3), wr_(0), wr_(1), ALU.add)
                v.tensor_scalar_add(wr_(3), wr_(3), -1.0)
                v.tensor_single_scalar(acc_b, wr_(3), float(C), ALU.is_lt)
                v.scalar_tensor_tensor(wr_(4), wr_(2), float(C), wr_(3),
                                       ALU.mult, ALU.add)
                v.tensor_single_scalar(wr_(3), wr_(3), float(C), ALU.is_ge)
                return v.scalar_tensor_tensor(slf_b, wr_(3), BIG, wr_(4),
                                              ALU.mult, ALU.add)
            row_toks.append(P.step("v", d_rows, deps=[t_e3]))
            if debug and to == 0:
                t_dbg_s = P.dma("g", 1, lambda gp, fin: fin(
                    out=ds["rows_dbg"][7:8, 0:512],
                    in_=slotf_r[0:1, 0:512]), deps=row_toks[-1:])
                row_toks[-1] = t_dbg_s
            if debug:
                P.dma("g", 3, lambda gp, fin, to=to: [
                    fin(out=ds["rows_dbg"][2:3, to * 512:(to + 1) * 512],
                        in_=wr_(0)),
                    fin(out=ds["rows_dbg"][3:4, to * 512:(to + 1) * 512],
                        in_=wr_(1)),
                    fin(out=ds["rows_dbg"][4:5, to * 512:(to + 1) * 512],
                        in_=wr_(2))][-1], deps=row_toks[-1:])
        if debug:
            P.dma("g", 4, lambda gp, fin: [
                fin(out=ds["ohm_dbg"][:, :], in_=ohm[:, :]),
                fin(out=ds["ccm_dbg"][:, :], in_=ccm[:, :]),
                fin(out=ds["rows_dbg"][0:1, :], in_=slotf_r),
                fin(out=ds["rows_dbg"][1:2, :], in_=accf_r)][-1],
                deps=row_toks + [t_scan])

        prev_tr2 = None
        for j in range(16):
            ttr = P.step("t", lambda t, j=j: [
                t.matmul(psb[5][0:128, 0:1],
                         slotf_r[0:1, j * 128:(j + 1) * 128],
                         onesc[0:1, 0:1], start=True, stop=True),
                t.matmul(psb[4][0:128, 0:1],
                         accf_r[0:1, j * 128:(j + 1) * 128],
                         onesc[0:1, 0:1], start=True, stop=True)][-1],
                deps=row_toks + ([prev_tr2] if prev_tr2 else []))
            tev = P.step("v", lambda v: [
                v.tensor_copy(r8t[:, 0:1], psb[5][0:128, 0:1]),
                v.tensor_copy(r8t[:, 5:6], psb[4][0:128, 0:1])][-1],
                deps=[ttr])

            def d_rt3(v, j=j):
                v.tensor_copy(rt[:, j * 8 + 2:j * 8 + 3], r8t[:, 0:1])
                v.tensor_tensor(rt[:, j * 8 + 3:j * 8 + 4],
                                rt[:, j * 8 + 1:j * 8 + 2], r8t[:, 5:6],
                                ALU.mult)
                return v.tensor_scalar_min(rt[:, j * 8 + 4:j * 8 + 5],
                                           r8t[:, 0:1], float(EC - 1))
            prev_tr2 = P.step("v", d_rt3, deps=[tev])
            rt_toks.append(prev_tr2)
        if debug:
            P.dma("g", 1, lambda gp, fin: fin(out=ds["rt_dbg"][:, :],
                                              in_=rt[:, :]), deps=rt_toks)

        t_slotd = P.dma("g", 1, lambda gp, fin: fin(
            out=ds["slot_in"][:].unsqueeze(0), in_=slotf_r),
            deps=row_toks)
        t_cc_slot = P.cc("AllGather", ALU.bypass, CC_ALL,
                         [ds["slot_in"][:]], [ds["slot_out"][:]],
                         deps=[t_slotd])
        if debug:
            P.dma("g", 1, lambda gp, fin: fin(out=ds["slot_out_d"][:],
                                              in_=ds["slot_out"][:]),
                  deps=[t_cc_slot])

        t_zb = P.step("v", lambda v: v.memset(zb[:, :], 0.0),
                      deps=base_toks + rt_toks)
        t_zero = P.dma("g", 20, lambda gp, fin: [
            fin(out=ds["buf"][i * 128:(i + 1) * 128, :], in_=zb[:, :])
            for i in range(20)][-1], deps=[t_zb])
        t_lds0 = P.dma("g", 1, lambda sy, fin: fin(
            out=slot_sb[0:128, :],
            in_=ds["slot_out"].rearrange("(p a) -> p a", a=128)),
            deps=[t_cc_slot] + base_toks + rt_toks)
        t_lds1 = P.step("t", lambda t: t.transpose(
            psb[5][:, :128], slot_sb[0:128, :], ident[:, :]),
            deps=[t_lds0, prev_tr2])
        t_lds = P.step("v", lambda v: v.tensor_copy(
            slot_sb[0:128, :], psb[5][:, :128]), deps=[t_lds1])
        scat_toks = []
        for p2 in range(64):
            ja, jb = 2 * p2, 2 * p2 + 1
            t_la = P.dma("g", 1, lambda sy, fin, ja=ja: fin(
                out=sct[:, 0:1024],
                in_=ds["xt_full"][ja * 128:(ja + 1) * 128, :]),
                deps=[t_cc_xt, t_cc_slot] + rows_guard + rt_toks +
                scat_toks[-2:])
            t_lbx = P.dma("g", 1, lambda sy, fin, jb=jb: fin(
                out=sct[:, 1024:2048],
                in_=ds["xt_full"][jb * 128:(jb + 1) * 128, :]),
                deps=scat_toks[-2:])

            def d_idx(v, ja=ja, jb=jb):
                for i, j2 in enumerate((ja, jb)):
                    v.tensor_tensor(idxt[:, i * 4:i * 4 + 1],
                                    slot_sb[0:128, j2:j2 + 1],
                                    meta_sb[:, 0:1], ALU.subtract)
                for i in range(2):
                    v.tensor_single_scalar(idxt[:, i * 4 + 1:i * 4 + 2],
                                           idxt[:, i * 4:i * 4 + 1], 0.0,
                                           ALU.is_lt)
                for i in range(2):
                    v.scalar_tensor_tensor(idxt[:, i * 4 + 2:i * 4 + 3],
                                           idxt[:, i * 4 + 1:i * 4 + 2], BIG,
                                           idxt[:, i * 4:i * 4 + 1],
                                           ALU.mult, ALU.add)
                for i in range(2):
                    v.tensor_scalar_min(idxt[:, i * 4 + 2:i * 4 + 3],
                                        idxt[:, i * 4 + 2:i * 4 + 3], 4096.0)
                last = None
                for i in range(2):
                    last = v.tensor_copy(idxi[:, i:i + 1],
                                         idxt[:, i * 4 + 2:i * 4 + 3])
                return last
            t_ix = P.step("v", d_idx,
                          deps=[t_lds, t_w] + scat_toks[-2:])
            for i in range(2):
                scat_toks.append(P.dma("g", 1, lambda gp, fin, i=i: fin(
                    nc.gpsimd.indirect_dma_start(
                        out=ds["buf"][:, :],
                        out_offset=IndirectOffsetOnAxis(idxi[:, i:i + 1], 0),
                        in_=sct[:, i * 1024:(i + 1) * 1024],
                        in_offset=None,
                        bounds_check=bc_scat,
                        oob_is_err=False)),
                    deps=[t_la, t_lbx, t_ix, t_zero]))

        bfm = reg(0, 5120)
        hgm = reg(5120, 10240)
        yfm = reg(10240, 15360)
        ytm = reg(15360, 16384)

        agy_toks = []
        prev_gu = []
        prev_dn = []
        for le in range(ELOC):
            t_wg = P.dma("sy", 8, lambda sy, fin, le=le: [
                fin(out=wbuf[:, hc * 1024:(hc + 1) * 1024],
                    in_=din["wg"][le, hc * 128:(hc + 1) * 128, :])
                for hc in range(8)][-1],
                deps=[mm_toks_c[7]] + prev_gu + prev_dn)
            t_wu = P.dma("sy", 8, lambda sy, fin, le=le: [
                fin(out=wbuf[:, 8192 + hc * 1024:8192 + (hc + 1) * 1024],
                    in_=din["wu"][le, hc * 128:(hc + 1) * 128, :])
                for hc in range(8)][-1],
                deps=[mm_toks_c[7]] + prev_gu + prev_dn)
            prev_gu, prev_dn = [], []
            bf_toks = []
            bt_prev = []
            for bt in range(5):
                t_lb = P.dma("sy", 1, lambda sy, fin, le=le, bt=bt:
                             fin(out=ytm[:, :],
                                 in_=ds["buf"][le * C + bt * 128:
                                               le * C + (bt + 1) * 128, :]),
                             deps=scat_toks + bt_prev + agy_toks[-1:])
                evs = []
                for hc in range(8):
                    bank = psb[6 + hc % 2]
                    ttr = P.step("t", lambda t, hc=hc, bank=bank:
                                 t.transpose(bank[:, :128],
                                             ytm[:, hc * 128:(hc + 1) * 128],
                                             ident[:, :]),
                                 deps=[t_lb] +
                                 ([evs[-2]] if len(evs) >= 2 else
                                  (bf_toks[-1:] if bf_toks
                                   else agy_toks[-1:])))
                    tev = P.step("v", lambda v, hc=hc, bt=bt, bank=bank:
                                 v.tensor_copy(bfm[:, hc * 640 + bt * 128:
                                                   hc * 640 + (bt + 1) * 128],
                                               bank[:, :128]), deps=[ttr])
                    evs.append(tev)
                bf_toks += evs
                bt_prev = [evs[-1]]
            hg_toks = []
            for half in range(2):
                for fj in range(8):
                    gb = psb[(fj % 2) * 2]
                    ub = psb[(fj % 2) * 2 + 1]
                    t_g = P.step("t", lambda t, half=half, fj=fj, gb=gb: [
                        t.matmul(gb[:, 0:320],
                                 wbuf[:, hc * 1024 + fj * 128:
                                      hc * 1024 + (fj + 1) * 128],
                                 bfm[:, hc * 640 + half * 320:
                                     hc * 640 + half * 320 + 320],
                                 start=(hc == 0), stop=(hc == 7))
                        for hc in range(8)][-1],
                        deps=bf_toks + [t_wg] + hg_toks[-2:-1])
                    t_u = P.step("t", lambda t, half=half, fj=fj, ub=ub: [
                        t.matmul(ub[:, 0:320],
                                 wbuf[:, 8192 + hc * 1024 + fj * 128:
                                      8192 + hc * 1024 + (fj + 1) * 128],
                                 bfm[:, hc * 640 + half * 320:
                                     hc * 640 + half * 320 + 320],
                                 start=(hc == 0), stop=(hc == 7))
                        for hc in range(8)][-1],
                        deps=[t_wu] + hg_toks[-2:-1])
                    prev_gu += [t_g, t_u]
                    t_si = P.step("a", lambda sc, gb=gb: sc.activation(
                        ytm[:, 0:320], gb[:, 0:320], ACT.Silu),
                        deps=[t_g] + hg_toks[-1:])
                    t_hg = P.step("v", lambda v, half=half, fj=fj, ub=ub:
                                  v.tensor_tensor(
                                      hgm[:, fj * 640 + half * 320:
                                          fj * 640 + half * 320 + 320],
                                      ytm[:, 0:320], ub[:, 0:320],
                                      ALU.mult),
                                  deps=[t_si, t_u])
                    hg_toks.append(t_hg)
            t_wd = P.dma("sy", 8, lambda sy, fin, le=le: [
                fin(out=wbuf[:, hc * 1024:(hc + 1) * 1024],
                    in_=din["wd"][le, hc * 128:(hc + 1) * 128, :])
                for hc in range(8)][-1], deps=prev_gu)
            y_toks = []
            for half in range(2):
                for hj in range(8):
                    yb = psb[4 + (hj % 2)]
                    t_y = P.step("t", lambda t, half=half, hj=hj, yb=yb: [
                        t.matmul(yb[:, 0:320],
                                 wbuf[:, fc * 1024 + hj * 128:
                                      fc * 1024 + (hj + 1) * 128],
                                 hgm[:, fc * 640 + half * 320:
                                     fc * 640 + half * 320 + 320],
                                 start=(fc == 0), stop=(fc == 7))
                        for fc in range(8)][-1],
                        deps=hg_toks + [t_wd] + y_toks[-2:-1])
                    prev_dn.append(t_y)
                    t_ye = P.step("v", lambda v, half=half, hj=hj, yb=yb:
                                  v.tensor_copy(
                                      yfm[:, hj * 640 + half * 320:
                                          hj * 640 + half * 320 + 320],
                                      yb[:, 0:320]), deps=[t_y])
                    y_toks.append(t_ye)
            for bt in range(5):
                evs = []
                for hc in range(8):
                    bank = psb[6 + hc % 2]
                    ttr = P.step("t", lambda t, bt=bt, hc=hc, bank=bank:
                                 t.transpose(bank[:, :128],
                                             yfm[:, hc * 640 + bt * 128:
                                                 hc * 640 + (bt + 1) * 128],
                                             ident[:, :]),
                                 deps=y_toks + ([evs[-2]] if len(evs) >= 2
                                                else agy_toks[-1:] +
                                                hg_toks[-1:]))
                    tev = P.step("v", lambda v, hc=hc, bank=bank:
                                 v.tensor_copy(ytm[:, hc * 128:(hc + 1) * 128],
                                               bank[:, :128]),
                                 deps=[ttr] + agy_toks[-1:] + bt_prev)
                    evs.append(tev)
                agy_toks.append(P.dma("g", 1, lambda gp, fin, le=le, bt=bt:
                                      fin(out=ds["agy_in"][
                                          le * C + bt * 128:
                                          le * C + (bt + 1) * 128, :],
                                          in_=ytm[:, :]), deps=evs))

        t_cc_y = P.cc("AllGather", ALU.bypass, CC_ALL,
                      [ds["agy_in"].flatten()], [ds["y_full"].flatten()],
                      deps=agy_toks)
        if debug:
            P.dma("g", 1, lambda gp, fin: fin(out=ds["y_full_d"][:, :],
                                              in_=ds["y_full"][:, :]),
                  deps=[t_cc_y])

        gat = reg(0, 2048)
        bstl = reg(2048, 4096)
        outt = reg(4096, 6144)
        out_toks = []
        for j in range(16):
            ssl = j % 2
            t_gi = P.step("v", lambda v, j=j, ssl=ssl: v.tensor_copy(
                gidxi[:, ssl:ssl + 1], rt[:, j * 8 + 4:j * 8 + 5]),
                deps=rt_toks + out_toks[-2:-1])
            t_lb = P.dma("g", 1, lambda sy, fin, j=j, ssl=ssl: fin(
                out=bstl[:, ssl * 1024:(ssl + 1) * 1024],
                in_=ds["base_hbm"][j * 128:(j + 1) * 128, :]),
                deps=base_toks + [t_cc_y] + out_toks[-2:-1])
            t_ga = P.dma("g", 1, lambda gp, fin, ssl=ssl: fin(
                nc.gpsimd.indirect_dma_start(
                    out=gat[:, ssl * 1024:(ssl + 1) * 1024],
                    out_offset=None,
                    in_=ds["y_full"][:, :],
                    in_offset=IndirectOffsetOnAxis(gidxi[:, ssl:ssl + 1], 0),
                    bounds_check=bc_gath,
                    oob_is_err=False)),
                deps=[t_cc_y, t_gi] + out_toks[-2:-1])
            t_fo = P.step("v", lambda v, j=j, ssl=ssl:
                          v.scalar_tensor_tensor(
                              outt[:, ssl * 1024:(ssl + 1) * 1024],
                              gat[:, ssl * 1024:(ssl + 1) * 1024],
                              rt[:, j * 8 + 3:j * 8 + 4],
                              bstl[:, ssl * 1024:(ssl + 1) * 1024],
                              ALU.mult, ALU.add),
                          deps=[t_ga, t_lb] + out_toks[-2:-1])
            out_toks.append(P.dma("g", 1, lambda gp, fin, j=j, ssl=ssl: fin(
                out=out_own[j * 128:(j + 1) * 128, :],
                in_=outt[:, ssl * 1024:(ssl + 1) * 1024]), deps=[t_fo]))

        P.step("g", lambda gp: None, deps=out_toks)

        block = es.enter_context(nc.Block())
        P.finalize(block)
    return nc, din, ds, out_own


_CACHE = {}


def _get_program(debug=False):
    key = bool(debug)
    if key not in _CACHE:
        _CACHE[key] = build_program(debug=debug)
    return _CACHE[key]


def make_in_maps(inputs):
    x = np.asarray(inputs["L0_attn_x"], dtype=np.float32)
    Wq = np.asarray(inputs["L0_attn_Wq"], dtype=np.float32)
    Wk = np.asarray(inputs["L0_attn_Wk"], dtype=np.float32)
    Wv = np.asarray(inputs["L0_attn_Wv"], dtype=np.float32)
    Wo = np.asarray(inputs["L0_attn_Wo"], dtype=np.float32)
    bq = np.asarray(inputs["L0_attn_bq"], dtype=np.float32)
    bk = np.asarray(inputs["L0_attn_bk"], dtype=np.float32)
    bv = np.asarray(inputs["L0_attn_bv"], dtype=np.float32)
    bo = np.asarray(inputs["L0_attn_bo"], dtype=np.float32)
    wr = np.asarray(inputs["L0_ffn_w_router"], dtype=np.float32)
    wg = np.asarray(inputs["L0_ffn_w_gate"], dtype=np.float32)
    wu = np.asarray(inputs["L0_ffn_w_up"], dtype=np.float32)
    wd = np.asarray(inputs["L0_ffn_w_down"], dtype=np.float32)
    maps = []
    for c in range(NCORE):
        b, hh = c // 2, c % 2
        sl = slice(hh * 512, (hh + 1) * 512)
        meta = np.zeros((128, 4), np.float32)
        meta[:, 0] = c * ELOC * C
        mr = np.zeros(NCORE, np.float32)
        mr[:c] = 1.0
        maps.append({
            "x_b": np.ascontiguousarray(x[b]),
            "wqT": np.ascontiguousarray(Wq[sl, :].T),
            "wkT": np.ascontiguousarray(Wk[sl, :].T),
            "wvT": np.ascontiguousarray(Wv[sl, :].T),
            "bqs": np.ascontiguousarray(bq[sl]),
            "bks": np.ascontiguousarray(bk[sl]),
            "bvs": np.ascontiguousarray(bv[sl]),
            "woT": np.ascontiguousarray(Wo[:, sl].T),
            "bo": bo,
            "w_router": wr,
            "wg": np.ascontiguousarray(wg[c * ELOC:(c + 1) * ELOC]),
            "wu": np.ascontiguousarray(wu[c * ELOC:(c + 1) * ELOC]),
            "wd": np.ascontiguousarray(wd[c * ELOC:(c + 1) * ELOC]),
            "meta": meta,
            "maskrow": mr,
        })
    return maps


def kernel(**inputs):
    nc, din, ds, out_own = _get_program(debug=False)
    maps = make_in_maps(inputs)
    res = run_bass_kernel_spmd(nc, maps, core_ids=list(range(NCORE)))
    out = np.empty((B, S, H), np.float32)
    for c in range(NCORE):
        b, hh = c // 2, c % 2
        out[b, hh * TOWN:(hh + 1) * TOWN, :] = np.asarray(
            res.results[c]["out_own"])
    return out
